# version 21
# baseline (speedup 1.0000x reference)
import contextlib
import math

import numpy as np

import concourse.bass as bass
import concourse.mybir as mybir
from concourse.bass_utils import run_bass_kernel_spmd

F32 = mybir.dt.float32
BF16 = mybir.dt.bfloat16
OP = mybir.AluOpType
AF = mybir.ActivationFunctionType
AX = mybir.AxisListType.X

B, S, DM = 2, 4096, 2048
T = 1024
NT = T // 128
NST = S // 128
NH, DN, DR, DV, KR = 16, 128, 64, 128, 512
HI, DI = 4, 64
INTER = 8192
MNH, MHD, MN = 64, 64, 128
CONV = 4096 + 2 * MN
PROJ = 4096 + CONV + MNH
NEG = -1.0e9
SCALE = 1.0 / math.sqrt(DN + DR)
N_ITER = 52

ENGINES = ("pe", "dve", "act", "gp", "sync")


class Buf:
    __slots__ = ("name", "last_w", "readers")

    def __init__(self, name):
        self.name = name
        self.last_w = None
        self.readers = []


class Op:
    __slots__ = ("idx", "eng", "fn", "waits", "inc", "drain", "kind",
                 "resolved_val", "qcount")

    def __init__(self, idx, eng, fn, kind):
        self.idx = idx
        self.eng = eng
        self.fn = fn
        self.waits = set()
        self.inc = kind != "compute"
        self.drain = False
        self.kind = kind
        self.resolved_val = None
        self.qcount = 0


class Prog:

    def __init__(self):
        self.ops = []
        self.eng_ops = {e: [] for e in ENGINES}
        self.last_drain = {e: -1 for e in ENGINES}
        self.dma_count = {e: 0 for e in ENGINES}
        self.n_cc = 0

    def op(self, eng, fn, reads=(), writes=(), dma=False, cc=False):
        kind = "cc" if cc else ("dma" if dma else "compute")
        o = Op(len(self.ops), eng, fn, kind)
        if kind == "dma":
            self.dma_count[eng] += 1
            o.qcount = self.dma_count[eng]
        if kind == "cc":
            self.n_cc += 1
            o.qcount = self.n_cc
        self.ops.append(o)
        self.eng_ops[eng].append(o)
        deps = set()
        for b in reads:
            if b.last_w is not None:
                deps.add(b.last_w)
        for b in writes:
            if b.last_w is not None:
                deps.add(b.last_w)
            deps.update(b.readers)
        need_drain = False
        for d in deps:
            if d == o.idx:
                continue
            p = self.ops[d]
            if p.eng == eng and p.kind == "compute" and kind == "compute":
                if d > self.last_drain[eng]:
                    need_drain = True
            elif p.eng == eng and p.kind == "compute" and kind != "compute":
                need_drain = True
                o.waits.add(d)
            else:
                o.waits.add(d)
                p.inc = True
        if need_drain and eng not in ("sync", "pe"):
            o.drain = True
            self.last_drain[eng] = o.idx - 1
        for b in reads:
            b.readers.append(o.idx)
        for b in writes:
            b.last_w = o.idx
            b.readers = []
        return o

    def finalize(self, nc, sems, semq, ccsems):
        cum = {e: 0 for e in ENGINES}
        for o in self.ops:
            if o.kind == "compute" and o.inc:
                cum[o.eng] += 1
                o.resolved_val = cum[o.eng]
        batch_val = {}
        self_pref = {}
        for e in ENGINES:
            run = 0
            cur = []
            for o in self.eng_ops[e]:
                if o.waits:
                    self_pref[o.idx] = 16 * run
                    for m in cur:
                        batch_val[m] = 16 * run
                    cur = []
                if o.kind == "dma":
                    run += 1
                    cur.append(o.idx)
            for m in cur:
                batch_val[m] = 16 * run

        def emit_for(eng_name, eng_obj):
            waited = {e: 0 for e in ENGINES}
            waitedq = {e: 0 for e in ENGINES}
            waitedcc = {}
            for o in self.eng_ops[eng_name]:
                if o.waits:
                    v = self_pref[o.idx]
                    if v > waitedq[eng_name]:
                        eng_obj.wait_ge(semq[eng_name], v)
                        waitedq[eng_name] = v
                for d in sorted(o.waits):
                    p = self.ops[d]
                    if p.kind == "compute":
                        if p.eng == eng_name and o.kind == "compute":
                            continue
                        if p.eng == eng_name:
                            continue
                        v = p.resolved_val
                        if v > waited[p.eng]:
                            eng_obj.wait_ge(sems[p.eng], v)
                            waited[p.eng] = v
                    elif p.kind == "dma":
                        v = batch_val[p.idx]
                        if v > waitedq[p.eng]:
                            eng_obj.wait_ge(semq[p.eng], v)
                            waitedq[p.eng] = v
                    else:
                        key = p.qcount
                        if key not in waitedcc:
                            eng_obj.wait_ge(ccsems[key - 1], 1)
                            waitedcc[key] = True
                if o.drain:
                    eng_obj.drain()
                ins = o.fn(eng_obj)
                if o.kind == "dma":
                    ins.then_inc(semq[o.eng], 16)
                elif o.kind == "cc":
                    ins.then_inc(ccsems[o.qcount - 1], 1)
                elif o.inc:
                    ins.then_inc(sems[o.eng], 1)
        return emit_for


class Arena:

    def __init__(self, handle, n_f32, name, prog=None):
        self.prog = prog
        self.h = handle
        self.cap = n_f32
        self.off = 0
        self.name = name
        self.live = []
        self.residue = []
        self.marks = []
        self.hw = 0

    def alloc(self, name, free_elems, dtype=F32):
        esz = mybir.dt.size(dtype)
        words = (free_elems * esz + 3) // 4
        assert self.off + words <= self.cap, (
            f"arena {self.name} overflow at {name}: "
            f"{self.off}+{words}>{self.cap}")
        start, end = self.off, self.off + words
        base = self.h[:, start:end]
        self.off = end
        self.hw = max(self.hw, self.off)
        if dtype != F32:
            base = base.bitcast(dtype)[:, 0:free_elems]
        b = Buf(f"{self.name}.{name}")
        deps = set()
        for (s0, e0, ops) in self.residue:
            if s0 < end and e0 > start:
                deps.update(ops)
        b.readers = sorted(deps)
        self.live.append((b, start, end))
        return base, b

    def mark(self):
        self.marks.append((self.off, len(self.live)))

    def release(self):
        off, nlive = self.marks.pop()
        newres = []
        for (b, s0, e0) in self.live[nlive:]:
            fence = set(b.readers)
            if b.last_w is not None:
                fence.add(b.last_w)
            per_eng = {}
            for d in fence:
                e = self.prog.ops[d].eng
                if d > per_eng.get(e, -1):
                    per_eng[e] = d
            newres.append((s0, e0, sorted(per_eng.values())))
        kept = []
        for (s0, e0, ops) in self.residue:
            if e0 <= off:
                kept.append((s0, e0, ops))
            else:
                covered = any(s1 <= s0 and e1 >= e0 for (s1, e1, _)
                              in newres)
                if not covered:
                    kept.append((s0, e0, ops))
        self.residue = kept + newres
        self.live = self.live[:nlive]
        self.off = off


def build_nc(debug=(), cc_groups=None):
    nc = bass.Bass("TRN2", target_bir_lowering=False)
    P = Prog()
    CC_GROUPS = cc_groups if cc_groups is not None else [[0, 1, 2, 3], [4, 5, 6, 7]]
    dbg_req = set(debug)

    def din(name, shape, dtype=F32):
        return nc.dram_tensor(name, list(shape), dtype,
                              kind="ExternalInput").ap()

    x_full = din("x_full", (S, DM))
    x_own_d = din("x_own", (T, DM))
    pos0_d = din("pos0", (1, 1))
    halo_sel_d = din("halo_sel", (4, 1))
    state_sel_d = din("state_sel", (4, 4))
    state_mask_d = din("state_mask", (4, 1))

    Wq_d = din("Wq", (DM, NH * (DN + DR)))
    Wqi_d = din("Wq_idx", (DM, HI * DI))
    Wki_d = din("Wk_idx", (DM, HI * DI))
    bqi_d = din("bq_idx", (HI * DI, 1))
    bki_d = din("bk_idx", (HI * DI, 1))
    idxw_d = din("idx_w", (4, 1))
    wctrl_d = din("w_ctrl", (128, 16))
    bctrl_d = din("b_ctrl", (1, 1))
    Wdkv_d = din("W_dkv", (DM, KR + DR))
    kvnw_d = din("kv_norm_w", (1, KR))
    Wukv_d = din("W_ukv", (KR, NH * (DN + DV)))
    Wo_d = din("Wo", (NH * DV, DM))
    inpT_d = din("in_projT", (DM, PROJ))
    convw_d = din("conv_w", (CONV, 4))
    convb_d = din("conv_b", (CONV, 1))
    dtb_d = din("dt_bias", (MNH, 1))
    Alog_d = din("A_log", (MNH, 1))
    Dd_d = din("D_in", (MNH, 1))
    mnw_d = din("mnorm_w", (1, 4096))
    outT_d = din("out_projT", (4096, DM))
    wgT_d = din("w_gT", (DM, INTER))
    wuT_d = din("w_uT", (DM, INTER))
    wdT_d = din("w_dT", (INTER, DM))

    y_out = nc.dram_tensor("y_out", [T, DM], F32, kind="ExternalOutput").ap()

    def dscr(name, shape, dtype=F32):
        return nc.dram_tensor(name, list(shape), dtype).ap()

    kn_s = dscr("kn_s", (NH * DN, S))
    v_s = dscr("v_s", (S, NH * DV))
    qn_s = dscr("qn_s", (NH * DN, T))
    qr_s = dscr("qr_s", (NH * DR, T))

    dram_bufs = {}

    def DBuf(name):
        if name not in dram_bufs:
            dram_bufs[name] = Buf("dram." + name)
        return dram_bufs[name]

    ctx = contextlib.ExitStack()

    def sb(name, parts, frees, dtype=F32):
        h = ctx.enter_context(nc.sbuf_tensor(name, [parts, frees], dtype))
        return h, Buf(name)

    ident, ident_b = sb("ident", 128, 128)
    identh, identh_b = sb("identh", 128, 128, BF16)
    qpos_c, qpos_c_b = sb("qpos_c", 128, 1)
    onecol, onecol_b = sb("onecol", 128, 1)
    pos0_t, pos0_b = sb("pos0_t", 128, 1)
    scl_t, scl_b = sb("scl", 128, 1)
    cexp_t, cexp_b = sb("cexp", 128, 1)
    bqi_t, bqi_b = sb("bqi", 128, 2)
    bki_t, bki_b = sb("bki", 128, 2)
    idxw_t, idxw_b = sb("idxw", 128, 4)
    ccq_t, ccq_b = sb("ccq", 128, 128)
    thr_t, thr_b = sb("thr", 128, NT)
    kq_t, kq_b = sb("kq", 128, 1)
    eps6_t, eps6_b = sb("eps6", 128, 1)
    eps5_t, eps5_b = sb("eps5", 128, 1)
    wctrl_t, wctrl_b = sb("wctrl", 128, 16)
    bctrl_t, bctrl_b = sb("bctrl", 1, 1)
    ffsc_t, ffsc_b = sb("ffsc", 128, 4)

    ar_h, _ = sb("arena", 128, 51000)
    A = Arena(ar_h, 51000, "A", prog=P)

    psum = []
    for i in range(8):
        h = ctx.enter_context(nc.psum_tensor(f"ps{i}", [128, 512], F32))
        psum.append((h, Buf(f"psum{i}")))

    sems = {e: ctx.enter_context(nc.semaphore(f"sem_{e}")) for e in ENGINES}
    semq = {e: ctx.enter_context(nc.semaphore(f"semq_{e}")) for e in ENGINES}
    ccsems = [ctx.enter_context(nc.semaphore(f"cc{i}")) for i in range(4)]

    def dbg(name, ap, rbuf, shape, dtype=F32):
        if name in dbg_req:
            t = nc.dram_tensor(f"dbg_{name}", list(shape), dtype,
                               kind="ExternalOutput").ap()
            P.op("gp", lambda e, a=ap, t=t: e.dma_start(out=t, in_=a),
                 reads=[rbuf], writes=[DBuf("dbg_" + name)], dma=True)

    def LD(dst_ap, dbuf, src_ap, sbufs=(), q="sync", slow=False):
        P.op(q, lambda e, d=dst_ap, s=src_ap, sl=slow: e.dma_start(
            out=d, in_=s, allow_slow_non_contiguous=sl),
            reads=list(sbufs), writes=[dbuf], dma=True)

    def ST(dst_ap, dbuf, src_ap, sbufs, q="gp"):
        P.op(q, lambda e, d=dst_ap, s=src_ap: e.dma_start(out=d, in_=s),
             reads=list(sbufs), writes=[dbuf], dma=True)

    def MM(out_ap, pb, lhsT, rhs, rbufs, start, stop):
        P.op("pe", lambda e, o=out_ap, l=lhsT, r=rhs, s1=start, s2=stop:
             e.matmul(o, l, r, start=s1, stop=s2),
             reads=list(rbufs), writes=[pb])

    def TRP(out_ap, pb, in_ap, in_buf, bf=False, start=True, stop=True):
        idt, idb = (identh, identh_b) if bf else (ident, ident_b)
        pk = in_ap.shape[0]
        P.op("pe", lambda e, o=out_ap, i=in_ap, d=idt[0:pk, 0:pk], s1=start,
             s2=stop: e.matmul(o, i, d, start=s1, stop=s2,
                               is_transpose=True),
             reads=[in_buf, idb], writes=[pb])

    def TS(eng, out, ob, in0, rb, s1, s2, op0, op1=OP.bypass, accum=None,
           ab=None):
        wr = [ob] + ([ab] if ab is not None else [])
        P.op(eng, lambda e, o=out, i=in0, a=s1, b2=s2, p0=op0, p1=op1,
             ac=accum: e.tensor_scalar(o, i, a, b2, p0, op1=p1,
                                       accum_out=ac),
             reads=list(rb), writes=wr)

    def TT(eng, out, ob, in0, in1, rb, op):
        P.op(eng, lambda e, o=out, a=in0, b2=in1, p=op:
             e.tensor_tensor(o, a, b2, p), reads=list(rb), writes=[ob])

    def STT(eng, out, ob, in0, scalar, in1, rb, op0, op1):
        P.op(eng, lambda e, o=out, a=in0, s=scalar, b2=in1, p0=op0, p1=op1:
             e.scalar_tensor_tensor(o, a, s, b2, p0, p1),
             reads=list(rb), writes=[ob])

    def ACTV(out, ob, in0, rb, func, bias=0.0, scale=1.0, accum=None,
             ab=None):
        wr = [ob] + ([ab] if ab is not None else [])
        P.op("act", lambda e, o=out, i=in0, b2=bias, s=scale, f=func,
             ac=accum: e.activation(o, i, f, bias=b2, scale=s,
                                    accum_out=ac),
             reads=list(rb), writes=wr)

    def CPY(eng, out, ob, in0, rb):
        if eng == "act":
            P.op("act", lambda e, o=out, i=in0: e.activation(
                o, i, AF.Copy, bias=0.0, scale=1.0),
                reads=list(rb), writes=[ob])
        else:
            P.op(eng, lambda e, o=out, i=in0: e.tensor_copy(o, i),
                 reads=list(rb), writes=[ob])

    def RED(eng, out, ob, in0, rb, op, absval=False):
        P.op(eng, lambda e, o=out, i=in0, p=op, v=absval: e.tensor_reduce(
            o, i, AX, p, apply_absolute_value=v), reads=list(rb),
            writes=[ob])

    def MEMSET(eng, ap, b, val):
        P.op(eng, lambda e, a=ap, v=val: e.memset(a, v), writes=[b])

    def PB(out, ob, in0, rb):
        p_out = out.shape[0]
        p_in = in0.shape[0]
        same = (in0.tensor is out.tensor and in0.offset == out.offset)
        if not same:
            P.op("sync", lambda e, o=out[0:p_in, :], i=in0:
                 e.dma_start(out=o, in_=i),
                 reads=list(rb), writes=[ob], dma=True)
        cur = p_in
        while cur < p_out:
            n = min(cur, p_out - cur)
            P.op("sync", lambda e, o=out[cur:cur + n, :], i=out[0:n, :]:
                 e.dma_start(out=o, in_=i),
                 reads=[ob], writes=[ob], dma=True)
            cur += n

    def CPRED(eng, out, ob, mask, data, rb):
        P.op(eng, lambda e, o=out, m=mask, d=data: e.copy_predicated(o, m, d),
             reads=list(rb) + [ob], writes=[ob])

    def IOTA(out_ap, ob, pattern, ch_mult, scratch_ap, scratch_b):
        P.op("gp", lambda e, o=scratch_ap, p=pattern, c=ch_mult:
             e.iota(o, pattern=p, base=0, channel_multiplier=c),
             writes=[scratch_b])
        CPY("dve", out_ap, ob, scratch_ap, [scratch_b])

    def FLOORF(out, ob, tin, tb, iscr, iscr_b, fscr, fscr_b, mskv, eng="dve"):
        CPY(eng, iscr, iscr_b, tin, [tb])
        CPY(eng, fscr, fscr_b, iscr, [iscr_b])
        TT(eng, mskv, iscr_b, tin, fscr, [tb, fscr_b], OP.is_lt)
        TT(eng, out, ob, fscr, mskv, [fscr_b, iscr_b], OP.subtract)

    def RECIP(out, ob, in0, rb):
        P.op("dve", lambda e, o=out, i=in0: e.reciprocal(o, i),
             reads=list(rb), writes=[ob])

    def rsqrt_col(name, ssq_ap, ssq_b, scale, eps_ap, eps_b):
        ve, ve_b = A.alloc(name + ".ve", 1)
        sq, sq_b = A.alloc(name + ".sq", 1)
        r, r_b = A.alloc(name + ".r", 1)
        t, t_b = A.alloc(name + ".t", 1)
        TS("dve", ve, ve_b, ssq_ap, [ssq_b, eps_b], scale, eps_ap, OP.mult,
           OP.add)
        ACTV(sq, sq_b, ve, [ve_b], AF.Sqrt)
        RECIP(r, r_b, sq, [sq_b])
        for _ in range(2):
            TT("dve", t, t_b, r, r, [r_b], OP.mult)
            TS("dve", t, t_b, t, [t_b, ve_b], ve, None, OP.mult)
            TS("dve", t, t_b, t, [t_b], -0.5, 1.5, OP.mult, OP.add)
            TT("dve", r, r_b, r, t, [r_b, t_b], OP.mult)
        return r, r_b

    def layernorm_tile(name, out, ob, xt, xb, width=DM, scratch=None):
        A.mark()
        m, m_b = A.alloc(name + ".m", 1)
        ssq, ssq_b = A.alloc(name + ".ssq", 1)
        if scratch is None:
            sc, sc_b = A.alloc(name + ".sc", width)
        else:
            sc, sc_b = scratch
        RED("dve", m, m_b, xt, [xb], OP.add)
        TS("dve", m, m_b, m, [m_b], 1.0 / width, None, OP.mult)
        TS("dve", out, ob, xt, [xb, m_b], m, None, OP.subtract)
        ACTV(sc, sc_b, out, [ob], AF.Square, accum=ssq, ab=ssq_b)
        r, r_b = rsqrt_col(name + ".rs", ssq, ssq_b, 1.0 / width,
                           eps5_t[:, 0:1], eps5_b)
        TS("dve", out, ob, out, [ob, r_b], r, None, OP.mult)
        A.release()

    def transpose_tile(out, ob, in_ap, ib, n, pbanks=(6, 7), bf=False,
                       colmap=None):
        if colmap is None:
            colmap = lambda c: c * 128
        for c4 in range(0, n, 4):
            m = min(4, n - c4)
            pi = pbanks[(c4 // 4) % len(pbanks)]
            h, pb = psum[pi]
            hv = h[:, :].bitcast(BF16) if bf else h[:, :]
            for j in range(m):
                c = c4 + j
                TRP(hv[:, j * 128:(j + 1) * 128], pb,
                    in_ap[:, c * 128:(c + 1) * 128], ib, bf=bf,
                    start=(j == 0), stop=(j == m - 1))
            contig = all(colmap(c4 + j + 1) == colmap(c4 + j) + 128
                         for j in range(m - 1))
            if contig:
                CPY("act", out[:, colmap(c4):colmap(c4) + m * 128], ob,
                    hv[:, 0:m * 128], [pb])
            else:
                for j in range(m):
                    CPY("act", out[:, colmap(c4 + j):colmap(c4 + j) + 128],
                        ob, hv[:, j * 128:(j + 1) * 128], [pb])

    MEMSET("gp", ident[:, :], ident_b, 0.0)
    P.op("gp", lambda e: e.affine_select(
        out=ident[:, :], in_=ident[:, :], compare_op=OP.not_equal, fill=1.0,
        base=0, pattern=[[-1, 128]], channel_multiplier=1),
        reads=[ident_b], writes=[ident_b])
    CPY("dve", identh[:, :], identh_b, ident[:, :], [ident_b])
    qpi, qpi_b = A.alloc("qpos_i", 1, mybir.dt.int32)
    IOTA(qpos_c[:, :], qpos_c_b, [[0, 1]], 1, qpi[:, :], qpi_b)
    MEMSET("gp", onecol[:, :], onecol_b, 1.0)
    MEMSET("dve", eps6_t[:, :], eps6_b, 1e-6)
    MEMSET("dve", eps5_t[:, :], eps5_b, 1e-5)
    MEMSET("dve", scl_t[:, :], scl_b, SCALE)
    MEMSET("dve", cexp_t[:, :], cexp_b, -2.0 * math.log(10000.0) / 64.0)
    MEMSET("dve", ccq_t[:, :], ccq_b, 0.0)
    LD(pos0_t[0:1, 0:1], pos0_b, pos0_d)
    PB(pos0_t[:, :], pos0_b, pos0_t[0:1, 0:1], [pos0_b])
    LD(wctrl_t[:, :], wctrl_b, wctrl_d)
    LD(bctrl_t[:, :], bctrl_b, bctrl_d)
    LD(bqi_t[:, 0:1], bqi_b, bqi_d[0:128, 0:1])
    LD(bqi_t[:, 1:2], bqi_b, bqi_d[128:256, 0:1])
    LD(bki_t[:, 0:1], bki_b, bki_d[0:128, 0:1])
    LD(bki_t[:, 1:2], bki_b, bki_d[128:256, 0:1])
    LD(idxw_t[0:4, 0:1], idxw_b, idxw_d)
    for j in range(1, 4):
        P.op("sync", lambda e, o=idxw_t[0:1, j:j + 1],
             i=idxw_t[j:j + 1, 0:1]: e.dma_start(out=o, in_=i),
             reads=[idxw_b], writes=[idxw_b], dma=True)
    for j in range(4):
        PB(idxw_t[:, j:j + 1], idxw_b, idxw_t[0:1, j:j + 1], [idxw_b])

    A.mark()
    kr_t, kr_b = A.alloc("kr", S)
    ki0, ki0_b = A.alloc("ki0", S)
    ki1, ki1_b = A.alloc("ki1", S)
    qi0, qi0_b = A.alloc("qi0", T)
    qi1, qi1_b = A.alloc("qi1", T)
    kvnw_t, kvnw_b = A.alloc("kvnw", KR)
    LD(kvnw_t[0:1, :], kvnw_b, kvnw_d)
    PB(kvnw_t[:, :], kvnw_b, kvnw_t[0:1, :], [kvnw_b])

    A.mark()
    cosk, cosk_b = A.alloc("cosk", S)
    sink, sink_b = A.alloc("sink", S)
    cosq, cosq_b = A.alloc("cosq", T)
    sinq, sinq_b = A.alloc("sinq", T)
    A.mark()
    ivf, ivf_b = A.alloc("ivf", 1)
    iota_r, iota_r_b = A.alloc("iota_r", S)
    ioi, ioi_b = A.alloc("iota_i", S, mybir.dt.int32)
    IOTA(iota_r[0:1, :], iota_r_b, [[1, S]], 0, ioi[0:1, :], ioi_b)
    IOTA(ivf[0:64, :], ivf_b, [[0, 1]], 1, ioi[0:64, 0:1], ioi_b)
    ACTV(ivf[0:64, :], ivf_b, ivf[0:64, :], [ivf_b], AF.Exp,
         scale=cexp_t[0:64, 0:1])
    P.op("sync", lambda e: e.dma_start(out=ivf[32:64, 0:1],
                                       in_=ivf[0:32, 0:1]),
         reads=[ivf_b], writes=[ivf_b], dma=True)

    def make_tables(cos_ap, cos_b, sin_ap, sin_b, n, own):
        twopi = 2.0 * math.pi
        CH = 2048
        A.mark()
        ang, ang_b = A.alloc("tb.ang", CH)
        raw, raw_b = A.alloc("tb.raw", CH)
        qq, qq_b = A.alloc("tb.q", CH)
        qi_, qi_b2 = A.alloc("tb.qi", CH, mybir.dt.int32)
        qf_, qf_b2 = A.alloc("tb.qf", CH)
        qif = qi_.bitcast(F32)
        for ofs in range(0, n, CH):
            nn = min(CH, n - ofs)
            if own:
                TS("dve", raw[0:1, 0:nn], raw_b, iota_r[0:1, ofs:ofs + nn],
                   [iota_r_b, pos0_b], pos0_t[0:1, 0:1], None, OP.add)
                PB(ang[0:64, 0:nn], ang_b, raw[0:1, 0:nn], [raw_b])
            else:
                PB(ang[0:64, 0:nn], ang_b, iota_r[0:1, ofs:ofs + nn],
                   [iota_r_b])
            TS("dve", ang[0:64, 0:nn], ang_b, ang[0:64, 0:nn],
               [ang_b, ivf_b], ivf[0:64, 0:1], None, OP.mult)
            for phase, (dst, dstb) in ((math.pi, (sin_ap, sin_b)),
                                       (math.pi * 1.5, (cos_ap, cos_b))):
                TS("dve", raw[0:64, 0:nn], raw_b, ang[0:64, 0:nn], [ang_b],
                   phase, None, OP.add)
                TS("dve", qq[0:64, 0:nn], qq_b, raw[0:64, 0:nn], [raw_b],
                   1.0 / twopi, None, OP.mult)
                FLOORF(qq[0:64, 0:nn], qq_b, qq[0:64, 0:nn], qq_b,
                       qi_[0:64, 0:nn], qi_b2, qf_[0:64, 0:nn], qf_b2,
                       qif[0:64, 0:nn])
                STT("dve", raw[0:64, 0:nn], raw_b, qq[0:64, 0:nn], -twopi,
                    raw[0:64, 0:nn], [qq_b, raw_b], OP.mult, OP.add)
                TS("dve", raw[0:64, 0:nn], raw_b, raw[0:64, 0:nn], [raw_b],
                   -math.pi, None, OP.add)
                ACTV(dst[0:64, ofs:ofs + nn], dstb, raw[0:64, 0:nn],
                     [raw_b], AF.Sin)
        TS("dve", sin_ap[0:32, 0:n], sin_b, sin_ap[0:32, 0:n], [sin_b],
           -1.0, None, OP.mult)
        A.release()

    make_tables(cosk, cosk_b, sink, sink_b, S, own=False)
    make_tables(cosq, cosq_b, sinq, sinq_b, T, own=True)
    A.release()

    def rope_fm(out, ob, x_ap, xb, cos_ap, cos_b, sin_ap, sin_b, c0, n):
        A.mark()
        a, a_b = A.alloc("rope.a", n)
        rx, rx_b = A.alloc("rope.rx", n)
        P.op("sync", lambda e, o=rx[0:32, 0:n], i=x_ap[32:64, :]:
             e.dma_start(out=o, in_=i), reads=[xb], writes=[rx_b], dma=True)
        P.op("sync", lambda e, o=rx[32:64, 0:n], i=x_ap[0:32, :]:
             e.dma_start(out=o, in_=i), reads=[xb], writes=[rx_b], dma=True)
        TT("dve", a[0:64, :], a_b, x_ap, cos_ap[0:64, c0:c0 + n],
           [xb, cos_b], OP.mult)
        TT("dve", rx[0:64, :], rx_b, rx[0:64, 0:n],
           sin_ap[0:64, c0:c0 + n], [rx_b, sin_b], OP.mult)
        TT("dve", out, ob, a[0:64, :], rx[0:64, :], [a_b, rx_b], OP.add)
        A.release()

    cnT_s = dscr("cnT_s", (4 * 128, S))
    A.mark()
    Wdkv_t, Wdkv_b = A.alloc("Wdkv", 16 * 576)
    LD(Wdkv_t, Wdkv_b, Wdkv_d.rearrange("(c p) f -> p c f", p=128))
    Wki_t, Wki_b = A.alloc("Wki", 16 * 256)
    LD(Wki_t, Wki_b, Wki_d.rearrange("(c p) f -> p c f", p=128))

    pool_ps, pool_ps_b = psum[5]
    xtl = [A.alloc(f"p1.x{i}", DM) for i in range(2)]
    lnb, lnb_b = A.alloc("p1.ln", DM)
    lnT, lnT_b = A.alloc("p1.lnT", 16 * 128)
    ckv, ckv_b = A.alloc("p1.ckv", 576)
    cn, cn_b = A.alloc("p1.cn", KR)
    cnT, cnT_b = A.alloc("p1.cnT", 4 * 128)
    sq_s, sq_s_b = A.alloc("p1.sqs", KR)

    LD(xtl[0][0], xtl[0][1], x_full[0:128, :])
    for st in range(NST):
        xt, xb = xtl[st % 2]
        if st + 1 < NST:
            LD(xtl[(st + 1) % 2][0], xtl[(st + 1) % 2][1],
               x_full[(st + 1) * 128:(st + 2) * 128, :])
        layernorm_tile(f"p1.ln{st}", lnb, lnb_b, xt, xb,
                        scratch=(lnT[:, 0:DM], lnT_b))
        for c in range(16):
            MM(pool_ps[:, c:c + 1], pool_ps_b,
               lnb[:, c * 128:(c + 1) * 128], onecol[:, :],
               [lnb_b, onecol_b], start=(st == 0 and c == 0),
               stop=(st == NST - 1 and c == 15))
        transpose_tile(lnT, lnT_b, lnb, lnb_b, 16)
        for c in range(16):
            MM(psum[0][0][:, 0:512], psum[0][1],
               lnT[:, c * 128:(c + 1) * 128],
               Wdkv_t[:, c * 576:c * 576 + 512], [lnT_b, Wdkv_b],
               start=(c == 0), stop=(c == 15))
        for c in range(16):
            MM(psum[1][0][:, 0:64], psum[1][1],
               lnT[:, c * 128:(c + 1) * 128],
               Wdkv_t[:, c * 576 + 512:(c + 1) * 576], [lnT_b, Wdkv_b],
               start=(c == 0), stop=(c == 15))
        CPY("act", ckv[:, 0:512], ckv_b, psum[0][0][:, 0:512], [psum[0][1]])
        CPY("act", ckv[:, 512:576], ckv_b, psum[1][0][:, 0:64], [psum[1][1]])
        TRP(psum[2][0][0:64, 0:128], psum[2][1], ckv[:, 512:576], ckv_b)
        CPY("dve", kr_t[0:64, st * 128:(st + 1) * 128], kr_b,
            psum[2][0][0:64, 0:128], [psum[2][1]])
        ssq, ssq_b = A.alloc(f"p1.ssq{st}", 1)
        ACTV(sq_s, sq_s_b, ckv[:, 0:512], [ckv_b], AF.Square, accum=ssq,
             ab=ssq_b)
        r, r_b = rsqrt_col(f"p1.r{st}", ssq, ssq_b, 1.0 / KR,
                           eps6_t[:, 0:1], eps6_b)
        TS("dve", cn, cn_b, ckv[:, 0:512], [ckv_b, r_b], r, None, OP.mult)
        TT("dve", cn, cn_b, cn, kvnw_t[:, 0:KR], [cn_b, kvnw_b], OP.mult)
        transpose_tile(cnT, cnT_b, cn, cn_b, 4, pbanks=(3,))
        for rc in range(4):
            ST(cnT_s[rc * 128:(rc + 1) * 128, st * 128:(st + 1) * 128],
               DBuf("cnT_s"), cnT[:, rc * 128:(rc + 1) * 128], [cnT_b])
        for fc, (kit, kib) in enumerate(((ki0, ki0_b), (ki1, ki1_b))):
            for c in range(16):
                MM(psum[0][0][:, 0:128], psum[0][1],
                   Wki_t[:, c * 256 + fc * 128:c * 256 + (fc + 1) * 128],
                   lnT[:, c * 128:(c + 1) * 128], [Wki_b, lnT_b],
                   start=(c == 0), stop=(c == 15))
            P.op("act", lambda e, o=kit[:, st * 128:(st + 1) * 128],
                 i=psum[0][0][:, 0:128], b2=bki_t[:, fc:fc + 1]:
                 e.activation(o, i, AF.Identity, bias=b2, scale=1.0),
                 reads=[psum[0][1], bki_b], writes=[kib])

    pool_sb, pool_sb_b = A.alloc("pool", 16)
    CPY("dve", pool_sb, pool_sb_b, pool_ps[:, 0:16], [pool_ps_b])
    TS("dve", pool_sb, pool_sb_b, pool_sb, [pool_sb_b], 1.0 / S, None,
       OP.mult)
    pw, pw_b = A.alloc("pw", 16)
    TT("dve", pw, pw_b, pool_sb, wctrl_t[:, :], [pool_sb_b, wctrl_b],
       OP.mult)
    pr, pr_b = A.alloc("pr", 1)
    RED("dve", pr, pr_b, pw, [pw_b], OP.add)
    MM(psum[0][0][0:1, 0:1], psum[0][1], pr, onecol[:, :],
       [pr_b, onecol_b], start=True, stop=True)
    frac, frac_b = A.alloc("frac", 1)
    P.op("act", lambda e, o=frac[0:1, 0:1], i=psum[0][0][0:1, 0:1],
         b2=bctrl_t[0:1, 0:1]: e.activation(o, i, AF.Sigmoid, bias=b2,
                                            scale=1.0),
         reads=[psum[0][1], bctrl_b], writes=[frac_b])
    TS("dve", frac[0:1, 0:1], frac_b, frac[0:1, 0:1], [frac_b], float(S),
       0.5, OP.mult, OP.add)
    km, km_b = A.alloc("km", 1, mybir.dt.int32)
    kf, kf_b = A.alloc("kf", 1)
    FLOORF(frac[0:1, 0:1], frac_b, frac[0:1, 0:1], frac_b, km[0:1, 0:1],
           km_b, kf[0:1, 0:1], kf_b, km.bitcast(F32)[0:1, 0:1])
    TS("dve", frac[0:1, 0:1], frac_b, frac[0:1, 0:1], [frac_b], 1.0,
       float(S), OP.max, OP.min)
    PB(kq_t[:, :], kq_b, frac[0:1, 0:1], [frac_b])
    dbg("kq", kq_t[:, 0:1], kq_b, (128, 1))
    A.release()

    A.mark()
    kro, kro_b = A.alloc("p1.kro", S // 2)
    for hf in range(2):
        rope_fm(kro[0:64, :], kro_b, kr_t[0:64, hf * (S // 2):(hf + 1) * (S // 2)],
                kr_b, cosk, cosk_b, sink, sink_b, hf * (S // 2), S // 2)
        CPY("dve", kr_t[0:64, hf * (S // 2):(hf + 1) * (S // 2)], kr_b,
            kro[0:64, :], [kro_b])
    A.release()

    dbg("kr", kr_t[0:64, :], kr_b, (64, S))
    dbg("ki0", ki0[:, :], ki0_b, (128, S))
    dbg("ki1", ki1[:, :], ki1_b, (128, S))

    A.mark()
    xo, xo_b = A.alloc("p1b.x", DM)
    lno, lno_b = A.alloc("p1b.ln", DM)
    lnoT, lnoT_b = A.alloc("p1b.lnT", 16 * 512)
    sqsc, sqsc_b = A.alloc("p1b.sq", DM)
    wqh = [A.alloc(f"p1b.wq{i}", 16 * 192) for i in range(2)]
    qev, qev_b = A.alloc("p1b.qev", 512)
    qrp, qrp_b = A.alloc("p1b.qrp", 512)

    for half in range(2):
        for tt in range(4):
            ti = half * 4 + tt
            LD(xo, xo_b, x_own_d[ti * 128:(ti + 1) * 128, :])
            layernorm_tile(f"p1b.ln{ti}", lno, lno_b, xo, xo_b,
                               scratch=(sqsc, sqsc_b))
            transpose_tile(lnoT, lnoT_b, lno, lno_b, 16,
                           colmap=lambda c, t=tt: c * 512 + t * 128)
        def _ld_h(h2):
            wq2, wq2_b = wqh[h2 % 2]
            LD(wq2[:, 0:16 * 192], wq2_b,
               Wq_d[:, h2 * 192:(h2 + 1) * 192].rearrange(
                   "(c p) f -> p c f", p=128),
               q="act" if h2 % 2 else "sync")
        _ld_h(0)
        for h in range(NH):
            wq_t, wq_b2 = wqh[h % 2]
            if h + 1 < NH:
                _ld_h(h + 1)
            for c in range(16):
                MM(psum[0][0][:, 0:512], psum[0][1],
                   wq_t[:, c * 192:c * 192 + 128],
                   lnoT[:, c * 512:(c + 1) * 512], [wq_b2, lnoT_b],
                   start=(c == 0), stop=(c == 15))
            for c in range(16):
                MM(psum[1][0][0:64, 0:512], psum[1][1],
                   wq_t[:, c * 192 + 128:(c + 1) * 192],
                   lnoT[:, c * 512:(c + 1) * 512], [wq_b2, lnoT_b],
                   start=(c == 0), stop=(c == 15))
            P.op("act", lambda e, o=qev[:, 0:512], i=psum[0][0][:, 0:512],
                 s=scl_t[:, 0:1]: e.activation(o, i, AF.Identity, bias=0.0,
                                               scale=s),
                 reads=[psum[0][1], scl_b], writes=[qev_b])
            ST(qn_s[h * 128:(h + 1) * 128, half * 512:(half + 1) * 512],
               DBuf("qn_s"), qev[:, 0:512], [qev_b])
            P.op("act", lambda e, o=qrp[0:64, 0:512],
                 i=psum[1][0][0:64, 0:512], s=scl_t[0:64, 0:1]:
                 e.activation(o, i, AF.Identity, bias=0.0, scale=s),
                 reads=[psum[1][1], scl_b], writes=[qrp_b])
            rope_fm(qrp[0:64, 0:512], qrp_b, qrp[0:64, 0:512], qrp_b,
                    cosq, cosq_b, sinq, sinq_b, half * 512, 512)
            ST(qr_s[h * 64:(h + 1) * 64, half * 512:(half + 1) * 512],
               DBuf("qr_s"), qrp[0:64, 0:512], [qrp_b])
        for fc, (qit, qib) in enumerate(((qi0, qi0_b), (qi1, qi1_b))):
            wq_t, wq_b2 = wqh[0]
            LD(wq_t[:, 0:16 * 128], wq_b2,
               Wqi_d[:, fc * 128:(fc + 1) * 128].rearrange(
                   "(c p) f -> p c f", p=128))
            for c in range(16):
                MM(psum[0][0][:, 0:512], psum[0][1],
                   wq_t[:, c * 128:(c + 1) * 128],
                   lnoT[:, c * 512:(c + 1) * 512], [wq_b2, lnoT_b],
                   start=(c == 0), stop=(c == 15))
            P.op("act", lambda e, o=qit[:, half * 512:(half + 1) * 512],
                 i=psum[0][0][:, 0:512], b2=bqi_t[:, fc:fc + 1]:
                 e.activation(o, i, AF.Identity, bias=b2, scale=1.0),
                 reads=[psum[0][1], bqi_b], writes=[qib])
    A.release()
    A.release()

    dbg("qi0", qi0[:, :], qi0_b, (128, T))
    dbg("qi1", qi1[:, :], qi1_b, (128, T))
    A.mark()
    Wukv_t, Wukv_b = A.alloc("Wukv", 4 * 4096)
    LD(Wukv_t, Wukv_b, Wukv_d.rearrange("(c p) f -> p c f", p=128))
    cnTg, cnTg_b = A.alloc("p1c.cnT", 4 * 512)
    kvev, kvev_b = A.alloc("p1c.kvev", 512)
    for stg in range(8):
        for rc in range(4):
            LD(cnTg[:, rc * 512:(rc + 1) * 512], cnTg_b,
               cnT_s[rc * 128:(rc + 1) * 128, stg * 512:(stg + 1) * 512],
               sbufs=[DBuf("cnT_s")])
        for h in range(NH):
            pk, pk_b = psum[h % 2]
            for rc in range(4):
                MM(pk[:, 0:512], pk_b,
                   Wukv_t[:, rc * 4096 + h * 256:rc * 4096 + h * 256 + 128],
                   cnTg[:, rc * 512:(rc + 1) * 512], [Wukv_b, cnTg_b],
                   start=(rc == 0), stop=(rc == 3))
            CPY("act", kvev, kvev_b, pk[:, 0:512], [pk_b])
            ST(kn_s[h * 128:(h + 1) * 128, stg * 512:(stg + 1) * 512],
               DBuf("kn_s"), kvev[:, 0:512], [kvev_b])
        for ts4 in range(4):
            for hg in range(4):
                pv, pv_b = psum[2 + (hg % 2)]
                for hh in range(4):
                    h = hg * 4 + hh
                    for rc in range(4):
                        MM(pv[:, hh * 128:(hh + 1) * 128], pv_b,
                           cnTg[:, rc * 512 + ts4 * 128:
                                rc * 512 + (ts4 + 1) * 128],
                           Wukv_t[:, rc * 4096 + h * 256 + 128:
                                  rc * 4096 + h * 256 + 256],
                           [Wukv_b, cnTg_b],
                           start=(hh == 0 and rc == 0),
                           stop=(hh == 3 and rc == 3))
                CPY("act", kvev, kvev_b, pv[:, :], [pv_b])
                ST(v_s[stg * 512 + ts4 * 128:stg * 512 + (ts4 + 1) * 128,
                       hg * 512:(hg + 1) * 512],
                   DBuf("v_s"), kvev[:, 0:512], [kvev_b])
    A.release()

    keep_t, keep_b = A.alloc("keep", NT * S, mybir.dt.uint8)
    A.mark()
    CHW = 2048
    wst = [A.alloc(f"ffw.w{i}", CHW) for i in range(2)]
    sacc, sacc_b = A.alloc("ffw.sacc", 1)
    red1, red1_b = A.alloc("ffw.red", 1)
    a1t, a1t_b = A.alloc("ffw.a1", CHW)
    a2t, a2t_b = A.alloc("ffw.a2", CHW)
    wqo = [A.alloc(f"ffw.wq{i}", CHW, BF16) for i in range(2)]
    A.mark()
    kpos_bt, kpos_bb = A.alloc("p2.kpos", S)
    idx2 = [A.alloc(f"p2.idx{w}", S) for w in range(2)]
    kpi = idx2[0][0].bitcast(mybir.dt.int32)[:, 0:S]
    kpi_b = idx2[0][1]
    IOTA(kpos_bt[:, :], kpos_bb, [[1, S]], 0, kpi[:, :], kpi_b)
    relu2 = [A.alloc(f"p2.relu{w}", 512) for w in range(2)]
    junk2 = [A.alloc(f"p2.junk{w}", S, mybir.dt.uint8) for w in range(2)]
    caus, caus_b = A.alloc("p2.caus", S, mybir.dt.uint8)
    lo2 = [A.alloc(f"p2.lo{w}", 1) for w in range(2)]
    hi2 = [A.alloc(f"p2.hi{w}", 1) for w in range(2)]
    mid2 = [A.alloc(f"p2.mid{w}", 1) for w in range(2)]
    cnt2 = [A.alloc(f"p2.cnt{w}", 1) for w in range(2)]
    take2 = []
    tkn2 = []
    for w in range(2):
        t_, tb_ = A.alloc(f"p2.take{w}", 4, mybir.dt.uint8)
        take2.append((t_[:, 0:1], tb_))
        t_, tb_ = A.alloc(f"p2.tkn{w}", 4, mybir.dt.uint8)
        tkn2.append((t_[:, 0:1], tb_))
    qpos_t, qpos_t_b = A.alloc("p2.qpos", 1)
    rfl, rfl_b = A.alloc("p2.rfl", 1)

    qis = ((qi0, qi0_b), (qi1, qi1_b))
    kis = ((ki0, ki0_b), (ki1, ki1_b))
    for qp in range(4):
        for w in range(2):
            qt = qp * 2 + w
            idxt, idxt_b = idx2[w]
            for kb in range(8):
                for hiq in range(4):
                    qsrc, qsb = qis[hiq // 2]
                    ksrc, ksb = kis[hiq // 2]
                    roff = (hiq % 2) * 64
                    pb_, pbb_ = psum[w]
                    MM(pb_[:, 0:512], pbb_,
                       qsrc[roff:roff + 64, qt * 128:(qt + 1) * 128],
                       ksrc[roff:roff + 64, kb * 512:(kb + 1) * 512],
                       [qsb, ksb], start=True, stop=True)
                    ACTV(relu2[w][0], relu2[w][1], pb_[:, 0:512], [pbb_],
                         AF.Relu)
                    if hiq == 0:
                        TS("dve", idxt[:, kb * 512:(kb + 1) * 512], idxt_b,
                           relu2[w][0], [relu2[w][1], idxw_b],
                           idxw_t[:, 0:1], None, OP.mult)
                    else:
                        STT("dve", idxt[:, kb * 512:(kb + 1) * 512], idxt_b,
                            relu2[w][0], idxw_t[:, hiq:hiq + 1],
                            idxt[:, kb * 512:(kb + 1) * 512],
                            [relu2[w][1], idxw_b, idxt_b], OP.mult, OP.add)
            RED("dve", lo2[w][0], lo2[w][1], idxt, [idxt_b], OP.min)
            RED("dve", hi2[w][0], hi2[w][1], idxt, [idxt_b], OP.max)
            TS("dve", mid2[w][0], mid2[w][1], hi2[w][0], [hi2[w][1]], -1.0,
               None, OP.mult)
            TT("dve", mid2[w][0], mid2[w][1], mid2[w][0], hi2[w][0],
               [mid2[w][1], hi2[w][1]], OP.max)
            TS("dve", mid2[w][0], mid2[w][1], mid2[w][0], [mid2[w][1]],
               1e-6, 1e-20, OP.mult, OP.add)
            TT("dve", hi2[w][0], hi2[w][1], hi2[w][0], mid2[w][0],
               [hi2[w][1], mid2[w][1]], OP.add)
        for it in range(N_ITER):
            for w in range(2):
                lo, lo_b = lo2[w]
                hi, hi_b = hi2[w]
                mid, mid_b = mid2[w]
                TT("dve", mid, mid_b, lo, hi, [lo_b, hi_b], OP.add)
                TS("dve", mid, mid_b, mid, [mid_b], 0.5, None, OP.mult)
            for w in range(2):
                idxt, idxt_b = idx2[w]
                TS("dve", junk2[w][0], junk2[w][1], idxt,
                   [idxt_b, mid2[w][1]], mid2[w][0], None, OP.is_ge,
                   OP.add, accum=cnt2[w][0], ab=cnt2[w][1])
            for w in range(2):
                TT("dve", take2[w][0], take2[w][1], cnt2[w][0],
                   kq_t[:, 0:1], [cnt2[w][1], kq_b], OP.is_ge)
                TS("dve", tkn2[w][0], tkn2[w][1], take2[w][0],
                   [take2[w][1]], -1.0, 1.0, OP.mult, OP.add)
            for w in range(2):
                CPRED("dve", lo2[w][0], lo2[w][1], take2[w][0], mid2[w][0],
                      [take2[w][1], mid2[w][1]])
                CPRED("dve", hi2[w][0], hi2[w][1], tkn2[w][0], mid2[w][0],
                      [tkn2[w][1], mid2[w][1]])
        for w in range(2):
            qt = qp * 2 + w
            idxt, idxt_b = idx2[w]
            lo, lo_b = lo2[w]
            CPY("dve", thr_t[:, qt:qt + 1], thr_b, lo, [lo_b])
            TS("dve", qpos_t, qpos_t_b, qpos_c[:, 0:1], [qpos_c_b, pos0_b],
               pos0_t[:, 0:1], float(qt * 128), OP.add, OP.add)
            TS("dve", caus[:, 0:S], caus_b, kpos_bt[:, :],
               [kpos_bb, qpos_t_b], qpos_t, None, OP.is_le)
            TS("dve", keep_t[:, qt * S:(qt + 1) * S], keep_b, idxt,
               [idxt_b, lo_b], lo, None, OP.is_ge)
            TT("dve", keep_t[:, qt * S:(qt + 1) * S], keep_b,
               keep_t[:, qt * S:(qt + 1) * S], caus[:, 0:S],
               [keep_b, caus_b], OP.mult)
            RED("dve", rfl, rfl_b, keep_t[:, qt * S:(qt + 1) * S], [keep_b],
                OP.max)
            TS("dve", ccq_t[:, qt:qt + 1], ccq_b, rfl, [rfl_b], -1.0, 1e9,
               OP.add, OP.mult)
    A.release()
    dbg("thr", thr_t[:, :], thr_b, (128, NT))
    dbg("ccq", ccq_t[:, 0:NT], ccq_b, (128, NT))
    dbg("keep0", keep_t[:, 0:S], keep_b, (128, S), mybir.dt.uint8)
    x2_s = dscr("x2_s", (T, DM))
    x3_s = dscr("x3_s", (T, DM))
    h2_s = dscr("h2_s", (T, INTER))
    wq_g = dscr("wq_g", (DM, INTER), BF16)
    wq_u = dscr("wq_u", (DM, INTER), BF16)
    wq_d = dscr("wq_d", (INTER, DM), BF16)
    sca05 = {}

    for wname, wd, wqd in (("g", wgT_d, wq_g), ("u", wuT_d, wq_u),
                           ("d", wdT_d, wq_d)):
        nrows, ncols = wd.shape[0], wd.shape[1]
        nch = nrows // 128
        nsub = ncols // CHW if ncols > CHW else 1
        csz = ncols // nsub
        first = True
        nit = nch * nsub

        def _src(i, wd=wd, csz=csz, nsub=nsub):
            c, sub = divmod(i, nsub)
            return wd[c * 128:(c + 1) * 128, sub * csz:(sub + 1) * csz]
        LD(wst[0][0][:, 0:csz], wst[0][1], _src(0), q="act")
        for i in range(nit):
            wt, wt_b = wst[i % 2]
            RED("dve", red1, red1_b, wt[:, 0:csz], [wt_b], OP.add,
                absval=True)
            if first:
                CPY("dve", sacc, sacc_b, red1, [red1_b])
                first = False
            else:
                TT("dve", sacc, sacc_b, sacc, red1, [sacc_b, red1_b],
                   OP.add)
            if i + 1 < nit:
                LD(wst[(i + 1) % 2][0][:, 0:csz], wst[(i + 1) % 2][1],
                   _src(i + 1), q="act")
        MM(psum[0][0][0:1, 0:1], psum[0][1], sacc, onecol[:, :],
           [sacc_b, onecol_b], start=True, stop=True)
        wi = {"g": 0, "u": 1, "d": 2}[wname]
        a05 = ffsc_t[:, wi:wi + 1]
        TS("dve", a05[0:1, 0:1], ffsc_b, psum[0][0][0:1, 0:1], [psum[0][1]],
           0.5 / (DM * INTER), 0.5e-5, OP.mult, OP.add)
        PB(a05, ffsc_b, a05[0:1, 0:1], [ffsc_b])
        n05, n05_b = A.alloc(f"ffw.n05{wname}", 1)
        TS("dve", n05, n05_b, a05, [ffsc_b], -1.0, None, OP.mult)
        sca05[wname] = (a05, ffsc_b)
        LD(wst[0][0][:, 0:csz], wst[0][1], _src(0), q="act")
        for i in range(nit):
            c, sub = divmod(i, nsub)
            wt, wt_b = wst[i % 2]
            ACTV(a1t[:, 0:csz], a1t_b, wt[:, 0:csz], [wt_b, n05_b],
                 AF.Sign, bias=n05)
            ACTV(a2t[:, 0:csz], a2t_b, wt[:, 0:csz], [wt_b, ffsc_b],
                 AF.Sign, bias=a05)
            if i + 1 < nit:
                LD(wst[(i + 1) % 2][0][:, 0:csz], wst[(i + 1) % 2][1],
                   _src(i + 1), q="act")
            wq2, wq2_b = wqo[i % 2]
            TT("gp", wq2[:, 0:csz], wq2_b, a1t[:, 0:csz],
               a2t[:, 0:csz], [a1t_b, a2t_b], OP.add)
            ST(wqd[c * 128:(c + 1) * 128, sub * csz:(sub + 1) * csz],
               DBuf("wq" + wname), wq2[:, 0:csz], [wq2_b])
    A.release()


    ao_s = dscr("ao_s", (NH * DV, T))
    x1_s = dscr("x1_s", (T, DM))
    A.mark()
    acc, acc_b = A.alloc("p3.acc", 64 * 130)
    pT = [A.alloc(f"p3.pT{i}", 512) for i in range(4)]
    addT = [A.alloc(f"p3.addT{i}", 512) for i in range(4)]
    ksb = [A.alloc(f"p3.k{i}", 128) for i in range(2)]
    vsb = [A.alloc(f"p3.v{i}", 132) for i in range(4)]
    qn_t = [A.alloc(f"p3.qn{i}", 512) for i in range(2)]
    qr_t = [A.alloc(f"p3.qr{i}", 512) for i in range(2)]
    negcc, negcc_b2 = A.alloc("p3.negcc", 512)
    negrow, negrow_b = A.alloc("p3.negrow", 512)
    ccT, ccT_b = A.alloc("p3.ccT", 128)
    keepbf, keepbf_b = A.alloc("p3.keepbf", 512, BF16)
    u_t, u_b = A.alloc("p3.u", 512)
    u2_t, u2_b = A.alloc("p3.u2", 512)
    rec_t, rec_b = A.alloc("p3.rec", 1)
    aout, aout_b = A.alloc("p3.aout", 2048)

    for i in range(4):
        MEMSET("dve", vsb[i][0][:, 128:129], vsb[i][1], 1.0)

    TRP(psum[6][0][:, 0:128], psum[6][1], ccq_t[:, 0:128], ccq_b)
    CPY("act", ccT[:, 0:128], ccT_b, psum[6][0][:, 0:128], [psum[6][1]])

    for qpass in range(2):
        P.op("sync", lambda e, o=negrow[0:1, 0:512],
             i=ccT[qpass * 4:qpass * 4 + 4, 0:128]:
             e.dma_start(out=o, in_=i),
             reads=[ccT_b], writes=[negrow_b], dma=True)
        TS("dve", negrow[0:1, 0:512], negrow_b, negrow[0:1, 0:512],
           [negrow_b], -1.0, None, OP.mult)
        PB(negcc[:, 0:512], negcc_b2, negrow[0:1, 0:512], [negrow_b])
        for kbg in range(8):
            kbs = [kbg * 4 + i for i in range(4)]
            for i, kb in enumerate(kbs):
                for j4 in range(4):
                    qt = qpass * 4 + j4
                    CPY("dve", keepbf[:, j4 * 128:(j4 + 1) * 128], keepbf_b,
                        keep_t[:, qt * S + kb * 128:qt * S + (kb + 1) * 128],
                        [keep_b])
                hv = psum[6][0][:, :].bitcast(BF16)
                for j4 in range(4):
                    TRP(hv[:, j4 * 128:(j4 + 1) * 128], psum[6][1],
                        keepbf[:, j4 * 128:(j4 + 1) * 128], keepbf_b,
                        bf=True, start=(j4 == 0), stop=(j4 == 3))
                TS("dve", addT[i][0][:, 0:512], addT[i][1], hv[:, 0:512],
                   [psum[6][1]], 1e9, -1e9, OP.mult, OP.add)
            def _ldq(h2, par):
                LD(qn_t[par][0][:, 0:512], qn_t[par][1],
                   qn_s[h2 * 128:(h2 + 1) * 128,
                        qpass * 512:(qpass + 1) * 512],
                   sbufs=[DBuf("qn_s")])
                LD(qr_t[par][0][0:64, 0:512], qr_t[par][1],
                   qr_s[h2 * 64:(h2 + 1) * 64,
                        qpass * 512:(qpass + 1) * 512],
                   sbufs=[DBuf("qr_s")])

            def _ldkk(h2, i2, par):
                kt2, kt2_b = ksb[par]
                LD(kt2[:, 0:128], kt2_b,
                   kn_s[h2 * 128:(h2 + 1) * 128,
                        kbs[i2] * 128:(kbs[i2] + 1) * 128],
                   sbufs=[DBuf("kn_s")])

            def _ldv(h2, i2):
                LD(vsb[i2][0][:, 0:128], vsb[i2][1],
                   v_s[kbs[i2] * 128:(kbs[i2] + 1) * 128,
                       h2 * 128:(h2 + 1) * 128], sbufs=[DBuf("v_s")])

            def _ldk(h2, i2, par):
                _ldkk(h2, i2, par)
                _ldv(h2, i2)

            _ldq(0, 0)
            _ldk(0, 0, 0)
            for h in range(NH):
                qn, qn_b2 = qn_t[h % 2]
                qr, qr_b2 = qr_t[h % 2]
                for i, kb in enumerate(kbs):
                    kt, kt_b = ksb[i % 2]
                    vt, vt_b = vsb[i]
                    sc, sc_b = psum[kb % 2]
                    MM(sc[:, 0:512], sc_b, kt[:, 0:128], qn[:, 0:512],
                       [kt_b, qn_b2], start=True, stop=False)
                    MM(sc[:, 0:512], sc_b,
                       kr_t[0:64, kb * 128:(kb + 1) * 128], qr[0:64, 0:512],
                       [kr_b, qr_b2], start=False, stop=True)
                    if i + 1 < 4:
                        _ldk(h, i + 1, (i + 1) % 2)
                    elif h + 1 < NH:
                        _ldkk(h + 1, 0, 0)
                    if i == 0 and h + 1 < NH:
                        _ldq(h + 1, (h + 1) % 2)
                    TT("dve", u_t[:, 0:512], u_b, sc[:, 0:512],
                       addT[i][0][:, 0:512], [sc_b, addT[i][1]], OP.add)
                    TT("gp", u2_t[:, 0:512], u2_b, u_t[:, 0:512],
                       negcc[:, 0:512], [u_b, negcc_b2], OP.add)
                    ACTV(pT[i][0][:, 0:512], pT[i][1], u2_t[:, 0:512],
                         [u2_b], AF.Exp)
                for q4 in range(4):
                    pa, pa_b = psum[2 + q4]
                    for i in range(4):
                        MM(pa[:, 0:129], pa_b,
                           pT[i][0][:, q4 * 128:(q4 + 1) * 128],
                           vsb[i][0][:, 0:129], [pT[i][1], vsb[i][1]],
                           start=(i == 0), stop=(i == 3))
                    base = (h * 4 + q4) * 130
                    if kbg == 0:
                        CPY("dve", acc[:, base:base + 129], acc_b,
                            pa[:, 0:129], [pa_b])
                    else:
                        TT("dve", acc[:, base:base + 129], acc_b,
                           acc[:, base:base + 129], pa[:, 0:129],
                           [acc_b, pa_b], OP.add)
                if h + 1 < NH:
                    _ldv(h + 1, 0)
        for q4 in range(4):
            qt = qpass * 4 + q4
            for h in range(NH):
                base = (h * 4 + q4) * 130
                RECIP(rec_t[:, 0:1], rec_b, acc[:, base + 128:base + 129],
                      [acc_b])
                TS("dve", aout[:, h * 128:(h + 1) * 128], aout_b,
                   acc[:, base:base + 128], [acc_b, rec_b], rec_t[:, 0:1],
                   None, OP.mult)
            for c4 in range(0, 16, 4):
                hps, hps_b = psum[6 + (c4 // 4) % 2]
                for j in range(4):
                    c = c4 + j
                    TRP(hps[:, j * 128:(j + 1) * 128], hps_b,
                        aout[:, c * 128:(c + 1) * 128], aout_b,
                        start=(j == 0), stop=(j == 3))
                CPY("act", u_t[:, 0:512], u_b, hps[:, 0:512], [hps_b])
                for j in range(4):
                    c = c4 + j
                    ST(ao_s[c * 128:(c + 1) * 128, qt * 128:(qt + 1) * 128],
                       DBuf("ao_s"), u_t[:, j * 128:(j + 1) * 128], [u_b])
    A.release()
    A.release()

    A.mark()
    Wo_t, Wo_b = A.alloc("Wo", 16 * 2048)
    LD(Wo_t, Wo_b, Wo_d.rearrange("(c p) f -> p c f", p=128))
    aoT, aoT_b = A.alloc("p3b.aoT", 16 * 128)
    xot, xot_b = A.alloc("p3b.x", DM)
    x1t, x1t_b = A.alloc("p3b.x1", DM)
    for qt in range(NT):
        LD(aoT, aoT_b, ao_s[:, qt * 128:(qt + 1) * 128].rearrange(
            "(c p) f -> p c f", p=128), sbufs=[DBuf("ao_s")])
        LD(xot, xot_b, x_own_d[qt * 128:(qt + 1) * 128, :])
        for oc in range(4):
            po, po_b = psum[oc]
            for c in range(16):
                MM(po[:, 0:512], po_b, aoT[:, c * 128:(c + 1) * 128],
                   Wo_t[:, c * 2048 + oc * 512:c * 2048 + (oc + 1) * 512],
                   [aoT_b, Wo_b], start=(c == 0), stop=(c == 15))
            TT("dve", x1t[:, oc * 512:(oc + 1) * 512], x1t_b,
               po[:, 0:512], xot[:, oc * 512:(oc + 1) * 512],
               [po_b, xot_b], OP.add)
        ST(x1_s[qt * 128:(qt + 1) * 128, :], DBuf("x1_s"), x1t[:, :],
           [x1t_b])
    dbg("x1", x1t[:, :], x1t_b, (128, DM))
    A.release()

    def ffn_apply(tag, xin_s, xout_s, xout_name, xin_name):
        A.mark()
        xs1g, xs1g_b = A.alloc(tag + ".xs1g", 8)
        xs1u, xs1u_b = A.alloc(tag + ".xs1u", 8)
        rm2, rm2_b = A.alloc(tag + ".rm2", 8)
        xsd, xsd_b = A.alloc(tag + ".xsd", 8)
        A.mark()
        xqT = [A.alloc(f"{tag}.xqT{i}", 2048, BF16) for i in range(8)]
        xt, xt_b = A.alloc(tag + ".x", DM)
        lnt, lnt_b = A.alloc(tag + ".ln", DM)
        sqs, sqs_b = A.alloc(tag + ".sq", DM)
        t_, t_b = A.alloc(tag + ".t", DM)
        m_, m_b = A.alloc(tag + ".m", DM)
        i_, i_b = A.alloc(tag + ".i", DM, mybir.dt.int32)
        i_f = i_.bitcast(F32)[:, 0:DM]
        xqb, xqb_b = A.alloc(tag + ".xqb", DM, BF16)
        mx, mx_b = A.alloc(tag + ".mx", 1)
        xs_, xs_b = A.alloc(tag + ".xs", 1)
        r_, r_b = A.alloc(tag + ".r", 1)
        for tt in range(8):
            LD(xt, xt_b, xin_s[tt * 128:(tt + 1) * 128, :],
               sbufs=[DBuf(xin_name)])
            layernorm_tile(f"{tag}.ln{tt}", lnt, lnt_b, xt, xt_b,
                           scratch=(sqs, sqs_b))
            RED("dve", mx, mx_b, lnt, [lnt_b], OP.max, absval=True)
            TS("dve", xs_, xs_b, mx, [mx_b], 1.0 / 127.0, 1e-5, OP.mult,
               OP.add)
            RECIP(r_, r_b, xs_, [xs_b])
            TT("dve", xs1g[:, tt:tt + 1], xs1g_b, xs_, ffsc_t[:, 0:1],
               [xs_b, ffsc_b], OP.mult)
            TT("dve", xs1u[:, tt:tt + 1], xs1u_b, xs_, ffsc_t[:, 1:2],
               [xs_b, ffsc_b], OP.mult)
            TS("dve", t_, t_b, lnt, [lnt_b, r_b], r_, 0.5, OP.mult, OP.add)
            CPY("dve", i_, i_b, t_, [t_b])
            CPY("dve", m_, m_b, i_, [i_b])
            TT("dve", i_f, i_b, t_, m_, [t_b, m_b], OP.is_lt)
            TT("dve", xqb, xqb_b, m_, i_f, [m_b, i_b], OP.subtract)
            transpose_tile(xqT[tt][0], xqT[tt][1], xqb, xqb_b, 16, bf=True)
        wstr = [A.alloc(f"{tag}.w{i}", 512, BF16) for i in range(2)]
        gbuf = [A.alloc(f"{tag}.g{i}", 512) for i in range(8)]
        ubuf, ubuf_b = A.alloc(tag + ".u", 512)
        h2c, h2c_b = A.alloc(tag + ".h2c", 512)
        mxc, mxc_b = A.alloc(tag + ".mxc", 1)
        def _gsrc(j):
            of_, ctr_ = divmod(j, 16)
            return wq_g[ctr_ * 128:(ctr_ + 1) * 128,
                        of_ * 512:(of_ + 1) * 512]

        def _usrc(j):
            of_, ctr_ = divmod(j, 16)
            return wq_u[ctr_ * 128:(ctr_ + 1) * 128,
                        of_ * 512:(of_ + 1) * 512]
        LD(wstr[0][0], wstr[0][1], _gsrc(0), sbufs=[DBuf("wqg")], q="act")
        for of in range(16):
            for ctr in range(16):
                j = of * 16 + ctr
                wt, wt_b = wstr[j % 2]
                for tt in range(8):
                    MM(psum[tt][0][:, 0:512], psum[tt][1],
                       xqT[tt][0][:, ctr * 128:(ctr + 1) * 128], wt,
                       [xqT[tt][1], wt_b], start=(ctr == 0),
                       stop=(ctr == 15))
                nj = j + 1
                if ctr + 1 < 16:
                    LD(wstr[nj % 2][0], wstr[nj % 2][1], _gsrc(nj),
                       sbufs=[DBuf("wqg")], q="act" if nj % 2 else "sync")
                else:
                    LD(wstr[nj % 2][0], wstr[nj % 2][1], _usrc(j - 15),
                       sbufs=[DBuf("wqu")], q="act" if nj % 2 else "sync")
            for tt in range(8):
                P.op("act", lambda e, o=gbuf[tt][0], i=psum[tt][0][:, 0:512],
                     s=xs1g[:, tt:tt + 1]: e.activation(
                         o, i, AF.Silu, bias=0.0, scale=s),
                     reads=[psum[tt][1], xs1g_b], writes=[gbuf[tt][1]])
            for ctr in range(16):
                j = of * 16 + ctr
                wt, wt_b = wstr[ctr % 2]
                for tt in range(8):
                    MM(psum[tt][0][:, 0:512], psum[tt][1],
                       xqT[tt][0][:, ctr * 128:(ctr + 1) * 128], wt,
                       [xqT[tt][1], wt_b], start=(ctr == 0),
                       stop=(ctr == 15))
                if ctr + 1 < 16:
                    LD(wstr[(ctr + 1) % 2][0], wstr[(ctr + 1) % 2][1],
                       _usrc(j + 1), sbufs=[DBuf("wqu")],
                       q="act" if (ctr + 1) % 2 else "sync")
                elif of + 1 < 16:
                    LD(wstr[(ctr + 1) % 2][0], wstr[(ctr + 1) % 2][1],
                       _gsrc((of + 1) * 16), sbufs=[DBuf("wqg")],
                       q="act" if (ctr + 1) % 2 else "sync")
            for tt in range(8):
                P.op("act", lambda e, o=ubuf, i=psum[tt][0][:, 0:512],
                     s=xs1u[:, tt:tt + 1]: e.activation(
                         o, i, AF.Identity, bias=0.0, scale=s),
                     reads=[psum[tt][1], xs1u_b], writes=[ubuf_b])
                TT("dve", h2c, h2c_b, gbuf[tt][0], ubuf,
                   [gbuf[tt][1], ubuf_b], OP.mult)
                RED("dve", mxc, mxc_b, h2c, [h2c_b], OP.max, absval=True)
                if of == 0:
                    CPY("dve", rm2[:, tt:tt + 1], rm2_b, mxc, [mxc_b])
                else:
                    TT("dve", rm2[:, tt:tt + 1], rm2_b, rm2[:, tt:tt + 1],
                       mxc, [rm2_b, mxc_b], OP.max)
                ST(h2_s[tt * 128:(tt + 1) * 128, of * 512:(of + 1) * 512],
                   DBuf("h2_s"), h2c, [h2c_b])
        A.release()
        A.mark()
        xq2T = [A.alloc(f"{tag}.xq2T{i}", INTER, BF16) for i in range(8)]
        hc, hc_b = A.alloc(tag + ".hc", 512)
        tc, tc_b = A.alloc(tag + ".tc", 512)
        mc, mc_b = A.alloc(tag + ".mc", 512)
        ic_, ic_b = A.alloc(tag + ".ic", 512, mybir.dt.int32)
        ic_f = ic_.bitcast(F32)[:, 0:512]
        qc2, qc2_b = A.alloc(tag + ".qc2", 512, BF16)
        r2, r2_b = A.alloc(tag + ".r2", 1)
        xs2, xs2_b = A.alloc(tag + ".xs2", 1)
        for tt in range(8):
            TS("dve", xs2, xs2_b, rm2[:, tt:tt + 1], [rm2_b], 1.0 / 127.0,
               1e-5, OP.mult, OP.add)
            RECIP(r2, r2_b, xs2, [xs2_b])
            TT("dve", xsd[:, tt:tt + 1], xsd_b, xs2, ffsc_t[:, 2:3],
               [xs2_b, ffsc_b], OP.mult)
            for oc in range(16):
                LD(hc, hc_b, h2_s[tt * 128:(tt + 1) * 128,
                                  oc * 512:(oc + 1) * 512],
                   sbufs=[DBuf("h2_s")])
                TS("dve", tc, tc_b, hc, [hc_b, r2_b], r2, 0.5, OP.mult,
                   OP.add)
                CPY("dve", ic_, ic_b, tc, [tc_b])
                CPY("dve", mc, mc_b, ic_, [ic_b])
                TT("dve", ic_f, ic_b, tc, mc, [tc_b, mc_b], OP.is_lt)
                TT("dve", qc2, qc2_b, mc, ic_f, [mc_b, ic_b], OP.subtract)
                hv = psum[6][0][:, :].bitcast(BF16)
                for j in range(4):
                    TRP(hv[:, j * 128:(j + 1) * 128], psum[6][1],
                        qc2[:, j * 128:(j + 1) * 128], qc2_b, bf=True,
                        start=(j == 0), stop=(j == 3))
                CPY("act", xq2T[tt][0][:, oc * 512:(oc + 1) * 512],
                    xq2T[tt][1], hv[:, 0:512], [psum[6][1]])
        wsd = [A.alloc(f"{tag}.wd{i}", 512, BF16) for i in range(2)]
        xrt, xrt_b = A.alloc(tag + ".xr", 512)
        ot, ot_b = A.alloc(tag + ".ot", 512)
        def _dsrc(j):
            dc_, ct_ = divmod(j, 64)
            return wq_d[ct_ * 128:(ct_ + 1) * 128,
                        dc_ * 512:(dc_ + 1) * 512]
        LD(wsd[0][0], wsd[0][1], _dsrc(0), sbufs=[DBuf("wqd")], q="act")
        for dcol in range(4):
            for ctr in range(64):
                j = dcol * 64 + ctr
                wt, wt_b = wsd[j % 2]
                for tt in range(8):
                    MM(psum[tt][0][:, 0:512], psum[tt][1],
                       xq2T[tt][0][:, ctr * 128:(ctr + 1) * 128], wt,
                       [xq2T[tt][1], wt_b], start=(ctr == 0),
                       stop=(ctr == 63))
                if j + 1 < 256:
                    LD(wsd[(j + 1) % 2][0], wsd[(j + 1) % 2][1],
                       _dsrc(j + 1), sbufs=[DBuf("wqd")],
                       q="act" if (j + 1) % 2 else "sync")
            for tt in range(8):
                LD(xrt, xrt_b, xin_s[tt * 128:(tt + 1) * 128,
                                     dcol * 512:(dcol + 1) * 512],
                   sbufs=[DBuf(xin_name)])
                STT("dve", ot, ot_b, psum[tt][0][:, 0:512],
                    xsd[:, tt:tt + 1], xrt, [psum[tt][1], xsd_b, xrt_b],
                    OP.mult, OP.add)
                ST(xout_s[tt * 128:(tt + 1) * 128,
                          dcol * 512:(dcol + 1) * 512], DBuf(xout_name),
                   ot, [ot_b])
        A.release()
        A.release()

    ffn_apply("f1", x1_s, x2_s, "x2_s", "x1_s")

    hs_s = dscr("hs_s", (CONV, T))
    gate_s = dscr("gate_s", (T, 4096))
    yn_s = dscr("yn_s", (4096, T))
    NG = len(CC_GROUPS[0])
    hcc_in = dscr("hcc_in", (128, 34 * 3))
    hcc_out = dscr("hcc_out", (NG * 128, 34 * 3))
    st_in = dscr("st_in", (MN, MNH * MHD + 64))
    st_out = dscr("st_out", (NG * MN, MNH * MHD + 64))

    A.mark()
    A_t, A_t_b = A.alloc("p5.A", 1)
    dtb_t, dtb_b = A.alloc("p5.dtb", 1)
    Db, Db_b = A.alloc("p5.Db", 64)
    convb_t, convb_b = A.alloc("p5.convb", 34)
    convw_t, convw_b = A.alloc("p5.convw", 34 * 4)
    hselb, hselb_b = A.alloc("p5.hselb", 4)
    dt_t, dt_t_b = A.alloc("p5.dt", T)
    dtA_t, dtA_b = A.alloc("p5.dtA", T)
    LTs, LTs_b = A.alloc("p5.LTs", 8 * 64)
    eLTs, eLTs_b = A.alloc("p5.eLTs", 8 * 64)
    logs_t, logs_b = A.alloc("p5.logs", 1)
    mnwb, mnwb_b = A.alloc("p5.mnwb", 4096)
    mut, mut_b = A.alloc("p5.mut", 128)

    LD(A_t[0:64, 0:1], A_t_b, Alog_d[0:64, 0:1])
    ACTV(A_t[0:64, 0:1], A_t_b, A_t[0:64, 0:1], [A_t_b], AF.Exp)
    TS("dve", A_t[0:64, 0:1], A_t_b, A_t[0:64, 0:1], [A_t_b], -1.0, None,
       OP.mult)
    LD(dtb_t[0:64, 0:1], dtb_b, dtb_d[0:64, 0:1])
    LD(Db[0:64, 0:1], Db_b, Dd_d[0:64, 0:1])
    TRP(psum[7][0][0:1, 0:64], psum[7][1], Db[0:64, 0:1], Db_b)
    CPY("dve", Db[0:1, 0:64], Db_b, psum[7][0][0:1, 0:64], [psum[7][1]])
    PB(Db[:, 0:64], Db_b, Db[0:1, 0:64], [Db_b])
    LD(convb_t[:, 0:34], convb_b,
       convb_d.rearrange("(c p) one -> p (c one)", p=128), slow=True)
    LD(convw_t[:, 0:34 * 4], convw_b,
       convw_d.rearrange("(c p) k -> p c k", p=128))
    LD(mnwb[0:1, 0:4096], mnwb_b, mnw_d)
    PB(mnwb[:, 0:4096], mnwb_b, mnwb[0:1, 0:4096], [mnwb_b])
    LD(hselb[0:4, 0:1], hselb_b, halo_sel_d)
    TRP(psum[7][0][0:1, 0:4], psum[7][1], hselb[0:4, 0:1], hselb_b)
    CPY("dve", hselb[0:1, 0:4], hselb_b, psum[7][0][0:1, 0:4], [psum[7][1]])
    PB(hselb[:, 0:4], hselb_b, hselb[0:1, 0:4], [hselb_b])
    MEMSET("gp", mut[:, 0:128], mut_b, 0.0)
    P.op("gp", lambda e: e.affine_select(
        out=mut[:, 0:128], in_=mut[:, 0:128], compare_op=OP.is_ge,
        fill=-1e38, base=0, pattern=[[1, 128]], channel_multiplier=-1),
        reads=[mut_b], writes=[mut_b])

    A.mark()
    ln3T, ln3T_b = A.alloc("p5.ln3T", 16 * 512)
    xt5, xt5_b = A.alloc("p5.x", DM)
    ln5, ln5_b = A.alloc("p5.ln", DM)
    sq5, sq5_b = A.alloc("p5.sq", DM)
    wst5 = [A.alloc(f"p5.w{i}", 16 * 512) for i in range(2)]
    gev, gev_b = A.alloc("p5.gev", 512)
    tailb, tail_b = A.alloc("p5.tail", 34 * 3)
    raw3, raw3_b = A.alloc("p5.raw3", 34 * 3)
    ext, ext_b = A.alloc("p5.ext", 515 + 5)
    cacc, cacc_b = A.alloc("p5.cacc", 512)
    hsa, hsa_b = A.alloc("p5.hsa", 512)

    for half in range(2):
        for tt in range(4):
            ti = half * 4 + tt
            LD(xt5, xt5_b, x2_s[ti * 128:(ti + 1) * 128, :],
               sbufs=[DBuf("x2_s")])
            layernorm_tile(f"p5.ln{ti}", ln5, ln5_b, xt5, xt5_b,
                           scratch=(sq5, sq5_b))
            transpose_tile(ln3T, ln3T_b, ln5, ln5_b, 16,
                           colmap=lambda c, t=tt: c * 512 + t * 128)
        def _ld_gc(gc2):
            wt2, wt2_b = wst5[gc2 % 2]
            LD(wt2[:, 0:16 * 512], wt2_b,
               inpT_d[:, gc2 * 512:(gc2 + 1) * 512].rearrange(
                   "(c p) f -> p c f", p=128),
               q="act" if gc2 % 2 else "sync")
        _ld_gc(0)
        for gc in range(8):
            wt, wt_b = wst5[gc % 2]
            if gc + 1 < 8:
                _ld_gc(gc + 1)
            for tsub in range(4):
                pg, pg_b = psum[tsub % 2]
                for c in range(16):
                    MM(pg[:, 0:512], pg_b,
                       ln3T[:, c * 512 + tsub * 128:
                            c * 512 + (tsub + 1) * 128],
                       wt[:, c * 512:(c + 1) * 512], [ln3T_b, wt_b],
                       start=(c == 0), stop=(c == 15))
                CPY("act", gev, gev_b, pg[:, 0:512], [pg_b])
                ST(gate_s[half * 512 + tsub * 128:
                          half * 512 + (tsub + 1) * 128,
                          gc * 512:(gc + 1) * 512], DBuf("gate_s"), gev,
                   [gev_b])
        def _ld_fc(fc2):
            fw2 = 128 if fc2 < 34 else 64
            f02 = 4096 + fc2 * 128
            wt2, wt2_b = wst5[fc2 % 2]
            LD(wt2[:, 0:16 * fw2], wt2_b,
               inpT_d[:, f02:f02 + fw2].rearrange("(c p) f -> p c f",
                                                  p=128),
               q="act" if fc2 % 2 else "sync")
        _ld_fc(0)
        for fc in range(34 + 1):
            fw = 128 if fc < 34 else 64
            f0 = 4096 + fc * 128
            wt, wt_b = wst5[fc % 2]
            ph, ph_b = psum[2 + fc % 2]
            for c in range(16):
                MM(ph[0:fw, 0:512], ph_b, wt[:, c * fw:(c + 1) * fw],
                   ln3T[:, c * 512:(c + 1) * 512], [wt_b, ln3T_b],
                   start=(c == 0), stop=(c == 15))
            if fc < 34:
                _ld_fc(fc + 1)
            if fc == 34:
                P.op("act", lambda e,
                     o=dt_t[0:64, half * 512:(half + 1) * 512],
                     i=ph[0:64, 0:512], b2=dtb_t[0:64, 0:1]:
                     e.activation(o, i, AF.Exp, bias=b2, scale=1.0),
                     reads=[ph_b, dtb_b], writes=[dt_t_b])
                P.op("act", lambda e,
                     o=dt_t[0:64, half * 512:(half + 1) * 512],
                     i=dt_t[0:64, half * 512:(half + 1) * 512]:
                     e.activation(o, i, AF.Ln, bias=1.0, scale=1.0),
                     reads=[dt_t_b], writes=[dt_t_b])
                continue
            if half == 0:
                CPY("dve", raw3[:, fc * 3:(fc + 1) * 3], raw3_b,
                    ph[0:128, 0:3], [ph_b])
                MEMSET("dve", ext[:, 0:3], ext_b, 0.0)
            else:
                CPY("dve", ext[:, 0:3], ext_b, tailb[:, fc * 3:(fc + 1) * 3],
                    [tail_b])
            CPY("act", ext[:, 3:515], ext_b, ph[0:128, 0:512], [ph_b])
            CPY("dve", tailb[:, fc * 3:(fc + 1) * 3], tail_b,
                ext[:, 512:515], [ext_b])
            TS("dve", cacc, cacc_b, ext[:, 3:515], [ext_b, convw_b],
               convw_t[:, fc * 4 + 3:fc * 4 + 4], None, OP.mult)
            for kk in range(3):
                STT("dve", cacc, cacc_b, ext[:, kk:kk + 512],
                    convw_t[:, fc * 4 + kk:fc * 4 + kk + 1], cacc,
                    [ext_b, convw_b, cacc_b], OP.mult, OP.add)
            P.op("act", lambda e, o=hsa, i=cacc,
                 b2=convb_t[:, fc:fc + 1]: e.activation(
                     o, i, AF.Silu, bias=b2, scale=1.0),
                 reads=[cacc_b, convb_b], writes=[hsa_b])
            ST(hs_s[fc * 128:(fc + 1) * 128,
                    half * 512:(half + 1) * 512], DBuf("hs_s"), hsa,
               [hsa_b])
    ST(hcc_in[:, :], DBuf("hcc_in"), tailb[:, 0:102], [tail_b])
    P.op("gp", lambda e: e.collective_compute(
        "AllGather", OP.bypass, replica_groups=CC_GROUPS,
        ins=[hcc_in.opt()], outs=[hcc_out.opt()]),
        reads=[DBuf("hcc_in")], writes=[DBuf("hcc_out")], cc=True)
    halom, halom_b = A.alloc("p5.halom", 102)
    halog, halog_b = A.alloc("p5.halog", 102)
    MEMSET("dve", halom[:, 0:102], halom_b, 0.0)
    for j in range(NG):
        LD(halog[:, 0:102], halog_b, hcc_out[j * 128:(j + 1) * 128, :],
           sbufs=[DBuf("hcc_out")])
        STT("dve", halom[:, 0:102], halom_b, halog[:, 0:102],
            hselb[:, j:j + 1], halom[:, 0:102],
            [halog_b, hselb_b, halom_b], OP.mult, OP.add)
    ext6, ext6_b = A.alloc("p5.ext6", 8)
    out3, out3_b = A.alloc("p5.out3", 4)
    for fc in range(34):
        CPY("dve", ext6[:, 0:3], ext6_b, halom[:, fc * 3:(fc + 1) * 3],
            [halom_b])
        CPY("dve", ext6[:, 3:6], ext6_b, raw3[:, fc * 3:(fc + 1) * 3],
            [raw3_b])
        TS("dve", out3[:, 0:3], out3_b, ext6[:, 3:6], [ext6_b, convw_b],
           convw_t[:, fc * 4 + 3:fc * 4 + 4], None, OP.mult)
        for kk in range(3):
            STT("dve", out3[:, 0:3], out3_b, ext6[:, kk:kk + 3],
                convw_t[:, fc * 4 + kk:fc * 4 + kk + 1], out3[:, 0:3],
                [ext6_b, convw_b, out3_b], OP.mult, OP.add)
        P.op("act", lambda e, o=out3[:, 0:3], i=out3[:, 0:3],
             b2=convb_t[:, fc:fc + 1]: e.activation(
                 o, i, AF.Silu, bias=b2, scale=1.0),
             reads=[out3_b, convb_b], writes=[out3_b])
        ST(hs_s[fc * 128:(fc + 1) * 128, 0:3], DBuf("hs_s"), out3[:, 0:3],
           [out3_b])
    A.release()

    A.mark()
    TS("dve", dtA_t[0:64, :], dtA_b, dt_t[0:64, :], [dt_t_b, A_t_b],
       A_t[0:64, 0:1], None, OP.mult)
    Lc, Lc_b = A.alloc("p5.Lc", 128)
    Lcs, Lcs_b = A.alloc("p5.Lcs", 8 * 128)
    for c in range(8):
        P.op("dve", lambda e, o=Lc[0:64, 0:128],
             d0=dtA_t[0:64, c * 128:(c + 1) * 128]:
             e.tensor_tensor_scan(o, d0, d0, 0.0, OP.add, OP.bypass),
             reads=[dtA_b], writes=[Lc_b])
        CPY("dve", Lcs[0:64, c * 128:(c + 1) * 128], Lcs_b,
            Lc[0:64, 0:128], [Lc_b])
        TRP(psum[7][0][0:128, 0:64], psum[7][1], Lc[0:64, 0:128], Lc_b)
        CPY("dve", LTs[:, c * 64:(c + 1) * 64], LTs_b,
            psum[7][0][0:128, 0:64], [psum[7][1]])
        ACTV(eLTs[:, c * 64:(c + 1) * 64], eLTs_b,
             LTs[:, c * 64:(c + 1) * 64], [LTs_b], AF.Exp)
        if c == 0:
            CPY("dve", logs_t[0:64, 0:1], logs_b, Lc[0:64, 127:128], [Lc_b])
        else:
            TT("dve", logs_t[0:64, 0:1], logs_b, logs_t[0:64, 0:1],
               Lc[0:64, 127:128], [logs_b, Lc_b], OP.add)

    def bc3i(ap2d, h, d):
        return ap2d.rearrange("p (h one) -> p h one", one=1).to_broadcast(
            [128, h, d])

    def bc3m(ap2d, h, d):
        return ap2d.rearrange("p (one d) -> p one d", one=1).to_broadcast(
            [128, h, d])

    def v3(ap2d, h, d):
        return ap2d.rearrange("p (h d) -> p h d", h=h)

    hscm, hscm_b = A.alloc("p5.hscm", 32 * 128)
    hstm, hstm_b = A.alloc("p5.hstm", 4096)
    xti, xti_b = A.alloc("p5.xti", 4096)
    Bcm, Bcm_b = A.alloc("p5.Bcm", T)
    Ccm, Ccm_b = A.alloc("p5.Ccm", T)
    Btm, Btm_b = A.alloc("p5.Btm", 128)
    dtT, dtT_b = A.alloc("p5.dtT", 64)
    Scur, Scur_b = A.alloc("p5.Scur", 4096)
    Pall, Pall_b = A.alloc("p5.Pall", 64)
    DecT, DecT_b = A.alloc("p5.DecT", 64)
    xdec, xdec_b = A.alloc("p5.xdec", 4096)
    stmp, stmp_b = A.alloc("p5.stmp", 4096)
    LD(Bcm[:, :], Bcm_b, hs_s[4096:4224, :], sbufs=[DBuf("hs_s")])
    LD(Ccm[:, :], Ccm_b, hs_s[4224:4352, :], sbufs=[DBuf("hs_s")])
    MEMSET("dve", Scur[:, :], Scur_b, 0.0)

    def chunk_prep(c):
        LD(hscm[:, :], hscm_b,
           hs_s[0:4096, c * 128:(c + 1) * 128].rearrange(
               "(f p) t -> p f t", p=128), sbufs=[DBuf("hs_s")])
        transpose_tile(hstm, hstm_b, hscm, hscm_b, 32, pbanks=(6,))
        TRP(psum[7][0][0:128, 0:64], psum[7][1],
            dt_t[0:64, c * 128:(c + 1) * 128], dt_t_b)
        CPY("dve", dtT[:, 0:64], dtT_b, psum[7][0][0:128, 0:64],
            [psum[7][1]])
        TT("dve", v3(xti[:, 0:4096], 64, 64), xti_b,
           v3(hstm[:, 0:4096], 64, 64), bc3i(dtT[:, 0:64], 64, 64),
           [hstm_b, dtT_b], OP.mult)
        TRP(psum[7][0][0:128, 0:128], psum[7][1],
            Bcm[:, c * 128:(c + 1) * 128], Bcm_b)
        CPY("dve", Btm[:, 0:128], Btm_b, psum[7][0][0:128, 0:128],
            [psum[7][1]])
        P.op("sync", lambda e, o=Pall[0:1, 0:64],
             i=eLTs[127:128, c * 64:(c + 1) * 64]: e.dma_start(out=o, in_=i),
             reads=[eLTs_b], writes=[Pall_b], dma=True)
        PB(Pall[:, 0:64], Pall_b, Pall[0:1, 0:64], [Pall_b])
        P.op("sync", lambda e, o=DecT[0:1, 0:64],
             i=LTs[127:128, c * 64:(c + 1) * 64]: e.dma_start(out=o, in_=i),
             reads=[LTs_b], writes=[DecT_b], dma=True)
        PB(DecT[:, 0:64], DecT_b, DecT[0:1, 0:64], [DecT_b])
        TT("dve", DecT[:, 0:64], DecT_b, DecT[:, 0:64],
           LTs[:, c * 64:(c + 1) * 64], [DecT_b, LTs_b], OP.subtract)
        ACTV(DecT[:, 0:64], DecT_b, DecT[:, 0:64], [DecT_b], AF.Exp)

    def state_step():
        TT("dve", v3(xdec[:, 0:4096], 64, 64), xdec_b,
           v3(xti[:, 0:4096], 64, 64), bc3i(DecT[:, 0:64], 64, 64),
           [xti_b, DecT_b], OP.mult)
        for g in range(8):
            MM(psum[g][0][:, 0:512], psum[g][1], Btm[:, 0:128],
               xdec[:, g * 512:(g + 1) * 512], [Btm_b, xdec_b],
               start=True, stop=True)
        TT("dve", v3(stmp[:, 0:4096], 64, 64), stmp_b,
           v3(Scur[:, 0:4096], 64, 64), bc3i(Pall[:, 0:64], 64, 64),
           [Scur_b, Pall_b], OP.mult)
        for g in range(8):
            TT("dve", Scur[:, g * 512:(g + 1) * 512], Scur_b,
               stmp[:, g * 512:(g + 1) * 512], psum[g][0][:, 0:512],
               [stmp_b, psum[g][1]], OP.add)

    for c in range(8):
        chunk_prep(c)
        state_step()
    A.mark()
    stp, stp_b = A.alloc("p5.stp", MNH * MHD + 64)
    CPY("dve", stp[:, 0:4096], stp_b, Scur[:, :], [Scur_b])
    MEMSET("dve", stp[:, 4096:4160], stp_b, 0.0)
    CPY("dve", stp[0:64, 4096:4097], stp_b, logs_t[0:64, 0:1], [logs_b])
    ST(st_in[:, :], DBuf("st_in"), stp[:, 0:4160], [stp_b])
    P.op("gp", lambda e: e.collective_compute(
        "AllGather", OP.bypass, replica_groups=CC_GROUPS,
        ins=[st_in.opt()], outs=[st_out.opt()]),
        reads=[DBuf("st_in")], writes=[DBuf("st_out")], cc=True)
    Sg, Sg_b = A.alloc("p5.Sg", 4160)
    logsg, logsg_b = A.alloc("p5.logsg", 4 * 64)
    plog, plog_b = A.alloc("p5.plog", 4)
    selb, selb_b = A.alloc("p5.selb", 16)
    smb, smb_b = A.alloc("p5.smb", 4)
    Dcy, Dcy_b = A.alloc("p5.Dcy", 4 * 64)
    for j in range(NG):
        P.op("sync", lambda e, o=logsg[0:64, j:j + 1],
             i=st_out[j * 128:j * 128 + 64, 4096:4097]:
             e.dma_start(out=o, in_=i, allow_slow_non_contiguous=True),
             reads=[DBuf("st_out")], writes=[logsg_b], dma=True)
    LD(selb[0:4, 0:4], selb_b, state_sel_d)
    LD(smb[0:4, 0:1], smb_b, state_mask_d)
    sc1, sc1_b = A.alloc("p5.sc1", 1)
    for j in range(NG):
        MEMSET("dve", plog[0:64, j:j + 1], plog_b, 0.0)
        for i in range(4):
            P.op("sync", lambda e, o=sc1[0:1, 0:1], i2=selb[j:j + 1, i:i + 1]:
                 e.dma_start(out=o, in_=i2),
                 reads=[selb_b], writes=[sc1_b], dma=True)
            PB(sc1[0:64, 0:1], sc1_b, sc1[0:1, 0:1], [sc1_b])
            STT("dve", plog[0:64, j:j + 1], plog_b, logsg[0:64, i:i + 1],
                sc1[0:64, 0:1], plog[0:64, j:j + 1],
                [logsg_b, sc1_b, plog_b], OP.mult, OP.add)
        ACTV(plog[0:64, j:j + 1], plog_b, plog[0:64, j:j + 1], [plog_b],
             AF.Exp)
        P.op("sync", lambda e, o=sc1[0:1, 0:1], i2=smb[j:j + 1, 0:1]:
             e.dma_start(out=o, in_=i2),
             reads=[smb_b], writes=[sc1_b], dma=True)
        PB(sc1[0:64, 0:1], sc1_b, sc1[0:1, 0:1], [sc1_b])
        TS("dve", plog[0:64, j:j + 1], plog_b, plog[0:64, j:j + 1],
           [plog_b, sc1_b], sc1[0:64, 0:1], None, OP.mult)
        TRP(psum[7][0][0:1, 0:64], psum[7][1], plog[0:64, j:j + 1], plog_b)
        CPY("dve", Dcy[0:1, j * 64:(j + 1) * 64], Dcy_b,
            psum[7][0][0:1, 0:64], [psum[7][1]])
        PB(Dcy[:, j * 64:(j + 1) * 64], Dcy_b, Dcy[0:1, j * 64:(j + 1) * 64],
           [Dcy_b])
    Sstart, Sstart_b = A.alloc("p5.Sstart", 4096)
    MEMSET("dve", Sstart[:, :], Sstart_b, 0.0)
    for j in range(NG):
        LD(Sg[:, 0:4160], Sg_b, st_out[j * 128:(j + 1) * 128, :],
           sbufs=[DBuf("st_out")])
        for h in range(MNH):
            STT("dve", Sstart[:, h * 64:(h + 1) * 64], Sstart_b,
                Sg[:, h * 64:(h + 1) * 64], Dcy[:, j * 64 + h:j * 64 + h + 1],
                Sstart[:, h * 64:(h + 1) * 64], [Sg_b, Dcy_b, Sstart_b],
                OP.mult, OP.add)
    CPY("dve", Scur[:, :], Scur_b, Sstart[:, :], [Sstart_b])
    A.release()

    CBT, CBT_b = A.alloc("p5.CBT", 128)
    Lrow, Lrow_b = A.alloc("p5.Lrow", 8192)
    Et, Et_b = Lrow, Lrow_b
    ych, ych_b = A.alloc("p5.y", 4096)
    yi2, yi2_b = stmp, stmp_b
    gch, gch_b = hscm, hscm_b
    ssqy, ssqy_b = A.alloc("p5.ssqy", 1)
    ynT, ynT_b = xdec, xdec_b
    sgch, sgch_b = gch, gch_b
    sqy, sqy_b = Et, Et_b
    for c in range(8):
        chunk_prep(c)
        MM(psum[5][0][:, 0:128], psum[5][1], Bcm[:, c * 128:(c + 1) * 128],
           Ccm[:, c * 128:(c + 1) * 128], [Bcm_b, Ccm_b], start=True,
           stop=True)
        CPY("dve", CBT[:, 0:128], CBT_b, psum[5][0][:, 0:128], [psum[5][1]])
        P.op("sync", lambda e, o=Lrow[0:1, 0:8192],
             i=Lcs[0:64, c * 128:(c + 1) * 128]: e.dma_start(out=o, in_=i),
             reads=[Lcs_b], writes=[Lrow_b], dma=True)
        PB(Lrow[:, 0:8192], Lrow_b, Lrow[0:1, 0:8192], [Lrow_b])
        TT("dve", v3(Et[:, 0:8192], 64, 128), Et_b,
           v3(Lrow[:, 0:8192], 64, 128),
           bc3i(LTs[:, c * 64:(c + 1) * 64], 64, 128),
           [Lrow_b, LTs_b], OP.subtract)
        TT("dve", v3(Et[:, 0:8192], 64, 128), Et_b,
           v3(Et[:, 0:8192], 64, 128), bc3m(mut[:, 0:128], 64, 128),
           [Et_b, mut_b], OP.add)
        ACTV(Et[:, 0:8192], Et_b, Et[:, 0:8192], [Et_b], AF.Exp)
        TT("dve", v3(Et[:, 0:8192], 64, 128), Et_b,
           v3(Et[:, 0:8192], 64, 128), bc3m(CBT[:, 0:128], 64, 128),
           [Et_b, CBT_b], OP.mult)
        for h in range(MNH):
            g, sl = h // 8, (h % 8) * 64
            MM(psum[g][0][:, sl:sl + 64], psum[g][1],
               Et[:, h * 128:(h + 1) * 128], xti[:, h * 64:(h + 1) * 64],
               [Et_b, xti_b], start=(h % 8 == 0), stop=(h % 8 == 7))
        for g in range(8):
            CPY("act", ych[:, g * 512:(g + 1) * 512], ych_b,
                psum[g][0][:, 0:512], [psum[g][1]])
        for g in range(8):
            MM(psum[g][0][:, 0:512], psum[g][1],
               Ccm[:, c * 128:(c + 1) * 128],
               Scur[:, g * 512:(g + 1) * 512], [Ccm_b, Scur_b],
               start=True, stop=True)
        for g in range(8):
            CPY("act", yi2[:, g * 512:(g + 1) * 512], yi2_b,
                psum[g][0][:, 0:512], [psum[g][1]])
        TT("dve", v3(yi2[:, 0:4096], 64, 64), yi2_b,
           v3(yi2[:, 0:4096], 64, 64),
           bc3i(eLTs[:, c * 64:(c + 1) * 64], 64, 64), [yi2_b, eLTs_b],
           OP.mult)
        TT("dve", ych[:, 0:4096], ych_b, ych[:, 0:4096], yi2[:, 0:4096],
           [ych_b, yi2_b], OP.add)
        TT("dve", v3(yi2[:, 0:4096], 64, 64), yi2_b,
           v3(hstm[:, 0:4096], 64, 64), bc3i(Db[:, 0:64], 64, 64),
           [hstm_b, Db_b], OP.mult)
        TT("dve", ych[:, 0:4096], ych_b, ych[:, 0:4096], yi2[:, 0:4096],
           [ych_b, yi2_b], OP.add)
        state_step()
        LD(gch[:, :], gch_b, gate_s[c * 128:(c + 1) * 128, :],
           sbufs=[DBuf("gate_s")])
        ACTV(sgch[:, :], sgch_b, gch[:, :], [gch_b], AF.Silu)
        TT("dve", ych[:, :], ych_b, ych[:, :], sgch[:, :],
           [ych_b, sgch_b], OP.mult)
        ACTV(sqy[:, 0:4096], sqy_b, ych[:, :], [ych_b], AF.Square,
             accum=ssqy, ab=ssqy_b)
        rmy, rmy_b = rsqrt_col(f"p5.rmy{c}", ssqy, ssqy_b, 1.0 / 4096.0,
                               eps5_t[:, 0:1], eps5_b)
        TS("dve", ych[:, :], ych_b, ych[:, :], [ych_b, rmy_b], rmy, None,
           OP.mult)
        TT("dve", ych[:, :], ych_b, ych[:, :], mnwb[:, 0:4096],
           [ych_b, mnwb_b], OP.mult)
        transpose_tile(ynT, ynT_b, ych, ych_b, 32, pbanks=(6,))
        for fc in range(32):
            ST(yn_s[fc * 128:(fc + 1) * 128, c * 128:(c + 1) * 128],
               DBuf("yn_s"), ynT[:, fc * 128:(fc + 1) * 128], [ynT_b])
    A.release()
    A.release()

    A.mark()
    wop = [A.alloc(f"p5c.w{i}", 512) for i in range(2)]
    ynl = [A.alloc(f"p5c.yn{i}", 1024) for i in range(2)]
    xr5, xr5_b = A.alloc("p5c.xr", 512)
    o5, o5_b = A.alloc("p5c.o", 512)

    def _ld_oc(oc2, ctr2):
        LD(wop[ctr2 % 2][0], wop[ctr2 % 2][1],
           outT_d[ctr2 * 128:(ctr2 + 1) * 128,
                  oc2 * 512:(oc2 + 1) * 512],
           q="act" if ctr2 % 2 else "sync")
        LD(ynl[ctr2 % 2][0], ynl[ctr2 % 2][1],
           yn_s[ctr2 * 128:(ctr2 + 1) * 128, 0:1024],
           sbufs=[DBuf("yn_s")], q="act" if ctr2 % 2 else "sync")
    _ld_oc(0, 0)
    for oc in range(4):
        for ctr in range(32):
            wt, wt_b = wop[ctr % 2]
            yt, yt_b = ynl[ctr % 2]
            for tt in range(8):
                MM(psum[tt][0][:, 0:512], psum[tt][1],
                   yt[:, tt * 128:(tt + 1) * 128], wt,
                   [yt_b, wt_b], start=(ctr == 0), stop=(ctr == 31))
            if ctr + 1 < 32:
                _ld_oc(oc, ctr + 1)
            elif oc + 1 < 4:
                _ld_oc(oc + 1, 0)
        for tt in range(8):
            LD(xr5, xr5_b, x2_s[tt * 128:(tt + 1) * 128,
                                oc * 512:(oc + 1) * 512],
               sbufs=[DBuf("x2_s")])
            TT("dve", o5, o5_b, psum[tt][0][:, 0:512], xr5,
               [psum[tt][1], xr5_b], OP.add)
            ST(x3_s[tt * 128:(tt + 1) * 128, oc * 512:(oc + 1) * 512],
               DBuf("x3_s"), o5, [o5_b])
    A.release()

    ffn_apply("f2", x3_s, y_out, "y_out", "x3_s")
    P.op("gp", lambda e: e.engine_nop(), reads=[DBuf("y_out")])

    emit = P.finalize(nc, sems, semq, ccsems)
    with nc.Block() as block:
        @block.tensor
        def _(e):
            emit("pe", e)

        @block.vector
        def _(e):
            emit("dve", e)

        @block.scalar
        def _(e):
            emit("act", e)

        @block.gpsimd
        def _(e):
            emit("gp", e)

        @block.sync
        def _(e):
            emit("sync", e)

    ctx.close()
    return nc


def make_in_maps(inputs):
    x = np.ascontiguousarray(inputs["L0_attn_a_hidden_states"],
                             dtype=np.float32)
    g = lambda k: np.ascontiguousarray(inputs[k], dtype=np.float32)
    shared = {
        "Wq": g("L0_attn_a_Wq"),
        "Wq_idx": g("L0_attn_a_Wq_idx"),
        "Wk_idx": g("L0_attn_a_Wk_idx"),
        "bq_idx": g("L0_attn_a_bq_idx").reshape(HI * DI, 1),
        "bk_idx": g("L0_attn_a_bk_idx").reshape(HI * DI, 1),
        "idx_w": g("L0_attn_a_idx_w").reshape(4, 1),
        "w_ctrl": np.ascontiguousarray(
            g("L0_attn_a_w_ctrl").reshape(16, 128).T),
        "b_ctrl": g("L0_attn_a_b_ctrl").reshape(1, 1),
        "W_dkv": g("L0_attn_a_W_dkv"),
        "kv_norm_w": g("L0_attn_a_kv_norm_w").reshape(1, KR),
        "W_ukv": g("L0_attn_a_W_ukv"),
        "Wo": g("L0_attn_a_Wo"),
        "in_projT": np.ascontiguousarray(g("L0_attn_b_in_proj_w").T),
        "conv_w": np.ascontiguousarray(g("L0_attn_b_conv_w")[:, 0, :]),
        "conv_b": g("L0_attn_b_conv_b").reshape(CONV, 1),
        "dt_bias": g("L0_attn_b_dt_bias").reshape(MNH, 1),
        "A_log": g("L0_attn_b_A_log").reshape(MNH, 1),
        "D_in": g("L0_attn_b_D").reshape(MNH, 1),
        "mnorm_w": g("L0_attn_b_norm_weight").reshape(1, 4096),
        "out_projT": np.ascontiguousarray(g("L0_attn_b_out_proj_w").T),
        "w_gT": np.ascontiguousarray(g("L0_ffn_w_gate").T),
        "w_uT": np.ascontiguousarray(g("L0_ffn_w_up").T),
        "w_dT": np.ascontiguousarray(g("L0_ffn_w_down").T),
    }
    in_maps = []
    for c in range(8):
        bi, ci = c // 4, c % 4
        m = dict(shared)
        m["x_full"] = np.ascontiguousarray(x[bi])
        m["x_own"] = np.ascontiguousarray(x[bi, ci * T:(ci + 1) * T])
        m["pos0"] = np.array([[ci * T]], np.float32)
        hs = np.zeros((4, 1), np.float32)
        if ci > 0:
            hs[ci - 1, 0] = 1.0
        m["halo_sel"] = hs
        ss = np.zeros((4, 4), np.float32)
        sm = np.zeros((4, 1), np.float32)
        for j in range(ci):
            sm[j, 0] = 1.0
            for i in range(j + 1, ci):
                ss[j, i] = 1.0
        m["state_sel"] = ss
        m["state_mask"] = sm
        in_maps.append(m)
    return in_maps


def kernel(**inputs):
    nc = build_nc()
    in_maps = make_in_maps(inputs)
    res = run_bass_kernel_spmd(nc, in_maps, core_ids=list(range(8)))
    out = np.empty((B, S, DM), np.float32)
    for c in range(8):
        bi, ci = c // 4, c % 4
        out[bi, ci * T:(ci + 1) * T] = res.results[c]["y_out"]
    return out


# version 23 (final)
# speedup vs baseline: 1.1419x; 1.1419x over previous
import contextlib
import math

import numpy as np

import concourse.bass as bass
import concourse.mybir as mybir
from concourse.bass_utils import run_bass_kernel_spmd

F32 = mybir.dt.float32
BF16 = mybir.dt.bfloat16
OP = mybir.AluOpType
AF = mybir.ActivationFunctionType
AX = mybir.AxisListType.X

B, S, DM = 2, 4096, 2048
T = 1024
NT = T // 128
NST = S // 128
NH, DN, DR, DV, KR = 16, 128, 64, 128, 512
HI, DI = 4, 64
INTER = 8192
MNH, MHD, MN = 64, 64, 128
CONV = 4096 + 2 * MN
PROJ = 4096 + CONV + MNH
NEG = -1.0e9
SCALE = 1.0 / math.sqrt(DN + DR)
N_ITER = 44

ENGINES = ("pe", "dve", "act", "gp", "sync")


class Buf:
    __slots__ = ("name", "last_w", "readers")

    def __init__(self, name):
        self.name = name
        self.last_w = None
        self.readers = []


class Op:
    __slots__ = ("idx", "eng", "fn", "waits", "inc", "drain", "kind",
                 "resolved_val", "qcount")

    def __init__(self, idx, eng, fn, kind):
        self.idx = idx
        self.eng = eng
        self.fn = fn
        self.waits = set()
        self.inc = kind != "compute"
        self.drain = False
        self.kind = kind
        self.resolved_val = None
        self.qcount = 0


class Prog:

    def __init__(self):
        self.ops = []
        self.eng_ops = {e: [] for e in ENGINES}
        self.last_drain = {e: -1 for e in ENGINES}
        self.dma_count = {e: 0 for e in ENGINES}
        self.n_cc = 0

    def op(self, eng, fn, reads=(), writes=(), dma=False, cc=False):
        kind = "cc" if cc else ("dma" if dma else "compute")
        o = Op(len(self.ops), eng, fn, kind)
        if kind == "dma":
            self.dma_count[eng] += 1
            o.qcount = self.dma_count[eng]
        if kind == "cc":
            self.n_cc += 1
            o.qcount = self.n_cc
        self.ops.append(o)
        self.eng_ops[eng].append(o)
        deps = set()
        for b in reads:
            if b.last_w is not None:
                deps.add(b.last_w)
        for b in writes:
            if b.last_w is not None:
                deps.add(b.last_w)
            deps.update(b.readers)
        need_drain = False
        for d in deps:
            if d == o.idx:
                continue
            p = self.ops[d]
            if p.eng == eng and p.kind == "compute" and kind == "compute":
                if d > self.last_drain[eng]:
                    need_drain = True
            elif p.eng == eng and p.kind == "compute" and kind != "compute":
                need_drain = True
                o.waits.add(d)
            else:
                o.waits.add(d)
                p.inc = True
        if need_drain and eng not in ("sync", "pe"):
            o.drain = True
            self.last_drain[eng] = o.idx - 1
        for b in reads:
            b.readers.append(o.idx)
        for b in writes:
            b.last_w = o.idx
            b.readers = []
        return o

    def finalize(self, nc, sems, semq, ccsems):
        cum = {e: 0 for e in ENGINES}
        for o in self.ops:
            if o.kind == "compute" and o.inc:
                cum[o.eng] += 1
                o.resolved_val = cum[o.eng]
        batch_val = {}
        self_pref = {}
        for e in ENGINES:
            run = 0
            cur = []
            for o in self.eng_ops[e]:
                if o.waits:
                    self_pref[o.idx] = 16 * run
                    for m in cur:
                        batch_val[m] = 16 * run
                    cur = []
                if o.kind == "dma":
                    run += 1
                    cur.append(o.idx)
            for m in cur:
                batch_val[m] = 16 * run

        def emit_for(eng_name, eng_obj):
            waited = {e: 0 for e in ENGINES}
            waitedq = {e: 0 for e in ENGINES}
            waitedcc = {}
            for o in self.eng_ops[eng_name]:
                if o.waits:
                    v = self_pref[o.idx]
                    if v > waitedq[eng_name]:
                        eng_obj.wait_ge(semq[eng_name], v)
                        waitedq[eng_name] = v
                for d in sorted(o.waits):
                    p = self.ops[d]
                    if p.kind == "compute":
                        if p.eng == eng_name and o.kind == "compute":
                            continue
                        if p.eng == eng_name:
                            continue
                        v = p.resolved_val
                        if v > waited[p.eng]:
                            eng_obj.wait_ge(sems[p.eng], v)
                            waited[p.eng] = v
                    elif p.kind == "dma":
                        v = batch_val[p.idx]
                        if v > waitedq[p.eng]:
                            eng_obj.wait_ge(semq[p.eng], v)
                            waitedq[p.eng] = v
                    else:
                        key = p.qcount
                        if key not in waitedcc:
                            eng_obj.wait_ge(ccsems[key - 1], 1)
                            waitedcc[key] = True
                if o.drain:
                    eng_obj.drain()
                ins = o.fn(eng_obj)
                if o.kind == "dma":
                    ins.then_inc(semq[o.eng], 16)
                elif o.kind == "cc":
                    ins.then_inc(ccsems[o.qcount - 1], 1)
                elif o.inc:
                    ins.then_inc(sems[o.eng], 1)
        return emit_for


class Arena:

    def __init__(self, handle, n_f32, name, prog=None):
        self.prog = prog
        self.h = handle
        self.cap = n_f32
        self.off = 0
        self.name = name
        self.live = []
        self.residue = []
        self.marks = []
        self.hw = 0

    def alloc(self, name, free_elems, dtype=F32):
        esz = mybir.dt.size(dtype)
        words = (free_elems * esz + 3) // 4
        assert self.off + words <= self.cap, (
            f"arena {self.name} overflow at {name}: "
            f"{self.off}+{words}>{self.cap}")
        start, end = self.off, self.off + words
        base = self.h[:, start:end]
        self.off = end
        self.hw = max(self.hw, self.off)
        if dtype != F32:
            base = base.bitcast(dtype)[:, 0:free_elems]
        b = Buf(f"{self.name}.{name}")
        deps = set()
        for (s0, e0, ops) in self.residue:
            if s0 < end and e0 > start:
                deps.update(ops)
        b.readers = sorted(deps)
        self.live.append((b, start, end))
        return base, b

    def mark(self):
        self.marks.append((self.off, len(self.live)))

    def release(self):
        off, nlive = self.marks.pop()
        newres = []
        for (b, s0, e0) in self.live[nlive:]:
            fence = set(b.readers)
            if b.last_w is not None:
                fence.add(b.last_w)
            per_eng = {}
            for d in fence:
                e = self.prog.ops[d].eng
                if d > per_eng.get(e, -1):
                    per_eng[e] = d
            newres.append((s0, e0, sorted(per_eng.values())))
        kept = []
        for (s0, e0, ops) in self.residue:
            if e0 <= off:
                kept.append((s0, e0, ops))
            else:
                covered = any(s1 <= s0 and e1 >= e0 for (s1, e1, _)
                              in newres)
                if not covered:
                    kept.append((s0, e0, ops))
        self.residue = kept + newres
        self.live = self.live[:nlive]
        self.off = off


def build_nc(debug=(), cc_groups=None):
    nc = bass.Bass("TRN2", target_bir_lowering=False)
    P = Prog()
    CC_GROUPS = cc_groups if cc_groups is not None else [[0, 1, 2, 3], [4, 5, 6, 7]]
    dbg_req = set(debug)

    def din(name, shape, dtype=F32):
        return nc.dram_tensor(name, list(shape), dtype,
                              kind="ExternalInput").ap()

    x_full = din("x_full", (S, DM))
    x_own_d = din("x_own", (T, DM))
    pos0_d = din("pos0", (1, 1))
    halo_sel_d = din("halo_sel", (4, 1))
    state_sel_d = din("state_sel", (4, 4))
    state_mask_d = din("state_mask", (4, 1))

    Wq_d = din("Wq", (DM, NH * (DN + DR)))
    Wqi_d = din("Wq_idx", (DM, HI * DI))
    Wki_d = din("Wk_idx", (DM, HI * DI))
    bqi_d = din("bq_idx", (HI * DI, 1))
    bki_d = din("bk_idx", (HI * DI, 1))
    idxw_d = din("idx_w", (4, 1))
    wctrl_d = din("w_ctrl", (128, 16))
    bctrl_d = din("b_ctrl", (1, 1))
    Wdkv_d = din("W_dkv", (DM, KR + DR))
    kvnw_d = din("kv_norm_w", (1, KR))
    Wukv_d = din("W_ukv", (KR, NH * (DN + DV)))
    Wo_d = din("Wo", (NH * DV, DM))
    inpT_d = din("in_projT", (DM, PROJ))
    convw_d = din("conv_w", (CONV, 4))
    convb_d = din("conv_b", (CONV, 1))
    dtb_d = din("dt_bias", (MNH, 1))
    Alog_d = din("A_log", (MNH, 1))
    Dd_d = din("D_in", (MNH, 1))
    mnw_d = din("mnorm_w", (1, 4096))
    outT_d = din("out_projT", (4096, DM))
    wgT_d = din("w_gT", (DM, INTER))
    wuT_d = din("w_uT", (DM, INTER))
    wdT_d = din("w_dT", (INTER, DM))

    y_out = nc.dram_tensor("y_out", [T, DM], F32, kind="ExternalOutput").ap()

    def dscr(name, shape, dtype=F32):
        return nc.dram_tensor(name, list(shape), dtype).ap()

    kn_s = dscr("kn_s", (NH * DN, S))
    v_s = dscr("v_s", (S, NH * DV))
    qn_s = dscr("qn_s", (NH * DN, T))
    qr_s = dscr("qr_s", (NH * DR, T))

    dram_bufs = {}

    def DBuf(name):
        if name not in dram_bufs:
            dram_bufs[name] = Buf("dram." + name)
        return dram_bufs[name]

    ctx = contextlib.ExitStack()

    def sb(name, parts, frees, dtype=F32):
        h = ctx.enter_context(nc.sbuf_tensor(name, [parts, frees], dtype))
        return h, Buf(name)

    ident, ident_b = sb("ident", 128, 128)
    identh, identh_b = sb("identh", 128, 128, BF16)
    qpos_c, qpos_c_b = sb("qpos_c", 128, 1)
    onecol, onecol_b = sb("onecol", 128, 1)
    pos0_t, pos0_b = sb("pos0_t", 128, 1)
    scl_t, scl_b = sb("scl", 128, 1)
    cexp_t, cexp_b = sb("cexp", 128, 1)
    bqi_t, bqi_b = sb("bqi", 128, 2)
    bki_t, bki_b = sb("bki", 128, 2)
    idxw_t, idxw_b = sb("idxw", 128, 4)
    ccq_t, ccq_b = sb("ccq", 128, 128)
    thr_t, thr_b = sb("thr", 128, NT)
    kq_t, kq_b = sb("kq", 128, 1)
    eps6_t, eps6_b = sb("eps6", 128, 1)
    eps5_t, eps5_b = sb("eps5", 128, 1)
    wctrl_t, wctrl_b = sb("wctrl", 128, 16)
    bctrl_t, bctrl_b = sb("bctrl", 1, 1)
    ffsc_t, ffsc_b = sb("ffsc", 128, 4)

    ar_h, _ = sb("arena", 128, 51000)
    A = Arena(ar_h, 51000, "A", prog=P)

    psum = []
    for i in range(8):
        h = ctx.enter_context(nc.psum_tensor(f"ps{i}", [128, 512], F32))
        psum.append((h, Buf(f"psum{i}")))

    sems = {e: ctx.enter_context(nc.semaphore(f"sem_{e}")) for e in ENGINES}
    semq = {e: ctx.enter_context(nc.semaphore(f"semq_{e}")) for e in ENGINES}
    ccsems = [ctx.enter_context(nc.semaphore(f"cc{i}")) for i in range(4)]

    def dbg(name, ap, rbuf, shape, dtype=F32):
        if name in dbg_req:
            t = nc.dram_tensor(f"dbg_{name}", list(shape), dtype,
                               kind="ExternalOutput").ap()
            P.op("gp", lambda e, a=ap, t=t: e.dma_start(out=t, in_=a),
                 reads=[rbuf], writes=[DBuf("dbg_" + name)], dma=True)

    def LD(dst_ap, dbuf, src_ap, sbufs=(), q="sync", slow=False):
        P.op(q, lambda e, d=dst_ap, s=src_ap, sl=slow: e.dma_start(
            out=d, in_=s, allow_slow_non_contiguous=sl),
            reads=list(sbufs), writes=[dbuf], dma=True)

    def ST(dst_ap, dbuf, src_ap, sbufs, q="gp"):
        P.op(q, lambda e, d=dst_ap, s=src_ap: e.dma_start(out=d, in_=s),
             reads=list(sbufs), writes=[dbuf], dma=True)

    def MM(out_ap, pb, lhsT, rhs, rbufs, start, stop):
        P.op("pe", lambda e, o=out_ap, l=lhsT, r=rhs, s1=start, s2=stop:
             e.matmul(o, l, r, start=s1, stop=s2),
             reads=list(rbufs), writes=[pb])

    def TRP(out_ap, pb, in_ap, in_buf, bf=False, start=True, stop=True):
        idt, idb = (identh, identh_b) if bf else (ident, ident_b)
        pk = in_ap.shape[0]
        P.op("pe", lambda e, o=out_ap, i=in_ap, d=idt[0:pk, 0:pk], s1=start,
             s2=stop: e.matmul(o, i, d, start=s1, stop=s2,
                               is_transpose=True),
             reads=[in_buf, idb], writes=[pb])

    def TS(eng, out, ob, in0, rb, s1, s2, op0, op1=OP.bypass, accum=None,
           ab=None):
        wr = [ob] + ([ab] if ab is not None else [])
        P.op(eng, lambda e, o=out, i=in0, a=s1, b2=s2, p0=op0, p1=op1,
             ac=accum: e.tensor_scalar(o, i, a, b2, p0, op1=p1,
                                       accum_out=ac),
             reads=list(rb), writes=wr)

    def TT(eng, out, ob, in0, in1, rb, op):
        P.op(eng, lambda e, o=out, a=in0, b2=in1, p=op:
             e.tensor_tensor(o, a, b2, p), reads=list(rb), writes=[ob])

    def STT(eng, out, ob, in0, scalar, in1, rb, op0, op1):
        P.op(eng, lambda e, o=out, a=in0, s=scalar, b2=in1, p0=op0, p1=op1:
             e.scalar_tensor_tensor(o, a, s, b2, p0, p1),
             reads=list(rb), writes=[ob])

    def ACTV(out, ob, in0, rb, func, bias=0.0, scale=1.0, accum=None,
             ab=None):
        wr = [ob] + ([ab] if ab is not None else [])
        P.op("act", lambda e, o=out, i=in0, b2=bias, s=scale, f=func,
             ac=accum: e.activation(o, i, f, bias=b2, scale=s,
                                    accum_out=ac),
             reads=list(rb), writes=wr)

    def CPY(eng, out, ob, in0, rb):
        if eng == "act":
            P.op("act", lambda e, o=out, i=in0: e.activation(
                o, i, AF.Copy, bias=0.0, scale=1.0),
                reads=list(rb), writes=[ob])
        else:
            P.op(eng, lambda e, o=out, i=in0: e.tensor_copy(o, i),
                 reads=list(rb), writes=[ob])

    def RED(eng, out, ob, in0, rb, op, absval=False):
        P.op(eng, lambda e, o=out, i=in0, p=op, v=absval: e.tensor_reduce(
            o, i, AX, p, apply_absolute_value=v), reads=list(rb),
            writes=[ob])

    def MEMSET(eng, ap, b, val):
        P.op(eng, lambda e, a=ap, v=val: e.memset(a, v), writes=[b])

    def PB(out, ob, in0, rb):
        p_out = out.shape[0]
        p_in = in0.shape[0]
        same = (in0.tensor is out.tensor and in0.offset == out.offset)
        if not same:
            P.op("sync", lambda e, o=out[0:p_in, :], i=in0:
                 e.dma_start(out=o, in_=i),
                 reads=list(rb), writes=[ob], dma=True)
        cur = p_in
        while cur < p_out:
            n = min(cur, p_out - cur)
            P.op("sync", lambda e, o=out[cur:cur + n, :], i=out[0:n, :]:
                 e.dma_start(out=o, in_=i),
                 reads=[ob], writes=[ob], dma=True)
            cur += n

    def CPRED(eng, out, ob, mask, data, rb):
        P.op(eng, lambda e, o=out, m=mask, d=data: e.copy_predicated(o, m, d),
             reads=list(rb) + [ob], writes=[ob])

    def IOTA(out_ap, ob, pattern, ch_mult, scratch_ap, scratch_b):
        P.op("gp", lambda e, o=scratch_ap, p=pattern, c=ch_mult:
             e.iota(o, pattern=p, base=0, channel_multiplier=c),
             writes=[scratch_b])
        CPY("dve", out_ap, ob, scratch_ap, [scratch_b])

    def FLOORF(out, ob, tin, tb, iscr, iscr_b, fscr, fscr_b, mskv, eng="dve"):
        CPY(eng, iscr, iscr_b, tin, [tb])
        CPY(eng, fscr, fscr_b, iscr, [iscr_b])
        TT(eng, mskv, iscr_b, tin, fscr, [tb, fscr_b], OP.is_lt)
        TT(eng, out, ob, fscr, mskv, [fscr_b, iscr_b], OP.subtract)

    def RECIP(out, ob, in0, rb):
        P.op("dve", lambda e, o=out, i=in0: e.reciprocal(o, i),
             reads=list(rb), writes=[ob])

    def rsqrt_col(name, ssq_ap, ssq_b, scale, eps_ap, eps_b):
        ve, ve_b = A.alloc(name + ".ve", 1)
        sq, sq_b = A.alloc(name + ".sq", 1)
        r, r_b = A.alloc(name + ".r", 1)
        t, t_b = A.alloc(name + ".t", 1)
        TS("dve", ve, ve_b, ssq_ap, [ssq_b, eps_b], scale, eps_ap, OP.mult,
           OP.add)
        ACTV(sq, sq_b, ve, [ve_b], AF.Sqrt)
        RECIP(r, r_b, sq, [sq_b])
        for _ in range(2):
            TT("dve", t, t_b, r, r, [r_b], OP.mult)
            TS("dve", t, t_b, t, [t_b, ve_b], ve, None, OP.mult)
            TS("dve", t, t_b, t, [t_b], -0.5, 1.5, OP.mult, OP.add)
            TT("dve", r, r_b, r, t, [r_b, t_b], OP.mult)
        return r, r_b

    def layernorm_tile(name, out, ob, xt, xb, width=DM, scratch=None):
        A.mark()
        m, m_b = A.alloc(name + ".m", 1)
        ssq, ssq_b = A.alloc(name + ".ssq", 1)
        if scratch is None:
            sc, sc_b = A.alloc(name + ".sc", width)
        else:
            sc, sc_b = scratch
        RED("dve", m, m_b, xt, [xb], OP.add)
        TS("dve", m, m_b, m, [m_b], 1.0 / width, None, OP.mult)
        TS("dve", out, ob, xt, [xb, m_b], m, None, OP.subtract)
        ACTV(sc, sc_b, out, [ob], AF.Square, accum=ssq, ab=ssq_b)
        r, r_b = rsqrt_col(name + ".rs", ssq, ssq_b, 1.0 / width,
                           eps5_t[:, 0:1], eps5_b)
        TS("dve", out, ob, out, [ob, r_b], r, None, OP.mult)
        A.release()

    def transpose_tile(out, ob, in_ap, ib, n, pbanks=(6, 7), bf=False,
                       colmap=None):
        if colmap is None:
            colmap = lambda c: c * 128
        for c4 in range(0, n, 4):
            m = min(4, n - c4)
            pi = pbanks[(c4 // 4) % len(pbanks)]
            h, pb = psum[pi]
            hv = h[:, :].bitcast(BF16) if bf else h[:, :]
            for j in range(m):
                c = c4 + j
                TRP(hv[:, j * 128:(j + 1) * 128], pb,
                    in_ap[:, c * 128:(c + 1) * 128], ib, bf=bf,
                    start=(j == 0), stop=(j == m - 1))
            contig = all(colmap(c4 + j + 1) == colmap(c4 + j) + 128
                         for j in range(m - 1))
            if contig:
                CPY("act", out[:, colmap(c4):colmap(c4) + m * 128], ob,
                    hv[:, 0:m * 128], [pb])
            else:
                for j in range(m):
                    CPY("act", out[:, colmap(c4 + j):colmap(c4 + j) + 128],
                        ob, hv[:, j * 128:(j + 1) * 128], [pb])

    MEMSET("gp", ident[:, :], ident_b, 0.0)
    P.op("gp", lambda e: e.affine_select(
        out=ident[:, :], in_=ident[:, :], compare_op=OP.not_equal, fill=1.0,
        base=0, pattern=[[-1, 128]], channel_multiplier=1),
        reads=[ident_b], writes=[ident_b])
    CPY("dve", identh[:, :], identh_b, ident[:, :], [ident_b])
    qpi, qpi_b = A.alloc("qpos_i", 1, mybir.dt.int32)
    IOTA(qpos_c[:, :], qpos_c_b, [[0, 1]], 1, qpi[:, :], qpi_b)
    MEMSET("gp", onecol[:, :], onecol_b, 1.0)
    MEMSET("dve", eps6_t[:, :], eps6_b, 1e-6)
    MEMSET("dve", eps5_t[:, :], eps5_b, 1e-5)
    MEMSET("dve", scl_t[:, :], scl_b, SCALE)
    MEMSET("dve", cexp_t[:, :], cexp_b, -2.0 * math.log(10000.0) / 64.0)
    MEMSET("dve", ccq_t[:, :], ccq_b, 0.0)
    LD(pos0_t[0:1, 0:1], pos0_b, pos0_d)
    PB(pos0_t[:, :], pos0_b, pos0_t[0:1, 0:1], [pos0_b])
    LD(wctrl_t[:, :], wctrl_b, wctrl_d)
    LD(bctrl_t[:, :], bctrl_b, bctrl_d)
    LD(bqi_t[:, 0:1], bqi_b, bqi_d[0:128, 0:1])
    LD(bqi_t[:, 1:2], bqi_b, bqi_d[128:256, 0:1])
    LD(bki_t[:, 0:1], bki_b, bki_d[0:128, 0:1])
    LD(bki_t[:, 1:2], bki_b, bki_d[128:256, 0:1])
    LD(idxw_t[0:4, 0:1], idxw_b, idxw_d)
    for j in range(1, 4):
        P.op("sync", lambda e, o=idxw_t[0:1, j:j + 1],
             i=idxw_t[j:j + 1, 0:1]: e.dma_start(out=o, in_=i),
             reads=[idxw_b], writes=[idxw_b], dma=True)
    for j in range(4):
        PB(idxw_t[:, j:j + 1], idxw_b, idxw_t[0:1, j:j + 1], [idxw_b])

    A.mark()
    kr_t, kr_b = A.alloc("kr", S)
    ki0, ki0_b = A.alloc("ki0", S)
    ki1, ki1_b = A.alloc("ki1", S)
    qi0, qi0_b = A.alloc("qi0", T)
    qi1, qi1_b = A.alloc("qi1", T)
    kvnw_t, kvnw_b = A.alloc("kvnw", KR)
    LD(kvnw_t[0:1, :], kvnw_b, kvnw_d)
    PB(kvnw_t[:, :], kvnw_b, kvnw_t[0:1, :], [kvnw_b])

    A.mark()
    cosk, cosk_b = A.alloc("cosk", S)
    sink, sink_b = A.alloc("sink", S)
    cosq, cosq_b = A.alloc("cosq", T)
    sinq, sinq_b = A.alloc("sinq", T)
    A.mark()
    ivf, ivf_b = A.alloc("ivf", 1)
    iota_r, iota_r_b = A.alloc("iota_r", S)
    ioi, ioi_b = A.alloc("iota_i", S, mybir.dt.int32)
    IOTA(iota_r[0:1, :], iota_r_b, [[1, S]], 0, ioi[0:1, :], ioi_b)
    IOTA(ivf[0:64, :], ivf_b, [[0, 1]], 1, ioi[0:64, 0:1], ioi_b)
    ACTV(ivf[0:64, :], ivf_b, ivf[0:64, :], [ivf_b], AF.Exp,
         scale=cexp_t[0:64, 0:1])
    P.op("sync", lambda e: e.dma_start(out=ivf[32:64, 0:1],
                                       in_=ivf[0:32, 0:1]),
         reads=[ivf_b], writes=[ivf_b], dma=True)

    def make_tables(cos_ap, cos_b, sin_ap, sin_b, n, own):
        twopi = 2.0 * math.pi
        CH = 2048
        A.mark()
        ang, ang_b = A.alloc("tb.ang", CH)
        raw, raw_b = A.alloc("tb.raw", CH)
        qq, qq_b = A.alloc("tb.q", CH)
        qi_, qi_b2 = A.alloc("tb.qi", CH, mybir.dt.int32)
        qf_, qf_b2 = A.alloc("tb.qf", CH)
        qif = qi_.bitcast(F32)
        for ofs in range(0, n, CH):
            nn = min(CH, n - ofs)
            if own:
                TS("dve", raw[0:1, 0:nn], raw_b, iota_r[0:1, ofs:ofs + nn],
                   [iota_r_b, pos0_b], pos0_t[0:1, 0:1], None, OP.add)
                PB(ang[0:64, 0:nn], ang_b, raw[0:1, 0:nn], [raw_b])
            else:
                PB(ang[0:64, 0:nn], ang_b, iota_r[0:1, ofs:ofs + nn],
                   [iota_r_b])
            TS("dve", ang[0:64, 0:nn], ang_b, ang[0:64, 0:nn],
               [ang_b, ivf_b], ivf[0:64, 0:1], None, OP.mult)
            for phase, (dst, dstb) in ((math.pi, (sin_ap, sin_b)),
                                       (math.pi * 1.5, (cos_ap, cos_b))):
                TS("dve", raw[0:64, 0:nn], raw_b, ang[0:64, 0:nn], [ang_b],
                   phase, None, OP.add)
                TS("dve", qq[0:64, 0:nn], qq_b, raw[0:64, 0:nn], [raw_b],
                   1.0 / twopi, None, OP.mult)
                FLOORF(qq[0:64, 0:nn], qq_b, qq[0:64, 0:nn], qq_b,
                       qi_[0:64, 0:nn], qi_b2, qf_[0:64, 0:nn], qf_b2,
                       qif[0:64, 0:nn])
                STT("dve", raw[0:64, 0:nn], raw_b, qq[0:64, 0:nn], -twopi,
                    raw[0:64, 0:nn], [qq_b, raw_b], OP.mult, OP.add)
                TS("dve", raw[0:64, 0:nn], raw_b, raw[0:64, 0:nn], [raw_b],
                   -math.pi, None, OP.add)
                ACTV(dst[0:64, ofs:ofs + nn], dstb, raw[0:64, 0:nn],
                     [raw_b], AF.Sin)
        TS("dve", sin_ap[0:32, 0:n], sin_b, sin_ap[0:32, 0:n], [sin_b],
           -1.0, None, OP.mult)
        A.release()

    make_tables(cosk, cosk_b, sink, sink_b, S, own=False)
    make_tables(cosq, cosq_b, sinq, sinq_b, T, own=True)
    A.release()

    def rope_fm(out, ob, x_ap, xb, cos_ap, cos_b, sin_ap, sin_b, c0, n):
        A.mark()
        a, a_b = A.alloc("rope.a", n)
        rx, rx_b = A.alloc("rope.rx", n)
        P.op("sync", lambda e, o=rx[0:32, 0:n], i=x_ap[32:64, :]:
             e.dma_start(out=o, in_=i), reads=[xb], writes=[rx_b], dma=True)
        P.op("sync", lambda e, o=rx[32:64, 0:n], i=x_ap[0:32, :]:
             e.dma_start(out=o, in_=i), reads=[xb], writes=[rx_b], dma=True)
        TT("dve", a[0:64, :], a_b, x_ap, cos_ap[0:64, c0:c0 + n],
           [xb, cos_b], OP.mult)
        TT("dve", rx[0:64, :], rx_b, rx[0:64, 0:n],
           sin_ap[0:64, c0:c0 + n], [rx_b, sin_b], OP.mult)
        TT("dve", out, ob, a[0:64, :], rx[0:64, :], [a_b, rx_b], OP.add)
        A.release()

    cnT_s = dscr("cnT_s", (4 * 128, S))
    A.mark()
    Wdkv_t, Wdkv_b = A.alloc("Wdkv", 16 * 576)
    LD(Wdkv_t, Wdkv_b, Wdkv_d.rearrange("(c p) f -> p c f", p=128))
    Wki_t, Wki_b = A.alloc("Wki", 16 * 256)
    LD(Wki_t, Wki_b, Wki_d.rearrange("(c p) f -> p c f", p=128))

    pool_ps, pool_ps_b = psum[5]
    xtl = [A.alloc(f"p1.x{i}", DM) for i in range(2)]
    lnb, lnb_b = A.alloc("p1.ln", DM)
    lnT, lnT_b = A.alloc("p1.lnT", 16 * 128)
    ckv, ckv_b = A.alloc("p1.ckv", 576)
    cn, cn_b = A.alloc("p1.cn", KR)
    cnT, cnT_b = A.alloc("p1.cnT", 4 * 128)
    sq_s, sq_s_b = A.alloc("p1.sqs", KR)

    LD(xtl[0][0], xtl[0][1], x_full[0:128, :])
    for st in range(NST):
        xt, xb = xtl[st % 2]
        if st + 1 < NST:
            LD(xtl[(st + 1) % 2][0], xtl[(st + 1) % 2][1],
               x_full[(st + 1) * 128:(st + 2) * 128, :])
        layernorm_tile(f"p1.ln{st}", lnb, lnb_b, xt, xb,
                        scratch=(lnT[:, 0:DM], lnT_b))
        for c in range(16):
            MM(pool_ps[:, c:c + 1], pool_ps_b,
               lnb[:, c * 128:(c + 1) * 128], onecol[:, :],
               [lnb_b, onecol_b], start=(st == 0 and c == 0),
               stop=(st == NST - 1 and c == 15))
        transpose_tile(lnT, lnT_b, lnb, lnb_b, 16)
        for c in range(16):
            MM(psum[0][0][:, 0:512], psum[0][1],
               lnT[:, c * 128:(c + 1) * 128],
               Wdkv_t[:, c * 576:c * 576 + 512], [lnT_b, Wdkv_b],
               start=(c == 0), stop=(c == 15))
        for c in range(16):
            MM(psum[1][0][:, 0:64], psum[1][1],
               lnT[:, c * 128:(c + 1) * 128],
               Wdkv_t[:, c * 576 + 512:(c + 1) * 576], [lnT_b, Wdkv_b],
               start=(c == 0), stop=(c == 15))
        CPY("act", ckv[:, 0:512], ckv_b, psum[0][0][:, 0:512], [psum[0][1]])
        CPY("act", ckv[:, 512:576], ckv_b, psum[1][0][:, 0:64], [psum[1][1]])
        TRP(psum[2][0][0:64, 0:128], psum[2][1], ckv[:, 512:576], ckv_b)
        CPY("dve", kr_t[0:64, st * 128:(st + 1) * 128], kr_b,
            psum[2][0][0:64, 0:128], [psum[2][1]])
        ssq, ssq_b = A.alloc(f"p1.ssq{st}", 1)
        ACTV(sq_s, sq_s_b, ckv[:, 0:512], [ckv_b], AF.Square, accum=ssq,
             ab=ssq_b)
        r, r_b = rsqrt_col(f"p1.r{st}", ssq, ssq_b, 1.0 / KR,
                           eps6_t[:, 0:1], eps6_b)
        TS("dve", cn, cn_b, ckv[:, 0:512], [ckv_b, r_b], r, None, OP.mult)
        TT("dve", cn, cn_b, cn, kvnw_t[:, 0:KR], [cn_b, kvnw_b], OP.mult)
        transpose_tile(cnT, cnT_b, cn, cn_b, 4, pbanks=(3,))
        for rc in range(4):
            ST(cnT_s[rc * 128:(rc + 1) * 128, st * 128:(st + 1) * 128],
               DBuf("cnT_s"), cnT[:, rc * 128:(rc + 1) * 128], [cnT_b])
        for fc, (kit, kib) in enumerate(((ki0, ki0_b), (ki1, ki1_b))):
            for c in range(16):
                MM(psum[0][0][:, 0:128], psum[0][1],
                   Wki_t[:, c * 256 + fc * 128:c * 256 + (fc + 1) * 128],
                   lnT[:, c * 128:(c + 1) * 128], [Wki_b, lnT_b],
                   start=(c == 0), stop=(c == 15))
            P.op("act", lambda e, o=kit[:, st * 128:(st + 1) * 128],
                 i=psum[0][0][:, 0:128], b2=bki_t[:, fc:fc + 1]:
                 e.activation(o, i, AF.Identity, bias=b2, scale=1.0),
                 reads=[psum[0][1], bki_b], writes=[kib])

    pool_sb, pool_sb_b = A.alloc("pool", 16)
    CPY("dve", pool_sb, pool_sb_b, pool_ps[:, 0:16], [pool_ps_b])
    TS("dve", pool_sb, pool_sb_b, pool_sb, [pool_sb_b], 1.0 / S, None,
       OP.mult)
    pw, pw_b = A.alloc("pw", 16)
    TT("dve", pw, pw_b, pool_sb, wctrl_t[:, :], [pool_sb_b, wctrl_b],
       OP.mult)
    pr, pr_b = A.alloc("pr", 1)
    RED("dve", pr, pr_b, pw, [pw_b], OP.add)
    MM(psum[0][0][0:1, 0:1], psum[0][1], pr, onecol[:, :],
       [pr_b, onecol_b], start=True, stop=True)
    frac, frac_b = A.alloc("frac", 1)
    P.op("act", lambda e, o=frac[0:1, 0:1], i=psum[0][0][0:1, 0:1],
         b2=bctrl_t[0:1, 0:1]: e.activation(o, i, AF.Sigmoid, bias=b2,
                                            scale=1.0),
         reads=[psum[0][1], bctrl_b], writes=[frac_b])
    TS("dve", frac[0:1, 0:1], frac_b, frac[0:1, 0:1], [frac_b], float(S),
       0.5, OP.mult, OP.add)
    km, km_b = A.alloc("km", 1, mybir.dt.int32)
    kf, kf_b = A.alloc("kf", 1)
    FLOORF(frac[0:1, 0:1], frac_b, frac[0:1, 0:1], frac_b, km[0:1, 0:1],
           km_b, kf[0:1, 0:1], kf_b, km.bitcast(F32)[0:1, 0:1])
    TS("dve", frac[0:1, 0:1], frac_b, frac[0:1, 0:1], [frac_b], 1.0,
       float(S), OP.max, OP.min)
    PB(kq_t[:, :], kq_b, frac[0:1, 0:1], [frac_b])
    dbg("kq", kq_t[:, 0:1], kq_b, (128, 1))
    A.release()

    A.mark()
    kro, kro_b = A.alloc("p1.kro", S // 2)
    for hf in range(2):
        rope_fm(kro[0:64, :], kro_b, kr_t[0:64, hf * (S // 2):(hf + 1) * (S // 2)],
                kr_b, cosk, cosk_b, sink, sink_b, hf * (S // 2), S // 2)
        CPY("dve", kr_t[0:64, hf * (S // 2):(hf + 1) * (S // 2)], kr_b,
            kro[0:64, :], [kro_b])
    A.release()

    dbg("kr", kr_t[0:64, :], kr_b, (64, S))
    dbg("ki0", ki0[:, :], ki0_b, (128, S))
    dbg("ki1", ki1[:, :], ki1_b, (128, S))

    A.mark()
    xo, xo_b = A.alloc("p1b.x", DM)
    lno, lno_b = A.alloc("p1b.ln", DM)
    lnoT, lnoT_b = A.alloc("p1b.lnT", 16 * 512)
    sqsc, sqsc_b = A.alloc("p1b.sq", DM)
    wqh = [A.alloc(f"p1b.wq{i}", 16 * 192) for i in range(2)]
    qev, qev_b = A.alloc("p1b.qev", 512)
    qrp, qrp_b = A.alloc("p1b.qrp", 512)

    for half in range(2):
        for tt in range(4):
            ti = half * 4 + tt
            LD(xo, xo_b, x_own_d[ti * 128:(ti + 1) * 128, :])
            layernorm_tile(f"p1b.ln{ti}", lno, lno_b, xo, xo_b,
                               scratch=(sqsc, sqsc_b))
            transpose_tile(lnoT, lnoT_b, lno, lno_b, 16,
                           colmap=lambda c, t=tt: c * 512 + t * 128)
        def _ld_h(h2):
            wq2, wq2_b = wqh[h2 % 2]
            LD(wq2[:, 0:16 * 192], wq2_b,
               Wq_d[:, h2 * 192:(h2 + 1) * 192].rearrange(
                   "(c p) f -> p c f", p=128),
               q="act" if h2 % 2 else "sync")
        _ld_h(0)
        for h in range(NH):
            wq_t, wq_b2 = wqh[h % 2]
            if h + 1 < NH:
                _ld_h(h + 1)
            for c in range(16):
                MM(psum[0][0][:, 0:512], psum[0][1],
                   wq_t[:, c * 192:c * 192 + 128],
                   lnoT[:, c * 512:(c + 1) * 512], [wq_b2, lnoT_b],
                   start=(c == 0), stop=(c == 15))
            for c in range(16):
                MM(psum[1][0][0:64, 0:512], psum[1][1],
                   wq_t[:, c * 192 + 128:(c + 1) * 192],
                   lnoT[:, c * 512:(c + 1) * 512], [wq_b2, lnoT_b],
                   start=(c == 0), stop=(c == 15))
            P.op("act", lambda e, o=qev[:, 0:512], i=psum[0][0][:, 0:512],
                 s=scl_t[:, 0:1]: e.activation(o, i, AF.Identity, bias=0.0,
                                               scale=s),
                 reads=[psum[0][1], scl_b], writes=[qev_b])
            ST(qn_s[h * 128:(h + 1) * 128, half * 512:(half + 1) * 512],
               DBuf("qn_s"), qev[:, 0:512], [qev_b])
            P.op("act", lambda e, o=qrp[0:64, 0:512],
                 i=psum[1][0][0:64, 0:512], s=scl_t[0:64, 0:1]:
                 e.activation(o, i, AF.Identity, bias=0.0, scale=s),
                 reads=[psum[1][1], scl_b], writes=[qrp_b])
            rope_fm(qrp[0:64, 0:512], qrp_b, qrp[0:64, 0:512], qrp_b,
                    cosq, cosq_b, sinq, sinq_b, half * 512, 512)
            ST(qr_s[h * 64:(h + 1) * 64, half * 512:(half + 1) * 512],
               DBuf("qr_s"), qrp[0:64, 0:512], [qrp_b])
        for fc, (qit, qib) in enumerate(((qi0, qi0_b), (qi1, qi1_b))):
            wq_t, wq_b2 = wqh[0]
            LD(wq_t[:, 0:16 * 128], wq_b2,
               Wqi_d[:, fc * 128:(fc + 1) * 128].rearrange(
                   "(c p) f -> p c f", p=128))
            for c in range(16):
                MM(psum[0][0][:, 0:512], psum[0][1],
                   wq_t[:, c * 128:(c + 1) * 128],
                   lnoT[:, c * 512:(c + 1) * 512], [wq_b2, lnoT_b],
                   start=(c == 0), stop=(c == 15))
            P.op("act", lambda e, o=qit[:, half * 512:(half + 1) * 512],
                 i=psum[0][0][:, 0:512], b2=bqi_t[:, fc:fc + 1]:
                 e.activation(o, i, AF.Identity, bias=b2, scale=1.0),
                 reads=[psum[0][1], bqi_b], writes=[qib])
    A.release()
    A.release()

    dbg("qi0", qi0[:, :], qi0_b, (128, T))
    dbg("qi1", qi1[:, :], qi1_b, (128, T))
    A.mark()
    Wukv_t, Wukv_b = A.alloc("Wukv", 4 * 4096)
    LD(Wukv_t, Wukv_b, Wukv_d.rearrange("(c p) f -> p c f", p=128))
    cnTg, cnTg_b = A.alloc("p1c.cnT", 4 * 512)
    kvev, kvev_b = A.alloc("p1c.kvev", 512)
    for stg in range(8):
        for rc in range(4):
            LD(cnTg[:, rc * 512:(rc + 1) * 512], cnTg_b,
               cnT_s[rc * 128:(rc + 1) * 128, stg * 512:(stg + 1) * 512],
               sbufs=[DBuf("cnT_s")])
        for h in range(NH):
            pk, pk_b = psum[h % 2]
            for rc in range(4):
                MM(pk[:, 0:512], pk_b,
                   Wukv_t[:, rc * 4096 + h * 256:rc * 4096 + h * 256 + 128],
                   cnTg[:, rc * 512:(rc + 1) * 512], [Wukv_b, cnTg_b],
                   start=(rc == 0), stop=(rc == 3))
            CPY("act", kvev, kvev_b, pk[:, 0:512], [pk_b])
            ST(kn_s[h * 128:(h + 1) * 128, stg * 512:(stg + 1) * 512],
               DBuf("kn_s"), kvev[:, 0:512], [kvev_b])
        for ts4 in range(4):
            for hg in range(4):
                pv, pv_b = psum[2 + (hg % 2)]
                for hh in range(4):
                    h = hg * 4 + hh
                    for rc in range(4):
                        MM(pv[:, hh * 128:(hh + 1) * 128], pv_b,
                           cnTg[:, rc * 512 + ts4 * 128:
                                rc * 512 + (ts4 + 1) * 128],
                           Wukv_t[:, rc * 4096 + h * 256 + 128:
                                  rc * 4096 + h * 256 + 256],
                           [Wukv_b, cnTg_b],
                           start=(hh == 0 and rc == 0),
                           stop=(hh == 3 and rc == 3))
                CPY("act", kvev, kvev_b, pv[:, :], [pv_b])
                ST(v_s[stg * 512 + ts4 * 128:stg * 512 + (ts4 + 1) * 128,
                       hg * 512:(hg + 1) * 512],
                   DBuf("v_s"), kvev[:, 0:512], [kvev_b])
    A.release()

    keep_t, keep_b = A.alloc("keep", NT * S, mybir.dt.uint8)
    A.mark()
    CHW = 2048
    wst = [A.alloc(f"ffw.w{i}", CHW) for i in range(2)]
    sacc, sacc_b = A.alloc("ffw.sacc", 1)
    red1, red1_b = A.alloc("ffw.red", 1)
    a1t, a1t_b = A.alloc("ffw.a1", CHW)
    a2t, a2t_b = A.alloc("ffw.a2", CHW)
    wqo = [A.alloc(f"ffw.wq{i}", CHW, BF16) for i in range(2)]
    A.mark()
    kpos_bt, kpos_bb = A.alloc("p2.kpos", S)
    idx2 = [A.alloc(f"p2.idx{w}", S) for w in range(2)]
    kpi = idx2[0][0].bitcast(mybir.dt.int32)[:, 0:S]
    kpi_b = idx2[0][1]
    IOTA(kpos_bt[:, :], kpos_bb, [[1, S]], 0, kpi[:, :], kpi_b)
    relu2 = [A.alloc(f"p2.relu{w}", 512) for w in range(2)]
    junk2 = [A.alloc(f"p2.junk{w}", S, mybir.dt.uint8) for w in range(2)]
    caus, caus_b = A.alloc("p2.caus", S, mybir.dt.uint8)
    lo2 = [A.alloc(f"p2.lo{w}", 1) for w in range(2)]
    hi2 = [A.alloc(f"p2.hi{w}", 1) for w in range(2)]
    mid2 = [A.alloc(f"p2.mid{w}", 1) for w in range(2)]
    cnt2 = [A.alloc(f"p2.cnt{w}", 1) for w in range(2)]
    take2 = []
    tkn2 = []
    for w in range(2):
        t_, tb_ = A.alloc(f"p2.take{w}", 4, mybir.dt.uint8)
        take2.append((t_[:, 0:1], tb_))
        t_, tb_ = A.alloc(f"p2.tkn{w}", 4, mybir.dt.uint8)
        tkn2.append((t_[:, 0:1], tb_))
    qpos_t, qpos_t_b = A.alloc("p2.qpos", 1)
    rfl, rfl_b = A.alloc("p2.rfl", 1)

    qis = ((qi0, qi0_b), (qi1, qi1_b))
    kis = ((ki0, ki0_b), (ki1, ki1_b))
    for qp in range(4):
        for w in range(2):
            qt = qp * 2 + w
            idxt, idxt_b = idx2[w]
            for kb in range(8):
                for hiq in range(4):
                    qsrc, qsb = qis[hiq // 2]
                    ksrc, ksb = kis[hiq // 2]
                    roff = (hiq % 2) * 64
                    pb_, pbb_ = psum[w]
                    MM(pb_[:, 0:512], pbb_,
                       qsrc[roff:roff + 64, qt * 128:(qt + 1) * 128],
                       ksrc[roff:roff + 64, kb * 512:(kb + 1) * 512],
                       [qsb, ksb], start=True, stop=True)
                    ACTV(relu2[w][0], relu2[w][1], pb_[:, 0:512], [pbb_],
                         AF.Relu)
                    if hiq == 0:
                        TS("dve", idxt[:, kb * 512:(kb + 1) * 512], idxt_b,
                           relu2[w][0], [relu2[w][1], idxw_b],
                           idxw_t[:, 0:1], None, OP.mult)
                    else:
                        STT("dve", idxt[:, kb * 512:(kb + 1) * 512], idxt_b,
                            relu2[w][0], idxw_t[:, hiq:hiq + 1],
                            idxt[:, kb * 512:(kb + 1) * 512],
                            [relu2[w][1], idxw_b, idxt_b], OP.mult, OP.add)
            RED("dve", lo2[w][0], lo2[w][1], idxt, [idxt_b], OP.min)
            RED("dve", hi2[w][0], hi2[w][1], idxt, [idxt_b], OP.max)
            TS("dve", mid2[w][0], mid2[w][1], hi2[w][0], [hi2[w][1]], -1.0,
               None, OP.mult)
            TT("dve", mid2[w][0], mid2[w][1], mid2[w][0], hi2[w][0],
               [mid2[w][1], hi2[w][1]], OP.max)
            TS("dve", mid2[w][0], mid2[w][1], mid2[w][0], [mid2[w][1]],
               1e-6, 1e-20, OP.mult, OP.add)
            TT("dve", hi2[w][0], hi2[w][1], hi2[w][0], mid2[w][0],
               [hi2[w][1], mid2[w][1]], OP.add)
        for it in range(N_ITER):
            for w in range(2):
                lo, lo_b = lo2[w]
                hi, hi_b = hi2[w]
                mid, mid_b = mid2[w]
                TT("dve", mid, mid_b, lo, hi, [lo_b, hi_b], OP.add)
                TS("dve", mid, mid_b, mid, [mid_b], 0.5, None, OP.mult)
            for w in range(2):
                idxt, idxt_b = idx2[w]
                TS("dve", junk2[w][0], junk2[w][1], idxt,
                   [idxt_b, mid2[w][1]], mid2[w][0], None, OP.is_ge,
                   OP.add, accum=cnt2[w][0], ab=cnt2[w][1])
            for w in range(2):
                TT("dve", take2[w][0], take2[w][1], cnt2[w][0],
                   kq_t[:, 0:1], [cnt2[w][1], kq_b], OP.is_ge)
                TS("dve", tkn2[w][0], tkn2[w][1], take2[w][0],
                   [take2[w][1]], -1.0, 1.0, OP.mult, OP.add)
            for w in range(2):
                CPRED("dve", lo2[w][0], lo2[w][1], take2[w][0], mid2[w][0],
                      [take2[w][1], mid2[w][1]])
                CPRED("dve", hi2[w][0], hi2[w][1], tkn2[w][0], mid2[w][0],
                      [tkn2[w][1], mid2[w][1]])
        for w in range(2):
            qt = qp * 2 + w
            idxt, idxt_b = idx2[w]
            lo, lo_b = lo2[w]
            CPY("dve", thr_t[:, qt:qt + 1], thr_b, lo, [lo_b])
            TS("dve", qpos_t, qpos_t_b, qpos_c[:, 0:1], [qpos_c_b, pos0_b],
               pos0_t[:, 0:1], float(qt * 128), OP.add, OP.add)
            TS("dve", caus[:, 0:S], caus_b, kpos_bt[:, :],
               [kpos_bb, qpos_t_b], qpos_t, None, OP.is_le)
            TS("dve", keep_t[:, qt * S:(qt + 1) * S], keep_b, idxt,
               [idxt_b, lo_b], lo, None, OP.is_ge)
            TT("dve", keep_t[:, qt * S:(qt + 1) * S], keep_b,
               keep_t[:, qt * S:(qt + 1) * S], caus[:, 0:S],
               [keep_b, caus_b], OP.mult)
            RED("dve", rfl, rfl_b, keep_t[:, qt * S:(qt + 1) * S], [keep_b],
                OP.max)
            TS("dve", ccq_t[:, qt:qt + 1], ccq_b, rfl, [rfl_b], -1.0, 1e9,
               OP.add, OP.mult)
    A.release()
    dbg("thr", thr_t[:, :], thr_b, (128, NT))
    dbg("ccq", ccq_t[:, 0:NT], ccq_b, (128, NT))
    dbg("keep0", keep_t[:, 0:S], keep_b, (128, S), mybir.dt.uint8)
    x2_s = dscr("x2_s", (T, DM))
    x3_s = dscr("x3_s", (T, DM))
    h2_s = dscr("h2_s", (T, INTER))
    wq_g = dscr("wq_g", (DM, INTER), BF16)
    wq_u = dscr("wq_u", (DM, INTER), BF16)
    wq_d = dscr("wq_d", (INTER, DM), BF16)
    sca05 = {}

    for wname, wd, wqd in (("g", wgT_d, wq_g), ("u", wuT_d, wq_u),
                           ("d", wdT_d, wq_d)):
        nrows, ncols = wd.shape[0], wd.shape[1]
        nch = nrows // 128
        nsub = ncols // CHW if ncols > CHW else 1
        csz = ncols // nsub
        first = True
        nit = nch * nsub

        def _src(i, wd=wd, csz=csz, nsub=nsub):
            c, sub = divmod(i, nsub)
            return wd[c * 128:(c + 1) * 128, sub * csz:(sub + 1) * csz]
        LD(wst[0][0][:, 0:csz], wst[0][1], _src(0), q="act")
        for i in range(nit):
            wt, wt_b = wst[i % 2]
            RED("dve", red1, red1_b, wt[:, 0:csz], [wt_b], OP.add,
                absval=True)
            if first:
                CPY("dve", sacc, sacc_b, red1, [red1_b])
                first = False
            else:
                TT("dve", sacc, sacc_b, sacc, red1, [sacc_b, red1_b],
                   OP.add)
            if i + 1 < nit:
                LD(wst[(i + 1) % 2][0][:, 0:csz], wst[(i + 1) % 2][1],
                   _src(i + 1), q="act")
        MM(psum[0][0][0:1, 0:1], psum[0][1], sacc, onecol[:, :],
           [sacc_b, onecol_b], start=True, stop=True)
        wi = {"g": 0, "u": 1, "d": 2}[wname]
        a05 = ffsc_t[:, wi:wi + 1]
        TS("dve", a05[0:1, 0:1], ffsc_b, psum[0][0][0:1, 0:1], [psum[0][1]],
           0.5 / (DM * INTER), 0.5e-5, OP.mult, OP.add)
        PB(a05, ffsc_b, a05[0:1, 0:1], [ffsc_b])
        n05, n05_b = A.alloc(f"ffw.n05{wname}", 1)
        TS("dve", n05, n05_b, a05, [ffsc_b], -1.0, None, OP.mult)
        sca05[wname] = (a05, ffsc_b)
        LD(wst[0][0][:, 0:csz], wst[0][1], _src(0), q="act")
        for i in range(nit):
            c, sub = divmod(i, nsub)
            wt, wt_b = wst[i % 2]
            ACTV(a1t[:, 0:csz], a1t_b, wt[:, 0:csz], [wt_b, n05_b],
                 AF.Sign, bias=n05)
            ACTV(a2t[:, 0:csz], a2t_b, wt[:, 0:csz], [wt_b, ffsc_b],
                 AF.Sign, bias=a05)
            if i + 1 < nit:
                LD(wst[(i + 1) % 2][0][:, 0:csz], wst[(i + 1) % 2][1],
                   _src(i + 1), q="act")
            wq2, wq2_b = wqo[i % 2]
            TT("gp", wq2[:, 0:csz], wq2_b, a1t[:, 0:csz],
               a2t[:, 0:csz], [a1t_b, a2t_b], OP.add)
            ST(wqd[c * 128:(c + 1) * 128, sub * csz:(sub + 1) * csz],
               DBuf("wq" + wname), wq2[:, 0:csz], [wq2_b])
    A.release()


    ao_s = dscr("ao_s", (NH * DV, T))
    x1_s = dscr("x1_s", (T, DM))
    A.mark()
    acc, acc_b = A.alloc("p3.acc", 64 * 130)
    pT = [A.alloc(f"p3.pT{i}", 512) for i in range(8)]
    addT = [A.alloc(f"p3.addT{i}", 512) for i in range(4)]
    ksb = [A.alloc(f"p3.k{i}", 128) for i in range(8)]
    vsb = [A.alloc(f"p3.v{i}", 132) for i in range(8)]
    qn_t = [A.alloc(f"p3.qn{i}", 512) for i in range(2)]
    qr_t = [A.alloc(f"p3.qr{i}", 512) for i in range(2)]
    negcc, negcc_b2 = A.alloc("p3.negcc", 512)
    negrow, negrow_b = A.alloc("p3.negrow", 512)
    ccT, ccT_b = A.alloc("p3.ccT", 128)
    keepbf, keepbf_b = A.alloc("p3.keepbf", 512, BF16)
    u_t, u_b = A.alloc("p3.u", 512)
    rec_t, rec_b = A.alloc("p3.rec", 1)
    aout, aout_b = A.alloc("p3.aout", 2048)

    for i in range(8):
        MEMSET("dve", vsb[i][0][:, 128:129], vsb[i][1], 1.0)

    TRP(psum[6][0][:, 0:128], psum[6][1], ccq_t[:, 0:128], ccq_b)
    CPY("act", ccT[:, 0:128], ccT_b, psum[6][0][:, 0:128], [psum[6][1]])

    for qpass in range(2):
        P.op("sync", lambda e, o=negrow[0:1, 0:512],
             i=ccT[qpass * 4:qpass * 4 + 4, 0:128]:
             e.dma_start(out=o, in_=i),
             reads=[ccT_b], writes=[negrow_b], dma=True)
        TS("dve", negrow[0:1, 0:512], negrow_b, negrow[0:1, 0:512],
           [negrow_b], -1.0, None, OP.mult)
        PB(negcc[:, 0:512], negcc_b2, negrow[0:1, 0:512], [negrow_b])
        for kbg in range(8):
            kbs = [kbg * 4 + i for i in range(4)]
            for i, kb in enumerate(kbs):
                for j4 in range(4):
                    qt = qpass * 4 + j4
                    CPY("dve", keepbf[:, j4 * 128:(j4 + 1) * 128], keepbf_b,
                        keep_t[:, qt * S + kb * 128:qt * S + (kb + 1) * 128],
                        [keep_b])
                hv = psum[6][0][:, :].bitcast(BF16)
                for j4 in range(4):
                    TRP(hv[:, j4 * 128:(j4 + 1) * 128], psum[6][1],
                        keepbf[:, j4 * 128:(j4 + 1) * 128], keepbf_b,
                        bf=True, start=(j4 == 0), stop=(j4 == 3))
                TS("dve", addT[i][0][:, 0:512], addT[i][1], hv[:, 0:512],
                   [psum[6][1]], 1e9, -1e9, OP.mult, OP.add)
            def _ldq(h2, par):
                LD(qn_t[par][0][:, 0:512], qn_t[par][1],
                   qn_s[h2 * 128:(h2 + 1) * 128,
                        qpass * 512:(qpass + 1) * 512],
                   sbufs=[DBuf("qn_s")])
                LD(qr_t[par][0][0:64, 0:512], qr_t[par][1],
                   qr_s[h2 * 64:(h2 + 1) * 64,
                        qpass * 512:(qpass + 1) * 512],
                   sbufs=[DBuf("qr_s")])

            def _ldk(h2, i2, slot):
                kt2, kt2_b = ksb[slot]
                LD(kt2[:, 0:128], kt2_b,
                   kn_s[h2 * 128:(h2 + 1) * 128,
                        kbs[i2] * 128:(kbs[i2] + 1) * 128],
                   sbufs=[DBuf("kn_s")])
                LD(vsb[slot][0][:, 0:128], vsb[slot][1],
                   v_s[kbs[i2] * 128:(kbs[i2] + 1) * 128,
                       h2 * 128:(h2 + 1) * 128], sbufs=[DBuf("v_s")],
                   q="gp")

            _ldq(0, 0)
            for i2 in range(4):
                _ldk(0, i2, i2)

            def do_scores(h):
                qn, qn_b2 = qn_t[h % 2]
                qr, qr_b2 = qr_t[h % 2]
                for i, kb in enumerate(kbs):
                    kt, kt_b = ksb[(h % 2) * 4 + i]
                    sc, sc_b = psum[i]
                    MM(sc[:, 0:512], sc_b, kt[:, 0:128], qn[:, 0:512],
                       [kt_b, qn_b2], start=True, stop=False)
                    MM(sc[:, 0:512], sc_b,
                       kr_t[0:64, kb * 128:(kb + 1) * 128], qr[0:64, 0:512],
                       [kr_b, qr_b2], start=False, stop=True)
                p = h % 2
                for i in range(4):
                    pt, pt_b = pT[p * 4 + i]
                    TT("dve", pt[:, 0:512], pt_b, psum[i][0][:, 0:512],
                       addT[i][0][:, 0:512], [psum[i][1], addT[i][1]],
                       OP.add)
                for i in range(4):
                    pt, pt_b = pT[p * 4 + i]
                    TT("gp", pt[:, 0:512], pt_b, pt[:, 0:512],
                       negcc[:, 0:512], [pt_b, negcc_b2], OP.add)
                for i in range(4):
                    pt, pt_b = pT[p * 4 + i]
                    ACTV(pt[:, 0:512], pt_b, pt[:, 0:512], [pt_b], AF.Exp)

            def do_av(h):
                p = h % 2
                for q4 in range(4):
                    pa, pa_b = psum[4 + q4]
                    for i in range(4):
                        pt, pt_b = pT[p * 4 + i]
                        MM(pa[:, 0:129], pa_b,
                           pt[:, q4 * 128:(q4 + 1) * 128],
                           vsb[p * 4 + i][0][:, 0:129],
                           [pt_b, vsb[p * 4 + i][1]],
                           start=(i == 0), stop=(i == 3))
                    base = (h * 4 + q4) * 130
                    if kbg == 0:
                        CPY("dve", acc[:, base:base + 129], acc_b,
                            psum[4 + q4][0][:, 0:129], [psum[4 + q4][1]])
                    else:
                        TT("dve", acc[:, base:base + 129], acc_b,
                           acc[:, base:base + 129], psum[4 + q4][0][:, 0:129],
                           [acc_b, psum[4 + q4][1]], OP.add)

            for h in range(NH):
                do_scores(h)
                if h >= 1:
                    do_av(h - 1)
                if h + 1 < NH:
                    _ldq(h + 1, (h + 1) % 2)
                    for i2 in range(4):
                        _ldk(h + 1, i2, ((h + 1) % 2) * 4 + i2)
            do_av(NH - 1)
        for q4 in range(4):
            qt = qpass * 4 + q4
            for h in range(NH):
                base = (h * 4 + q4) * 130
                RECIP(rec_t[:, 0:1], rec_b, acc[:, base + 128:base + 129],
                      [acc_b])
                TS("dve", aout[:, h * 128:(h + 1) * 128], aout_b,
                   acc[:, base:base + 128], [acc_b, rec_b], rec_t[:, 0:1],
                   None, OP.mult)
            for c4 in range(0, 16, 4):
                hps, hps_b = psum[6 + (c4 // 4) % 2]
                for j in range(4):
                    c = c4 + j
                    TRP(hps[:, j * 128:(j + 1) * 128], hps_b,
                        aout[:, c * 128:(c + 1) * 128], aout_b,
                        start=(j == 0), stop=(j == 3))
                CPY("act", u_t[:, 0:512], u_b, hps[:, 0:512], [hps_b])
                for j in range(4):
                    c = c4 + j
                    ST(ao_s[c * 128:(c + 1) * 128, qt * 128:(qt + 1) * 128],
                       DBuf("ao_s"), u_t[:, j * 128:(j + 1) * 128], [u_b])
    A.release()
    A.release()

    A.mark()
    Wo_t, Wo_b = A.alloc("Wo", 16 * 2048)
    LD(Wo_t, Wo_b, Wo_d.rearrange("(c p) f -> p c f", p=128))
    aoT, aoT_b = A.alloc("p3b.aoT", 16 * 128)
    xot, xot_b = A.alloc("p3b.x", DM)
    x1t, x1t_b = A.alloc("p3b.x1", DM)
    for qt in range(NT):
        LD(aoT, aoT_b, ao_s[:, qt * 128:(qt + 1) * 128].rearrange(
            "(c p) f -> p c f", p=128), sbufs=[DBuf("ao_s")])
        LD(xot, xot_b, x_own_d[qt * 128:(qt + 1) * 128, :])
        for oc in range(4):
            po, po_b = psum[oc]
            for c in range(16):
                MM(po[:, 0:512], po_b, aoT[:, c * 128:(c + 1) * 128],
                   Wo_t[:, c * 2048 + oc * 512:c * 2048 + (oc + 1) * 512],
                   [aoT_b, Wo_b], start=(c == 0), stop=(c == 15))
            TT("dve", x1t[:, oc * 512:(oc + 1) * 512], x1t_b,
               po[:, 0:512], xot[:, oc * 512:(oc + 1) * 512],
               [po_b, xot_b], OP.add)
        ST(x1_s[qt * 128:(qt + 1) * 128, :], DBuf("x1_s"), x1t[:, :],
           [x1t_b])
    dbg("x1", x1t[:, :], x1t_b, (128, DM))
    A.release()

    def ffn_apply(tag, xin_s, xout_s, xout_name, xin_name):
        A.mark()
        xs1g, xs1g_b = A.alloc(tag + ".xs1g", 8)
        xs1u, xs1u_b = A.alloc(tag + ".xs1u", 8)
        rm2, rm2_b = A.alloc(tag + ".rm2", 8)
        xsd, xsd_b = A.alloc(tag + ".xsd", 8)
        A.mark()
        xqT = [A.alloc(f"{tag}.xqT{i}", 2048, BF16) for i in range(8)]
        xt, xt_b = A.alloc(tag + ".x", DM)
        lnt, lnt_b = A.alloc(tag + ".ln", DM)
        sqs, sqs_b = A.alloc(tag + ".sq", DM)
        t_, t_b = A.alloc(tag + ".t", DM)
        m_, m_b = A.alloc(tag + ".m", DM)
        i_, i_b = A.alloc(tag + ".i", DM, mybir.dt.int32)
        i_f = i_.bitcast(F32)[:, 0:DM]
        xqb, xqb_b = A.alloc(tag + ".xqb", DM, BF16)
        mx, mx_b = A.alloc(tag + ".mx", 1)
        xs_, xs_b = A.alloc(tag + ".xs", 1)
        r_, r_b = A.alloc(tag + ".r", 1)
        for tt in range(8):
            LD(xt, xt_b, xin_s[tt * 128:(tt + 1) * 128, :],
               sbufs=[DBuf(xin_name)])
            layernorm_tile(f"{tag}.ln{tt}", lnt, lnt_b, xt, xt_b,
                           scratch=(sqs, sqs_b))
            RED("dve", mx, mx_b, lnt, [lnt_b], OP.max, absval=True)
            TS("dve", xs_, xs_b, mx, [mx_b], 1.0 / 127.0, 1e-5, OP.mult,
               OP.add)
            RECIP(r_, r_b, xs_, [xs_b])
            TT("dve", xs1g[:, tt:tt + 1], xs1g_b, xs_, ffsc_t[:, 0:1],
               [xs_b, ffsc_b], OP.mult)
            TT("dve", xs1u[:, tt:tt + 1], xs1u_b, xs_, ffsc_t[:, 1:2],
               [xs_b, ffsc_b], OP.mult)
            TS("dve", t_, t_b, lnt, [lnt_b, r_b], r_, 0.5, OP.mult, OP.add)
            CPY("dve", i_, i_b, t_, [t_b])
            CPY("dve", m_, m_b, i_, [i_b])
            TT("dve", i_f, i_b, t_, m_, [t_b, m_b], OP.is_lt)
            TT("dve", xqb, xqb_b, m_, i_f, [m_b, i_b], OP.subtract)
            transpose_tile(xqT[tt][0], xqT[tt][1], xqb, xqb_b, 16, bf=True)
        wstr = [A.alloc(f"{tag}.w{i}", 512, BF16) for i in range(2)]
        gbuf = [A.alloc(f"{tag}.g{i}", 512) for i in range(8)]
        ubuf, ubuf_b = A.alloc(tag + ".u", 512)
        h2c, h2c_b = A.alloc(tag + ".h2c", 512)
        mxc, mxc_b = A.alloc(tag + ".mxc", 1)
        def _gsrc(j):
            of_, ctr_ = divmod(j, 16)
            return wq_g[ctr_ * 128:(ctr_ + 1) * 128,
                        of_ * 512:(of_ + 1) * 512]

        def _usrc(j):
            of_, ctr_ = divmod(j, 16)
            return wq_u[ctr_ * 128:(ctr_ + 1) * 128,
                        of_ * 512:(of_ + 1) * 512]
        LD(wstr[0][0], wstr[0][1], _gsrc(0), sbufs=[DBuf("wqg")], q="act")
        for of in range(16):
            for ctr in range(16):
                j = of * 16 + ctr
                wt, wt_b = wstr[j % 2]
                for tt in range(8):
                    MM(psum[tt][0][:, 0:512], psum[tt][1],
                       xqT[tt][0][:, ctr * 128:(ctr + 1) * 128], wt,
                       [xqT[tt][1], wt_b], start=(ctr == 0),
                       stop=(ctr == 15))
                nj = j + 1
                if ctr + 1 < 16:
                    LD(wstr[nj % 2][0], wstr[nj % 2][1], _gsrc(nj),
                       sbufs=[DBuf("wqg")], q="act" if nj % 2 else "sync")
                else:
                    LD(wstr[nj % 2][0], wstr[nj % 2][1], _usrc(j - 15),
                       sbufs=[DBuf("wqu")], q="act" if nj % 2 else "sync")
            for tt in range(8):
                P.op("act", lambda e, o=gbuf[tt][0], i=psum[tt][0][:, 0:512],
                     s=xs1g[:, tt:tt + 1]: e.activation(
                         o, i, AF.Silu, bias=0.0, scale=s),
                     reads=[psum[tt][1], xs1g_b], writes=[gbuf[tt][1]])
            for ctr in range(16):
                j = of * 16 + ctr
                wt, wt_b = wstr[ctr % 2]
                for tt in range(8):
                    MM(psum[tt][0][:, 0:512], psum[tt][1],
                       xqT[tt][0][:, ctr * 128:(ctr + 1) * 128], wt,
                       [xqT[tt][1], wt_b], start=(ctr == 0),
                       stop=(ctr == 15))
                if ctr + 1 < 16:
                    LD(wstr[(ctr + 1) % 2][0], wstr[(ctr + 1) % 2][1],
                       _usrc(j + 1), sbufs=[DBuf("wqu")],
                       q="act" if (ctr + 1) % 2 else "sync")
                elif of + 1 < 16:
                    LD(wstr[(ctr + 1) % 2][0], wstr[(ctr + 1) % 2][1],
                       _gsrc((of + 1) * 16), sbufs=[DBuf("wqg")],
                       q="act" if (ctr + 1) % 2 else "sync")
            for tt in range(8):
                P.op("act", lambda e, o=ubuf, i=psum[tt][0][:, 0:512],
                     s=xs1u[:, tt:tt + 1]: e.activation(
                         o, i, AF.Identity, bias=0.0, scale=s),
                     reads=[psum[tt][1], xs1u_b], writes=[ubuf_b])
                TT("dve", h2c, h2c_b, gbuf[tt][0], ubuf,
                   [gbuf[tt][1], ubuf_b], OP.mult)
                RED("dve", mxc, mxc_b, h2c, [h2c_b], OP.max, absval=True)
                if of == 0:
                    CPY("dve", rm2[:, tt:tt + 1], rm2_b, mxc, [mxc_b])
                else:
                    TT("dve", rm2[:, tt:tt + 1], rm2_b, rm2[:, tt:tt + 1],
                       mxc, [rm2_b, mxc_b], OP.max)
                ST(h2_s[tt * 128:(tt + 1) * 128, of * 512:(of + 1) * 512],
                   DBuf("h2_s"), h2c, [h2c_b])
        A.release()
        A.mark()
        xq2T = [A.alloc(f"{tag}.xq2T{i}", INTER, BF16) for i in range(8)]
        hc, hc_b = A.alloc(tag + ".hc", 512)
        tc, tc_b = A.alloc(tag + ".tc", 512)
        mc, mc_b = A.alloc(tag + ".mc", 512)
        ic_, ic_b = A.alloc(tag + ".ic", 512, mybir.dt.int32)
        ic_f = ic_.bitcast(F32)[:, 0:512]
        qc2, qc2_b = A.alloc(tag + ".qc2", 512, BF16)
        r2, r2_b = A.alloc(tag + ".r2", 1)
        xs2, xs2_b = A.alloc(tag + ".xs2", 1)
        for tt in range(8):
            TS("dve", xs2, xs2_b, rm2[:, tt:tt + 1], [rm2_b], 1.0 / 127.0,
               1e-5, OP.mult, OP.add)
            RECIP(r2, r2_b, xs2, [xs2_b])
            TT("dve", xsd[:, tt:tt + 1], xsd_b, xs2, ffsc_t[:, 2:3],
               [xs2_b, ffsc_b], OP.mult)
            for oc in range(16):
                LD(hc, hc_b, h2_s[tt * 128:(tt + 1) * 128,
                                  oc * 512:(oc + 1) * 512],
                   sbufs=[DBuf("h2_s")])
                TS("dve", tc, tc_b, hc, [hc_b, r2_b], r2, 0.5, OP.mult,
                   OP.add)
                CPY("dve", ic_, ic_b, tc, [tc_b])
                CPY("dve", mc, mc_b, ic_, [ic_b])
                TT("dve", ic_f, ic_b, tc, mc, [tc_b, mc_b], OP.is_lt)
                TT("dve", qc2, qc2_b, mc, ic_f, [mc_b, ic_b], OP.subtract)
                hv = psum[6][0][:, :].bitcast(BF16)
                for j in range(4):
                    TRP(hv[:, j * 128:(j + 1) * 128], psum[6][1],
                        qc2[:, j * 128:(j + 1) * 128], qc2_b, bf=True,
                        start=(j == 0), stop=(j == 3))
                CPY("act", xq2T[tt][0][:, oc * 512:(oc + 1) * 512],
                    xq2T[tt][1], hv[:, 0:512], [psum[6][1]])
        wsd = [A.alloc(f"{tag}.wd{i}", 512, BF16) for i in range(2)]
        xrt, xrt_b = A.alloc(tag + ".xr", 512)
        ot, ot_b = A.alloc(tag + ".ot", 512)
        def _dsrc(j):
            dc_, ct_ = divmod(j, 64)
            return wq_d[ct_ * 128:(ct_ + 1) * 128,
                        dc_ * 512:(dc_ + 1) * 512]
        LD(wsd[0][0], wsd[0][1], _dsrc(0), sbufs=[DBuf("wqd")], q="act")
        for dcol in range(4):
            for ctr in range(64):
                j = dcol * 64 + ctr
                wt, wt_b = wsd[j % 2]
                for tt in range(8):
                    MM(psum[tt][0][:, 0:512], psum[tt][1],
                       xq2T[tt][0][:, ctr * 128:(ctr + 1) * 128], wt,
                       [xq2T[tt][1], wt_b], start=(ctr == 0),
                       stop=(ctr == 63))
                if j + 1 < 256:
                    LD(wsd[(j + 1) % 2][0], wsd[(j + 1) % 2][1],
                       _dsrc(j + 1), sbufs=[DBuf("wqd")],
                       q="act" if (j + 1) % 2 else "sync")
            for tt in range(8):
                LD(xrt, xrt_b, xin_s[tt * 128:(tt + 1) * 128,
                                     dcol * 512:(dcol + 1) * 512],
                   sbufs=[DBuf(xin_name)])
                STT("dve", ot, ot_b, psum[tt][0][:, 0:512],
                    xsd[:, tt:tt + 1], xrt, [psum[tt][1], xsd_b, xrt_b],
                    OP.mult, OP.add)
                ST(xout_s[tt * 128:(tt + 1) * 128,
                          dcol * 512:(dcol + 1) * 512], DBuf(xout_name),
                   ot, [ot_b])
        A.release()
        A.release()

    ffn_apply("f1", x1_s, x2_s, "x2_s", "x1_s")

    hs_s = dscr("hs_s", (CONV, T))
    gate_s = dscr("gate_s", (T, 4096))
    yn_s = dscr("yn_s", (4096, T))
    NG = len(CC_GROUPS[0])
    hcc_in = dscr("hcc_in", (128, 34 * 3))
    hcc_out = dscr("hcc_out", (NG * 128, 34 * 3))
    st_in = dscr("st_in", (MN, MNH * MHD + 64))
    st_out = dscr("st_out", (NG * MN, MNH * MHD + 64))

    A.mark()
    A_t, A_t_b = A.alloc("p5.A", 1)
    dtb_t, dtb_b = A.alloc("p5.dtb", 1)
    Db, Db_b = A.alloc("p5.Db", 64)
    convb_t, convb_b = A.alloc("p5.convb", 34)
    convw_t, convw_b = A.alloc("p5.convw", 34 * 4)
    hselb, hselb_b = A.alloc("p5.hselb", 4)
    dt_t, dt_t_b = A.alloc("p5.dt", T)
    dtA_t, dtA_b = A.alloc("p5.dtA", T)
    LTs, LTs_b = A.alloc("p5.LTs", 8 * 64)
    eLTs, eLTs_b = A.alloc("p5.eLTs", 8 * 64)
    logs_t, logs_b = A.alloc("p5.logs", 1)
    mnwb, mnwb_b = A.alloc("p5.mnwb", 4096)
    mut, mut_b = A.alloc("p5.mut", 128)

    LD(A_t[0:64, 0:1], A_t_b, Alog_d[0:64, 0:1])
    ACTV(A_t[0:64, 0:1], A_t_b, A_t[0:64, 0:1], [A_t_b], AF.Exp)
    TS("dve", A_t[0:64, 0:1], A_t_b, A_t[0:64, 0:1], [A_t_b], -1.0, None,
       OP.mult)
    LD(dtb_t[0:64, 0:1], dtb_b, dtb_d[0:64, 0:1])
    LD(Db[0:64, 0:1], Db_b, Dd_d[0:64, 0:1])
    TRP(psum[7][0][0:1, 0:64], psum[7][1], Db[0:64, 0:1], Db_b)
    CPY("dve", Db[0:1, 0:64], Db_b, psum[7][0][0:1, 0:64], [psum[7][1]])
    PB(Db[:, 0:64], Db_b, Db[0:1, 0:64], [Db_b])
    LD(convb_t[:, 0:34], convb_b,
       convb_d.rearrange("(c p) one -> p (c one)", p=128), slow=True)
    LD(convw_t[:, 0:34 * 4], convw_b,
       convw_d.rearrange("(c p) k -> p c k", p=128))
    LD(mnwb[0:1, 0:4096], mnwb_b, mnw_d)
    PB(mnwb[:, 0:4096], mnwb_b, mnwb[0:1, 0:4096], [mnwb_b])
    LD(hselb[0:4, 0:1], hselb_b, halo_sel_d)
    TRP(psum[7][0][0:1, 0:4], psum[7][1], hselb[0:4, 0:1], hselb_b)
    CPY("dve", hselb[0:1, 0:4], hselb_b, psum[7][0][0:1, 0:4], [psum[7][1]])
    PB(hselb[:, 0:4], hselb_b, hselb[0:1, 0:4], [hselb_b])
    MEMSET("gp", mut[:, 0:128], mut_b, 0.0)
    P.op("gp", lambda e: e.affine_select(
        out=mut[:, 0:128], in_=mut[:, 0:128], compare_op=OP.is_ge,
        fill=-1e38, base=0, pattern=[[1, 128]], channel_multiplier=-1),
        reads=[mut_b], writes=[mut_b])

    A.mark()
    ln3T, ln3T_b = A.alloc("p5.ln3T", 16 * 512)
    xt5, xt5_b = A.alloc("p5.x", DM)
    ln5, ln5_b = A.alloc("p5.ln", DM)
    sq5, sq5_b = A.alloc("p5.sq", DM)
    wst5 = [A.alloc(f"p5.w{i}", 16 * 512) for i in range(2)]
    gev, gev_b = A.alloc("p5.gev", 512)
    tailb, tail_b = A.alloc("p5.tail", 34 * 3)
    raw3, raw3_b = A.alloc("p5.raw3", 34 * 3)
    ext, ext_b = A.alloc("p5.ext", 515 + 5)
    cacc, cacc_b = A.alloc("p5.cacc", 512)
    hsa, hsa_b = A.alloc("p5.hsa", 512)

    for half in range(2):
        for tt in range(4):
            ti = half * 4 + tt
            LD(xt5, xt5_b, x2_s[ti * 128:(ti + 1) * 128, :],
               sbufs=[DBuf("x2_s")])
            layernorm_tile(f"p5.ln{ti}", ln5, ln5_b, xt5, xt5_b,
                           scratch=(sq5, sq5_b))
            transpose_tile(ln3T, ln3T_b, ln5, ln5_b, 16,
                           colmap=lambda c, t=tt: c * 512 + t * 128)
        def _ld_gc(gc2):
            wt2, wt2_b = wst5[gc2 % 2]
            LD(wt2[:, 0:16 * 512], wt2_b,
               inpT_d[:, gc2 * 512:(gc2 + 1) * 512].rearrange(
                   "(c p) f -> p c f", p=128),
               q="act" if gc2 % 2 else "sync")
        _ld_gc(0)
        for gc in range(8):
            wt, wt_b = wst5[gc % 2]
            if gc + 1 < 8:
                _ld_gc(gc + 1)
            for tsub in range(4):
                pg, pg_b = psum[tsub % 2]
                for c in range(16):
                    MM(pg[:, 0:512], pg_b,
                       ln3T[:, c * 512 + tsub * 128:
                            c * 512 + (tsub + 1) * 128],
                       wt[:, c * 512:(c + 1) * 512], [ln3T_b, wt_b],
                       start=(c == 0), stop=(c == 15))
                CPY("act", gev, gev_b, pg[:, 0:512], [pg_b])
                ST(gate_s[half * 512 + tsub * 128:
                          half * 512 + (tsub + 1) * 128,
                          gc * 512:(gc + 1) * 512], DBuf("gate_s"), gev,
                   [gev_b])
        def _ld_fc(fc2):
            fw2 = 128 if fc2 < 34 else 64
            f02 = 4096 + fc2 * 128
            wt2, wt2_b = wst5[fc2 % 2]
            LD(wt2[:, 0:16 * fw2], wt2_b,
               inpT_d[:, f02:f02 + fw2].rearrange("(c p) f -> p c f",
                                                  p=128),
               q="act" if fc2 % 2 else "sync")
        _ld_fc(0)
        for fc in range(34 + 1):
            fw = 128 if fc < 34 else 64
            f0 = 4096 + fc * 128
            wt, wt_b = wst5[fc % 2]
            ph, ph_b = psum[2 + fc % 2]
            for c in range(16):
                MM(ph[0:fw, 0:512], ph_b, wt[:, c * fw:(c + 1) * fw],
                   ln3T[:, c * 512:(c + 1) * 512], [wt_b, ln3T_b],
                   start=(c == 0), stop=(c == 15))
            if fc < 34:
                _ld_fc(fc + 1)
            if fc == 34:
                P.op("act", lambda e,
                     o=dt_t[0:64, half * 512:(half + 1) * 512],
                     i=ph[0:64, 0:512], b2=dtb_t[0:64, 0:1]:
                     e.activation(o, i, AF.Exp, bias=b2, scale=1.0),
                     reads=[ph_b, dtb_b], writes=[dt_t_b])
                P.op("act", lambda e,
                     o=dt_t[0:64, half * 512:(half + 1) * 512],
                     i=dt_t[0:64, half * 512:(half + 1) * 512]:
                     e.activation(o, i, AF.Ln, bias=1.0, scale=1.0),
                     reads=[dt_t_b], writes=[dt_t_b])
                continue
            if half == 0:
                CPY("dve", raw3[:, fc * 3:(fc + 1) * 3], raw3_b,
                    ph[0:128, 0:3], [ph_b])
                MEMSET("dve", ext[:, 0:3], ext_b, 0.0)
            else:
                CPY("dve", ext[:, 0:3], ext_b, tailb[:, fc * 3:(fc + 1) * 3],
                    [tail_b])
            CPY("act", ext[:, 3:515], ext_b, ph[0:128, 0:512], [ph_b])
            CPY("dve", tailb[:, fc * 3:(fc + 1) * 3], tail_b,
                ext[:, 512:515], [ext_b])
            TS("dve", cacc, cacc_b, ext[:, 3:515], [ext_b, convw_b],
               convw_t[:, fc * 4 + 3:fc * 4 + 4], None, OP.mult)
            for kk in range(3):
                STT("dve", cacc, cacc_b, ext[:, kk:kk + 512],
                    convw_t[:, fc * 4 + kk:fc * 4 + kk + 1], cacc,
                    [ext_b, convw_b, cacc_b], OP.mult, OP.add)
            P.op("act", lambda e, o=hsa, i=cacc,
                 b2=convb_t[:, fc:fc + 1]: e.activation(
                     o, i, AF.Silu, bias=b2, scale=1.0),
                 reads=[cacc_b, convb_b], writes=[hsa_b])
            ST(hs_s[fc * 128:(fc + 1) * 128,
                    half * 512:(half + 1) * 512], DBuf("hs_s"), hsa,
               [hsa_b])
    ST(hcc_in[:, :], DBuf("hcc_in"), tailb[:, 0:102], [tail_b])
    P.op("gp", lambda e: e.collective_compute(
        "AllGather", OP.bypass, replica_groups=CC_GROUPS,
        ins=[hcc_in.opt()], outs=[hcc_out.opt()]),
        reads=[DBuf("hcc_in")], writes=[DBuf("hcc_out")], cc=True)
    halom, halom_b = A.alloc("p5.halom", 102)
    halog, halog_b = A.alloc("p5.halog", 102)
    MEMSET("dve", halom[:, 0:102], halom_b, 0.0)
    for j in range(NG):
        LD(halog[:, 0:102], halog_b, hcc_out[j * 128:(j + 1) * 128, :],
           sbufs=[DBuf("hcc_out")])
        STT("dve", halom[:, 0:102], halom_b, halog[:, 0:102],
            hselb[:, j:j + 1], halom[:, 0:102],
            [halog_b, hselb_b, halom_b], OP.mult, OP.add)
    ext6, ext6_b = A.alloc("p5.ext6", 8)
    out3, out3_b = A.alloc("p5.out3", 4)
    for fc in range(34):
        CPY("dve", ext6[:, 0:3], ext6_b, halom[:, fc * 3:(fc + 1) * 3],
            [halom_b])
        CPY("dve", ext6[:, 3:6], ext6_b, raw3[:, fc * 3:(fc + 1) * 3],
            [raw3_b])
        TS("dve", out3[:, 0:3], out3_b, ext6[:, 3:6], [ext6_b, convw_b],
           convw_t[:, fc * 4 + 3:fc * 4 + 4], None, OP.mult)
        for kk in range(3):
            STT("dve", out3[:, 0:3], out3_b, ext6[:, kk:kk + 3],
                convw_t[:, fc * 4 + kk:fc * 4 + kk + 1], out3[:, 0:3],
                [ext6_b, convw_b, out3_b], OP.mult, OP.add)
        P.op("act", lambda e, o=out3[:, 0:3], i=out3[:, 0:3],
             b2=convb_t[:, fc:fc + 1]: e.activation(
                 o, i, AF.Silu, bias=b2, scale=1.0),
             reads=[out3_b, convb_b], writes=[out3_b])
        ST(hs_s[fc * 128:(fc + 1) * 128, 0:3], DBuf("hs_s"), out3[:, 0:3],
           [out3_b])
    A.release()

    A.mark()
    TS("dve", dtA_t[0:64, :], dtA_b, dt_t[0:64, :], [dt_t_b, A_t_b],
       A_t[0:64, 0:1], None, OP.mult)
    Lc, Lc_b = A.alloc("p5.Lc", 128)
    Lcs, Lcs_b = A.alloc("p5.Lcs", 8 * 128)
    for c in range(8):
        P.op("dve", lambda e, o=Lc[0:64, 0:128],
             d0=dtA_t[0:64, c * 128:(c + 1) * 128]:
             e.tensor_tensor_scan(o, d0, d0, 0.0, OP.add, OP.bypass),
             reads=[dtA_b], writes=[Lc_b])
        CPY("dve", Lcs[0:64, c * 128:(c + 1) * 128], Lcs_b,
            Lc[0:64, 0:128], [Lc_b])
        TRP(psum[7][0][0:128, 0:64], psum[7][1], Lc[0:64, 0:128], Lc_b)
        CPY("dve", LTs[:, c * 64:(c + 1) * 64], LTs_b,
            psum[7][0][0:128, 0:64], [psum[7][1]])
        ACTV(eLTs[:, c * 64:(c + 1) * 64], eLTs_b,
             LTs[:, c * 64:(c + 1) * 64], [LTs_b], AF.Exp)
        if c == 0:
            CPY("dve", logs_t[0:64, 0:1], logs_b, Lc[0:64, 127:128], [Lc_b])
        else:
            TT("dve", logs_t[0:64, 0:1], logs_b, logs_t[0:64, 0:1],
               Lc[0:64, 127:128], [logs_b, Lc_b], OP.add)

    def bc3i(ap2d, h, d):
        return ap2d.rearrange("p (h one) -> p h one", one=1).to_broadcast(
            [128, h, d])

    def bc3m(ap2d, h, d):
        return ap2d.rearrange("p (one d) -> p one d", one=1).to_broadcast(
            [128, h, d])

    def v3(ap2d, h, d):
        return ap2d.rearrange("p (h d) -> p h d", h=h)

    hscm, hscm_b = A.alloc("p5.hscm", 32 * 128)
    hstm, hstm_b = A.alloc("p5.hstm", 4096)
    xti, xti_b = A.alloc("p5.xti", 4096)
    Bcm, Bcm_b = A.alloc("p5.Bcm", T)
    Ccm, Ccm_b = A.alloc("p5.Ccm", T)
    Btm, Btm_b = A.alloc("p5.Btm", 128)
    dtT, dtT_b = A.alloc("p5.dtT", 64)
    Scur, Scur_b = A.alloc("p5.Scur", 4096)
    Pall, Pall_b = A.alloc("p5.Pall", 64)
    DecT, DecT_b = A.alloc("p5.DecT", 64)
    xdec, xdec_b = A.alloc("p5.xdec", 4096)
    stmp, stmp_b = A.alloc("p5.stmp", 4096)
    LD(Bcm[:, :], Bcm_b, hs_s[4096:4224, :], sbufs=[DBuf("hs_s")])
    LD(Ccm[:, :], Ccm_b, hs_s[4224:4352, :], sbufs=[DBuf("hs_s")])
    MEMSET("dve", Scur[:, :], Scur_b, 0.0)

    def chunk_prep(c):
        LD(hscm[:, :], hscm_b,
           hs_s[0:4096, c * 128:(c + 1) * 128].rearrange(
               "(f p) t -> p f t", p=128), sbufs=[DBuf("hs_s")])
        transpose_tile(hstm, hstm_b, hscm, hscm_b, 32, pbanks=(6,))
        TRP(psum[7][0][0:128, 0:64], psum[7][1],
            dt_t[0:64, c * 128:(c + 1) * 128], dt_t_b)
        CPY("dve", dtT[:, 0:64], dtT_b, psum[7][0][0:128, 0:64],
            [psum[7][1]])
        TT("dve", v3(xti[:, 0:4096], 64, 64), xti_b,
           v3(hstm[:, 0:4096], 64, 64), bc3i(dtT[:, 0:64], 64, 64),
           [hstm_b, dtT_b], OP.mult)
        TRP(psum[7][0][0:128, 0:128], psum[7][1],
            Bcm[:, c * 128:(c + 1) * 128], Bcm_b)
        CPY("dve", Btm[:, 0:128], Btm_b, psum[7][0][0:128, 0:128],
            [psum[7][1]])
        P.op("sync", lambda e, o=Pall[0:1, 0:64],
             i=eLTs[127:128, c * 64:(c + 1) * 64]: e.dma_start(out=o, in_=i),
             reads=[eLTs_b], writes=[Pall_b], dma=True)
        PB(Pall[:, 0:64], Pall_b, Pall[0:1, 0:64], [Pall_b])
        P.op("sync", lambda e, o=DecT[0:1, 0:64],
             i=LTs[127:128, c * 64:(c + 1) * 64]: e.dma_start(out=o, in_=i),
             reads=[LTs_b], writes=[DecT_b], dma=True)
        PB(DecT[:, 0:64], DecT_b, DecT[0:1, 0:64], [DecT_b])
        TT("dve", DecT[:, 0:64], DecT_b, DecT[:, 0:64],
           LTs[:, c * 64:(c + 1) * 64], [DecT_b, LTs_b], OP.subtract)
        ACTV(DecT[:, 0:64], DecT_b, DecT[:, 0:64], [DecT_b], AF.Exp)

    def state_step():
        TT("dve", v3(xdec[:, 0:4096], 64, 64), xdec_b,
           v3(xti[:, 0:4096], 64, 64), bc3i(DecT[:, 0:64], 64, 64),
           [xti_b, DecT_b], OP.mult)
        for g in range(8):
            MM(psum[g][0][:, 0:512], psum[g][1], Btm[:, 0:128],
               xdec[:, g * 512:(g + 1) * 512], [Btm_b, xdec_b],
               start=True, stop=True)
        TT("dve", v3(stmp[:, 0:4096], 64, 64), stmp_b,
           v3(Scur[:, 0:4096], 64, 64), bc3i(Pall[:, 0:64], 64, 64),
           [Scur_b, Pall_b], OP.mult)
        for g in range(8):
            TT("dve", Scur[:, g * 512:(g + 1) * 512], Scur_b,
               stmp[:, g * 512:(g + 1) * 512], psum[g][0][:, 0:512],
               [stmp_b, psum[g][1]], OP.add)

    for c in range(8):
        chunk_prep(c)
        state_step()
    A.mark()
    stp, stp_b = A.alloc("p5.stp", MNH * MHD + 64)
    CPY("dve", stp[:, 0:4096], stp_b, Scur[:, :], [Scur_b])
    MEMSET("dve", stp[:, 4096:4160], stp_b, 0.0)
    CPY("dve", stp[0:64, 4096:4097], stp_b, logs_t[0:64, 0:1], [logs_b])
    ST(st_in[:, :], DBuf("st_in"), stp[:, 0:4160], [stp_b])
    P.op("gp", lambda e: e.collective_compute(
        "AllGather", OP.bypass, replica_groups=CC_GROUPS,
        ins=[st_in.opt()], outs=[st_out.opt()]),
        reads=[DBuf("st_in")], writes=[DBuf("st_out")], cc=True)
    Sg, Sg_b = A.alloc("p5.Sg", 4160)
    logsg, logsg_b = A.alloc("p5.logsg", 4 * 64)
    plog, plog_b = A.alloc("p5.plog", 4)
    selb, selb_b = A.alloc("p5.selb", 16)
    smb, smb_b = A.alloc("p5.smb", 4)
    Dcy, Dcy_b = A.alloc("p5.Dcy", 4 * 64)
    for j in range(NG):
        P.op("sync", lambda e, o=logsg[0:64, j:j + 1],
             i=st_out[j * 128:j * 128 + 64, 4096:4097]:
             e.dma_start(out=o, in_=i, allow_slow_non_contiguous=True),
             reads=[DBuf("st_out")], writes=[logsg_b], dma=True)
    LD(selb[0:4, 0:4], selb_b, state_sel_d)
    LD(smb[0:4, 0:1], smb_b, state_mask_d)
    sc1, sc1_b = A.alloc("p5.sc1", 1)
    for j in range(NG):
        MEMSET("dve", plog[0:64, j:j + 1], plog_b, 0.0)
        for i in range(4):
            P.op("sync", lambda e, o=sc1[0:1, 0:1], i2=selb[j:j + 1, i:i + 1]:
                 e.dma_start(out=o, in_=i2),
                 reads=[selb_b], writes=[sc1_b], dma=True)
            PB(sc1[0:64, 0:1], sc1_b, sc1[0:1, 0:1], [sc1_b])
            STT("dve", plog[0:64, j:j + 1], plog_b, logsg[0:64, i:i + 1],
                sc1[0:64, 0:1], plog[0:64, j:j + 1],
                [logsg_b, sc1_b, plog_b], OP.mult, OP.add)
        ACTV(plog[0:64, j:j + 1], plog_b, plog[0:64, j:j + 1], [plog_b],
             AF.Exp)
        P.op("sync", lambda e, o=sc1[0:1, 0:1], i2=smb[j:j + 1, 0:1]:
             e.dma_start(out=o, in_=i2),
             reads=[smb_b], writes=[sc1_b], dma=True)
        PB(sc1[0:64, 0:1], sc1_b, sc1[0:1, 0:1], [sc1_b])
        TS("dve", plog[0:64, j:j + 1], plog_b, plog[0:64, j:j + 1],
           [plog_b, sc1_b], sc1[0:64, 0:1], None, OP.mult)
        TRP(psum[7][0][0:1, 0:64], psum[7][1], plog[0:64, j:j + 1], plog_b)
        CPY("dve", Dcy[0:1, j * 64:(j + 1) * 64], Dcy_b,
            psum[7][0][0:1, 0:64], [psum[7][1]])
        PB(Dcy[:, j * 64:(j + 1) * 64], Dcy_b, Dcy[0:1, j * 64:(j + 1) * 64],
           [Dcy_b])
    Sstart, Sstart_b = A.alloc("p5.Sstart", 4096)
    MEMSET("dve", Sstart[:, :], Sstart_b, 0.0)
    for j in range(NG):
        LD(Sg[:, 0:4160], Sg_b, st_out[j * 128:(j + 1) * 128, :],
           sbufs=[DBuf("st_out")])
        for h in range(MNH):
            STT("dve", Sstart[:, h * 64:(h + 1) * 64], Sstart_b,
                Sg[:, h * 64:(h + 1) * 64], Dcy[:, j * 64 + h:j * 64 + h + 1],
                Sstart[:, h * 64:(h + 1) * 64], [Sg_b, Dcy_b, Sstart_b],
                OP.mult, OP.add)
    CPY("dve", Scur[:, :], Scur_b, Sstart[:, :], [Sstart_b])
    A.release()

    CBT, CBT_b = A.alloc("p5.CBT", 128)
    Lrow, Lrow_b = A.alloc("p5.Lrow", 8192)
    Et, Et_b = Lrow, Lrow_b
    ych, ych_b = A.alloc("p5.y", 4096)
    yi2, yi2_b = stmp, stmp_b
    gch, gch_b = hscm, hscm_b
    ssqy, ssqy_b = A.alloc("p5.ssqy", 1)
    ynT, ynT_b = xdec, xdec_b
    sgch, sgch_b = gch, gch_b
    sqy, sqy_b = Et, Et_b
    for c in range(8):
        chunk_prep(c)
        MM(psum[5][0][:, 0:128], psum[5][1], Bcm[:, c * 128:(c + 1) * 128],
           Ccm[:, c * 128:(c + 1) * 128], [Bcm_b, Ccm_b], start=True,
           stop=True)
        CPY("dve", CBT[:, 0:128], CBT_b, psum[5][0][:, 0:128], [psum[5][1]])
        P.op("sync", lambda e, o=Lrow[0:1, 0:8192],
             i=Lcs[0:64, c * 128:(c + 1) * 128]: e.dma_start(out=o, in_=i),
             reads=[Lcs_b], writes=[Lrow_b], dma=True)
        PB(Lrow[:, 0:8192], Lrow_b, Lrow[0:1, 0:8192], [Lrow_b])
        TT("dve", v3(Et[:, 0:8192], 64, 128), Et_b,
           v3(Lrow[:, 0:8192], 64, 128),
           bc3i(LTs[:, c * 64:(c + 1) * 64], 64, 128),
           [Lrow_b, LTs_b], OP.subtract)
        TT("dve", v3(Et[:, 0:8192], 64, 128), Et_b,
           v3(Et[:, 0:8192], 64, 128), bc3m(mut[:, 0:128], 64, 128),
           [Et_b, mut_b], OP.add)
        ACTV(Et[:, 0:8192], Et_b, Et[:, 0:8192], [Et_b], AF.Exp)
        TT("dve", v3(Et[:, 0:8192], 64, 128), Et_b,
           v3(Et[:, 0:8192], 64, 128), bc3m(CBT[:, 0:128], 64, 128),
           [Et_b, CBT_b], OP.mult)
        for h in range(MNH):
            g, sl = h // 8, (h % 8) * 64
            MM(psum[g][0][:, sl:sl + 64], psum[g][1],
               Et[:, h * 128:(h + 1) * 128], xti[:, h * 64:(h + 1) * 64],
               [Et_b, xti_b], start=(h % 8 == 0), stop=(h % 8 == 7))
        for g in range(8):
            CPY("act", ych[:, g * 512:(g + 1) * 512], ych_b,
                psum[g][0][:, 0:512], [psum[g][1]])
        for g in range(8):
            MM(psum[g][0][:, 0:512], psum[g][1],
               Ccm[:, c * 128:(c + 1) * 128],
               Scur[:, g * 512:(g + 1) * 512], [Ccm_b, Scur_b],
               start=True, stop=True)
        for g in range(8):
            CPY("act", yi2[:, g * 512:(g + 1) * 512], yi2_b,
                psum[g][0][:, 0:512], [psum[g][1]])
        TT("dve", v3(yi2[:, 0:4096], 64, 64), yi2_b,
           v3(yi2[:, 0:4096], 64, 64),
           bc3i(eLTs[:, c * 64:(c + 1) * 64], 64, 64), [yi2_b, eLTs_b],
           OP.mult)
        TT("dve", ych[:, 0:4096], ych_b, ych[:, 0:4096], yi2[:, 0:4096],
           [ych_b, yi2_b], OP.add)
        TT("dve", v3(yi2[:, 0:4096], 64, 64), yi2_b,
           v3(hstm[:, 0:4096], 64, 64), bc3i(Db[:, 0:64], 64, 64),
           [hstm_b, Db_b], OP.mult)
        TT("dve", ych[:, 0:4096], ych_b, ych[:, 0:4096], yi2[:, 0:4096],
           [ych_b, yi2_b], OP.add)
        state_step()
        LD(gch[:, :], gch_b, gate_s[c * 128:(c + 1) * 128, :],
           sbufs=[DBuf("gate_s")])
        ACTV(sgch[:, :], sgch_b, gch[:, :], [gch_b], AF.Silu)
        TT("dve", ych[:, :], ych_b, ych[:, :], sgch[:, :],
           [ych_b, sgch_b], OP.mult)
        ACTV(sqy[:, 0:4096], sqy_b, ych[:, :], [ych_b], AF.Square,
             accum=ssqy, ab=ssqy_b)
        rmy, rmy_b = rsqrt_col(f"p5.rmy{c}", ssqy, ssqy_b, 1.0 / 4096.0,
                               eps5_t[:, 0:1], eps5_b)
        TS("dve", ych[:, :], ych_b, ych[:, :], [ych_b, rmy_b], rmy, None,
           OP.mult)
        TT("dve", ych[:, :], ych_b, ych[:, :], mnwb[:, 0:4096],
           [ych_b, mnwb_b], OP.mult)
        transpose_tile(ynT, ynT_b, ych, ych_b, 32, pbanks=(6,))
        for fc in range(32):
            ST(yn_s[fc * 128:(fc + 1) * 128, c * 128:(c + 1) * 128],
               DBuf("yn_s"), ynT[:, fc * 128:(fc + 1) * 128], [ynT_b])
    A.release()
    A.release()

    A.mark()
    wop = [A.alloc(f"p5c.w{i}", 512) for i in range(2)]
    ynl = [A.alloc(f"p5c.yn{i}", 1024) for i in range(2)]
    xr5, xr5_b = A.alloc("p5c.xr", 512)
    o5, o5_b = A.alloc("p5c.o", 512)

    def _ld_oc(oc2, ctr2):
        LD(wop[ctr2 % 2][0], wop[ctr2 % 2][1],
           outT_d[ctr2 * 128:(ctr2 + 1) * 128,
                  oc2 * 512:(oc2 + 1) * 512],
           q="act" if ctr2 % 2 else "sync")
        LD(ynl[ctr2 % 2][0], ynl[ctr2 % 2][1],
           yn_s[ctr2 * 128:(ctr2 + 1) * 128, 0:1024],
           sbufs=[DBuf("yn_s")], q="act" if ctr2 % 2 else "sync")
    _ld_oc(0, 0)
    for oc in range(4):
        for ctr in range(32):
            wt, wt_b = wop[ctr % 2]
            yt, yt_b = ynl[ctr % 2]
            for tt in range(8):
                MM(psum[tt][0][:, 0:512], psum[tt][1],
                   yt[:, tt * 128:(tt + 1) * 128], wt,
                   [yt_b, wt_b], start=(ctr == 0), stop=(ctr == 31))
            if ctr + 1 < 32:
                _ld_oc(oc, ctr + 1)
            elif oc + 1 < 4:
                _ld_oc(oc + 1, 0)
        for tt in range(8):
            LD(xr5, xr5_b, x2_s[tt * 128:(tt + 1) * 128,
                                oc * 512:(oc + 1) * 512],
               sbufs=[DBuf("x2_s")])
            TT("dve", o5, o5_b, psum[tt][0][:, 0:512], xr5,
               [psum[tt][1], xr5_b], OP.add)
            ST(x3_s[tt * 128:(tt + 1) * 128, oc * 512:(oc + 1) * 512],
               DBuf("x3_s"), o5, [o5_b])
    A.release()

    ffn_apply("f2", x3_s, y_out, "y_out", "x3_s")
    P.op("gp", lambda e: e.engine_nop(), reads=[DBuf("y_out")])

    emit = P.finalize(nc, sems, semq, ccsems)
    with nc.Block() as block:
        @block.tensor
        def _(e):
            emit("pe", e)

        @block.vector
        def _(e):
            emit("dve", e)

        @block.scalar
        def _(e):
            emit("act", e)

        @block.gpsimd
        def _(e):
            emit("gp", e)

        @block.sync
        def _(e):
            emit("sync", e)

    ctx.close()
    return nc


def make_in_maps(inputs):
    x = np.ascontiguousarray(inputs["L0_attn_a_hidden_states"],
                             dtype=np.float32)
    g = lambda k: np.ascontiguousarray(inputs[k], dtype=np.float32)
    shared = {
        "Wq": g("L0_attn_a_Wq"),
        "Wq_idx": g("L0_attn_a_Wq_idx"),
        "Wk_idx": g("L0_attn_a_Wk_idx"),
        "bq_idx": g("L0_attn_a_bq_idx").reshape(HI * DI, 1),
        "bk_idx": g("L0_attn_a_bk_idx").reshape(HI * DI, 1),
        "idx_w": g("L0_attn_a_idx_w").reshape(4, 1),
        "w_ctrl": np.ascontiguousarray(
            g("L0_attn_a_w_ctrl").reshape(16, 128).T),
        "b_ctrl": g("L0_attn_a_b_ctrl").reshape(1, 1),
        "W_dkv": g("L0_attn_a_W_dkv"),
        "kv_norm_w": g("L0_attn_a_kv_norm_w").reshape(1, KR),
        "W_ukv": g("L0_attn_a_W_ukv"),
        "Wo": g("L0_attn_a_Wo"),
        "in_projT": np.ascontiguousarray(g("L0_attn_b_in_proj_w").T),
        "conv_w": np.ascontiguousarray(g("L0_attn_b_conv_w")[:, 0, :]),
        "conv_b": g("L0_attn_b_conv_b").reshape(CONV, 1),
        "dt_bias": g("L0_attn_b_dt_bias").reshape(MNH, 1),
        "A_log": g("L0_attn_b_A_log").reshape(MNH, 1),
        "D_in": g("L0_attn_b_D").reshape(MNH, 1),
        "mnorm_w": g("L0_attn_b_norm_weight").reshape(1, 4096),
        "out_projT": np.ascontiguousarray(g("L0_attn_b_out_proj_w").T),
        "w_gT": np.ascontiguousarray(g("L0_ffn_w_gate").T),
        "w_uT": np.ascontiguousarray(g("L0_ffn_w_up").T),
        "w_dT": np.ascontiguousarray(g("L0_ffn_w_down").T),
    }
    in_maps = []
    for c in range(8):
        bi, ci = c // 4, c % 4
        m = dict(shared)
        m["x_full"] = np.ascontiguousarray(x[bi])
        m["x_own"] = np.ascontiguousarray(x[bi, ci * T:(ci + 1) * T])
        m["pos0"] = np.array([[ci * T]], np.float32)
        hs = np.zeros((4, 1), np.float32)
        if ci > 0:
            hs[ci - 1, 0] = 1.0
        m["halo_sel"] = hs
        ss = np.zeros((4, 4), np.float32)
        sm = np.zeros((4, 1), np.float32)
        for j in range(ci):
            sm[j, 0] = 1.0
            for i in range(j + 1, ci):
                ss[j, i] = 1.0
        m["state_sel"] = ss
        m["state_mask"] = sm
        in_maps.append(m)
    return in_maps


def kernel(**inputs):
    nc = build_nc()
    in_maps = make_in_maps(inputs)
    res = run_bass_kernel_spmd(nc, in_maps, core_ids=list(range(8)))
    out = np.empty((B, S, DM), np.float32)
    for c in range(8):
        bi, ci = c // 4, c % 4
        out[bi, ci * T:(ci + 1) * T] = res.results[c]["y_out"]
    return out
